# version 09
# baseline (speedup 1.0000x reference)
import numpy as np
from contextlib import ExitStack

import concourse.bass as bass
import concourse.mybir as mybir
from concourse.bass_utils import run_bass_kernel_spmd

F32 = mybir.dt.float32
BF16 = mybir.dt.bfloat16
AF = mybir.ActivationFunctionType
ALU = mybir.AluOpType
AX = mybir.AxisListType

NCORE = 8
NTOK = 16384
D = 1024
NT = NTOK // NCORE
NTILE = NT // 128
NBLK = NT // 512
MB = 512
NMB = NTOK // MB
NPAIR = NMB // 2
EPS = 1e-5
QSCALE = 1.0 / 32.0


def build_nc():
    nc = bass.Bass("TRN2", target_bir_lowering=False)

    x_h = nc.dram_tensor("x", [NT, D], F32, kind="ExternalInput").ap()
    wqkv_h = nc.dram_tensor("wqkv", [3 * D, D], F32, kind="ExternalInput").ap()
    g1r_h = nc.dram_tensor("g1r", [1, 32, 4, 32], F32, kind="ExternalInput").ap()
    g2r_h = nc.dram_tensor("g2r", [4, 16, 1, 16], F32, kind="ExternalInput").ap()
    g1i_h = nc.dram_tensor("g1i", [1, 32, 4, 32], F32, kind="ExternalInput").ap()
    g2i_h = nc.dram_tensor("g2i", [4, 16, 1, 16], F32, kind="ExternalInput").ap()
    br_h = nc.dram_tensor("br", [512], F32, kind="ExternalInput").ap()
    bi_h = nc.dram_tensor("bi", [512], F32, kind="ExternalInput").ap()
    out_h = nc.dram_tensor("out", [NT, D], F32, kind="ExternalOutput").ap()

    wqkv_bf = nc.dram_tensor("wqkv_bf", [3 * D, D], BF16, kind="Internal")
    kslice = nc.dram_tensor("kslice", [D, NT], BF16, kind="Internal")
    vslice = nc.dram_tensor("vslice", [NT, D], BF16, kind="Internal")
    kfull = nc.dram_tensor("kfull", [NCORE * D, NT], BF16, kind="Internal")
    vfull = nc.dram_tensor("vfull", [NTOK, D], BF16, kind="Internal")
    nrm_in = nc.dram_tensor("nrm_in", [1, 512], F32, kind="Internal")
    nrm_out = nc.dram_tensor("nrm_out", [1, 512], F32, kind="Internal")
    wb_l1 = [nc.dram_tensor(f"wb_l1_{w}", [1024, 256], F32, kind="Internal") for w in range(2)]
    wb_l2 = [nc.dram_tensor(f"wb_l2_{w}", [512, 512], F32, kind="Internal") for w in range(2)]
    eye_h = nc.inline_tensor(np.eye(128, dtype=np.float32), "eyec")

    ctx = ExitStack()
    wbctx = ExitStack()
    g1cm = wbctx.enter_context(nc.sbuf_tensor("g1cm", [4, 1024], F32))
    g2s = wbctx.enter_context(nc.sbuf_tensor("g2s", [4, 256], F32))
    a_tmp = [wbctx.enter_context(nc.sbuf_tensor(f"a_tmp{j}", [128, 256], F32)) for j in range(2)]
    wbA = wbctx.enter_context(nc.sbuf_tensor("wbA", [32, 4096], F32))
    wbA2 = wbctx.enter_context(nc.sbuf_tensor("wbA2", [32, 4096], F32))
    wbctx.close()

    regA = ctx.enter_context(nc.sbuf_tensor("regA", [128, 24576], BF16))
    regB = ctx.enter_context(nc.sbuf_tensor("regB", [128, 8192], BF16))
    regC = ctx.enter_context(nc.sbuf_tensor("regC", [128, 8192], BF16))
    wtR = ctx.enter_context(nc.sbuf_tensor("wtR", [128, 2048], BF16))
    wtI = ctx.enter_context(nc.sbuf_tensor("wtI", [128, 2048], BF16))
    wtIn = ctx.enter_context(nc.sbuf_tensor("wtIn", [128, 2048], BF16))
    qfm = ctx.enter_context(nc.sbuf_tensor("qfm", [128, 16384], BF16))
    oacc = ctx.enter_context(nc.sbuf_tensor("oacc", [128, 16384], F32))
    xt = ctx.enter_context(nc.sbuf_tensor("xt", [128, 2048], F32))
    xnf = ctx.enter_context(nc.sbuf_tensor("xnf", [128, 2048], BF16))
    eye_sb = ctx.enter_context(nc.sbuf_tensor("eye_sb", [128, 128], BF16))
    stats = ctx.enter_context(nc.sbuf_tensor("stats", [128, 16], F32))
    nstats = ctx.enter_context(nc.sbuf_tensor("nstats", [1, 16], F32))
    seps = ctx.enter_context(nc.sbuf_tensor("seps", [128, 1], F32))
    negm_bc = ctx.enter_context(nc.sbuf_tensor("negm_bc", [128, 1], F32))
    L_sb = ctx.enter_context(nc.sbuf_tensor("L_sb", [128, 16], F32))
    rl_sb = ctx.enter_context(nc.sbuf_tensor("rl_sb", [128, 16], F32))
    nrm_sb = ctx.enter_context(nc.sbuf_tensor("nrm_sb", [1, 512], F32))
    ones_f = ctx.enter_context(nc.sbuf_tensor("ones_f", [1, 128], F32))
    ones_b = ctx.enter_context(nc.sbuf_tensor("ones_b", [128, 1], BF16))
    sbr = ctx.enter_context(nc.sbuf_tensor("sbr", [1, 512], F32))
    sbi = ctx.enter_context(nc.sbuf_tensor("sbi", [1, 512], F32))
    assert nc.sbuf_bytes_remaining >= 0

    psA = ctx.enter_context(nc.psum_tensor("psA", [128, 1024], F32))
    psL = ctx.enter_context(nc.psum_tensor("psL", [128, 512], F32))
    psX = ctx.enter_context(nc.psum_tensor("psX", [128, 512], F32))
    ptctx = ExitStack()
    pT0 = ptctx.enter_context(nc.psum_tensor("pT0", [128, 512], F32))
    pT1 = ptctx.enter_context(nc.psum_tensor("pT1", [128, 512], F32))
    ptctx.close()
    psB = ctx.enter_context(nc.psum_tensor("psB", [128, 1024], F32))
    psV10 = ctx.enter_context(nc.psum_tensor("psV10", [128, 512], F32))
    psV11 = ctx.enter_context(nc.psum_tensor("psV11", [128, 512], F32))
    ps0 = psA[:, 0:512]
    ps1 = psA[:, 512:1024]
    _sbanks = [psA[:, 0:512], psA[:, 512:1024], psB[:, 0:512], psB[:, 512:1024]]
    pvpair = (psV10, psV11)
    pT = [pT0, pT1]

    semnames = ("s_dve s_act s_wb s_wbv s_wbs s_wbd s_wq s_wt s_x s_sq s_st s_rs s_ln s_tp s_fm s_pq s_ev "
                "s_ks s_qs s_nm s_nf s_nmsb s_store s_nst s_ccn s_cck s_ccv s_nb s_mi s_kv s_smm "
                "s_exp s_pv s_add s_lrl s_x2 s_la s_vv").split()
    sems = {n: ctx.enter_context(nc.semaphore(n)) for n in semnames}
    g = lambda n: sems[n]

    blk = ctx.enter_context(nc.Block())

    def wqkvT(c):
        return regA[:, 3072 * c:3072 * (c + 1)]
    def PT(u, mc):
        return regA[:, 8192 * u + 2048 * mc:8192 * u + 2048 * (mc + 1)]
    def hfm(c):
        return regA[:, 2048 * c:2048 * (c + 1)]
    def xnfm(par, c):
        return regB[:, 4096 * par + 512 * c:4096 * par + 512 * (c + 1)]
    def kbuf(par, c):
        return regB[:, 4096 * par + 512 * c:4096 * par + 512 * (c + 1)]
    def ktile(j):
        return regC[:, 512 * j:512 * (j + 1)]
    def vtile(j):
        return regC[:, 1024 + 512 * j:1024 + 512 * (j + 1)]
    def ksq(c):
        return regC[:, 2048 + 512 * c:2048 + 512 * (c + 1)]
    def trash():
        return regC[:, 6144:7168]
    def vbuf(par, mc):
        return regC[:, 4096 * par + 1024 * mc:4096 * par + 1024 * (mc + 1)]
    def xnf_t(t_idx):
        return xnf[:, 1024 * (t_idx % 2):1024 * (t_idx % 2 + 1)]
    def xt_t(t_idx):
        return xt[:, 1024 * (t_idx % 2):1024 * (t_idx % 2 + 1)]
    def oacc_t(t_idx):
        return oacc[:, 1024 * t_idx:1024 * (t_idx + 1)]
    def st_col(par, i):
        return stats[:, 8 * par + i:8 * par + i + 1]

    WB, WBv, WBs, WBd = {}, {}, {}, {}
    for w in range(2):
        for j in range(8):
            WB[f"mm{w}_{j}"] = 8 * w + j + 1
            WBv[f"evac{w}_{j}"] = 10 * w + j + 1
            WBd[f"store{w}_{j}"] = 16 * (14 * w + j + 1)
        for h in range(2):
            WBv[f"reorder{w}_{h}"] = 10 * w + 9 + h
            WBs[f"read1_{w}_{h}"] = 80 + 32 * w + 16 * h
            WBd[f"write2_{w}_{h}"] = 16 * (14 * w + 9 + h)
        WBs[f"loads{w}"] = 32 + 32 * w
        for q in range(4):
            WBd[f"read2_{w}_{q}"] = 16 * (14 * w + 11 + q)
    WB_ALL = 16
    WBV_ALL = 20
    WBD_ALL = 16 * 28
    KV_STORES = 16 * NBLK
    ST_ALL_P1 = KV_STORES + 1
    ST_ALL = ST_ALL_P1 + NTILE

    @blk.sync
    def _(sync):
        s = sync
        _qlast = [None]
        def sdma(out, in_, sem, **kw):
            if _qlast[0] is not None:
                s.wait_ge(*_qlast[0])
            ins = s.dma_start(out=out, in_=in_, **kw)
            ins.then_inc(sem, 16)
            _cnt = _qcnt.get(sem.name if hasattr(sem, 'name') else id(sem), 0) + 16
            _qcnt[sem.name if hasattr(sem, 'name') else id(sem)] = _cnt
            _qlast[0] = (sem, _cnt)
            return ins
        _qcnt = {}
        for w in range(2):
            g1h, g2h = (g1r_h, g2r_h) if w == 0 else (g1i_h, g2i_h)
            if w == 1:
                s.wait_ge(g("s_wb"), WB["mm0_7"])
            sdma(g1cm[:, :].rearrange("c (b d) -> c b d", b=32),
                 g1h.rearrange("a b c d -> c (a b) d"), g("s_wbs"))
            sdma(g2s[:, :], g2h.rearrange("c e f g -> c (e f g)"), g("s_wbs"))

        def xload(t_):
            if t_ >= 2:
                s.wait_ge(g("s_ln"), t_ - 1)
                s.wait_ge(g("s_sq"), t_ - 1)
            sdma(xt_t(t_), x_h[128 * t_:128 * (t_ + 1), :], g("s_x"))

        xload(0)
        xload(1)
        for w in range(2):
            for h in range(2):
                if (w, h) != (0, 0):
                    prevr = WBv[f"reorder{w}_{h - 1}"] if h == 1 else WBv[f"reorder{w - 1}_1"]
                    s.wait_ge(g("s_wbv"), prevr)
                s.wait_ge(g("s_wbd"), WBd[f"store{w}_{4 * h + 3}"])
                sdma(wbA[:, :].rearrange("p (b eg) -> p b eg", b=16),
                     wb_l1[w].ap()[512 * h:512 * (h + 1), :]
                     .rearrange("(b d) eg -> d b eg", b=16), g("s_wbs"))
        for t_ in range(2, 8):
            xload(t_)
        s.wait_ge(g("s_wbd"), WBD_ALL)
        s.wait_ge(g("s_wb"), WB_ALL)
        s.wait_ge(g("s_wbv"), WBV_ALL)
        s.wait_ge(g("s_wq"), 32)
        for c in range(8):
            sdma(wqkvT(c), wqkv_bf.ap()[:, 128 * c:128 * (c + 1)], g("s_wt"), transpose=True)
            if c < 8:
                pass
        for t_ in range(8, NTILE):
            xload(t_)
        sdma(sbr[:, :], br_h[None, :], g("s_nb"))
        sdma(sbi[:, :], bi_h[None, :], g("s_nb"))
        s.wait_ge(g("s_ccn"), 1)
        sdma(nrm_sb[:, :], nrm_out.ap()[:, :], g("s_nb"))
        def kload(mb):
            par = mb % 2
            r, lb = mb // 4, mb % 4
            if mb >= 2:
                s.wait_ge(g("s_smm"), 4 * (mb - 1))
            sdma(regB[:, 4096 * par:4096 * (par + 1)].rearrange("p (c m) -> p c m", c=8),
                 kfull.ap()[1024 * r:1024 * (r + 1), 512 * lb:512 * (lb + 1)]
                 .rearrange("(c p) m -> p c m", p=128), g("s_kv"))
        s.wait_ge(g("s_cck"), 1)
        s.wait_ge(g("s_pq"), 96)
        for mb_ in range(NMB):
            kload(mb_)
        for t in range(NTILE):
            if t >= 2:
                s.wait_ge(g("s_x2"), t - 1)
            sdma(xt_t(t), x_h[128 * t:128 * (t + 1), :], g("s_x"))

    @blk.gpsimd
    def _(gp):
        _glast = [None]
        _gcnt = {}
        def gdma(out, in_, sem, **kw):
            if _glast[0] is not None:
                gp.wait_ge(*_glast[0])
            ins = gp.dma_start(out=out, in_=in_, **kw)
            ins.then_inc(sem, 16)
            k = sem.name if hasattr(sem, 'name') else id(sem)
            _gcnt[k] = _gcnt.get(k, 0) + 16
            _glast[0] = (sem, _gcnt[k])
            return ins
        gdma(eye_sb[:, :], eye_h.ap()[:, :], g("s_wq"))
        gdma(wqkv_bf.ap()[:, :], wqkv_h, g("s_wq"))
        for w in range(2):
            for j in range(8):
                gp.wait_ge(g("s_wbv"), WBv[f"evac{w}_{j}"])
                gdma(wb_l1[w].ap()[128 * j:128 * (j + 1), :], a_tmp[j % 2][:, :], g("s_wbd"))
            for h in range(2):
                gp.wait_ge(g("s_wbv"), WBv[f"reorder{w}_{h}"])
                gdma(wb_l2[w].ap().rearrange("(d gg) (b e) -> d gg b e", gg=16, b=32)[:, :, 16 * h:16 * (h + 1), :],
                     wbA2[:, :].rearrange("p (gg b e) -> p gg b e", gg=16, b=16), g("s_wbd"))
            wt = wtR if w == 0 else wtI
            for q in range(4):
                gdma(wt[:, 512 * q:512 * (q + 1)], wb_l2[w].ap()[128 * q:128 * (q + 1), :], g("s_wbd"))
        gp.wait_ge(g("s_nmsb"), 1)
        gdma(nrm_in.ap()[:, :], nrm_sb[:, :], g("s_nst"))
        gp.wait_ge(g("s_store"), 16 * KV_STORES)
        gp.wait_ge(g("s_nst"), 16)
        _gv_after_cc = True
        rg = [list(range(NCORE))]
        gp.collective_compute("AllReduce", ALU.max, replica_groups=rg,
                              ins=[nrm_in.ap()[:, :]], outs=[nrm_out.ap()[:, :]]).then_inc(g("s_ccn"))
        gp.collective_compute("AllGather", ALU.bypass, replica_groups=rg,
                              ins=[kslice.ap()[:, :]], outs=[kfull.ap()[:, :]]).then_inc(g("s_cck"))
        gp.collective_compute("AllGather", ALU.bypass, replica_groups=rg,
                              ins=[vslice.ap()[:, :]], outs=[vfull.ap()[:, :]]).then_inc(g("s_ccv"))
        _glast[0] = (g("s_ccv"), 1)
        gp.wait_ge(g("s_nm"), 8)
        for mb_ in range(NMB):
            par_ = mb_ % 2
            if mb_ >= 2:
                gp.wait_ge(g("s_pv"), 16 * (mb_ - 1))
            gdma(regC[:, 4096 * par_:4096 * (par_ + 1)].rearrange("p (c m) -> p c m", c=4),
                 vfull.ap()[512 * mb_:512 * (mb_ + 1), :]
                 .rearrange("(c p) m -> p c m", p=128), g("s_vv"))


    @blk.tensor
    def _(t):
        for w in range(2):
            t.wait_ge(g("s_wbs"), WBs[f"loads{w}"])
            for j in range(8):
                if not (w == 0 and j == 0):
                    prev = WBv[f"evac{w}_{j - 1}"] if j > 0 else WBv["evac0_7"]
                    t.wait_ge(g("s_wbv"), prev)
                t.matmul(psX[:, 0:256], g1cm[:, 128 * j:128 * (j + 1)], g2s[:, :],
                         start=True, stop=True).then_inc(g("s_wb"), 1)

        t.wait_ge(g("s_wq"), 16)

        def transpose_tile(t_idx, ph3):
            base = 128 * ph3
            for c in range(8):
                G = base + 8 * t_idx + c
                if c == 0:
                    t.wait_ge(g("s_ln"), 16 * ph3 + t_idx + 1)
                if G >= 2:
                    t.wait_ge(g("s_fm"), G - 1)
                t.matmul(pT[c % 2][:, 0:128],
                         xnf_t(t_idx)[:, 128 * c:128 * (c + 1)],
                         eye_sb[:, :], start=True, stop=True).then_inc(g("s_tp"), 1)

        def proj_block(b):
            par = b % 2
            gbase = 24 * b
            if b == 0:
                t.wait_ge(g("s_wt"), 128)
            t.wait_ge(g("s_fm"), 32 * (b + 1))
            grp = 0
            for ek in range(8):
                gi = gbase + grp
                if gi >= 2:
                    t.wait_ge(g("s_ev"), gi - 1)
                ps = ps0 if grp % 2 == 0 else ps1
                for c in range(8):
                    ins = t.matmul(ps[:, :], wqkvT(c)[:, 1024 + 128 * ek:1024 + 128 * (ek + 1)],
                                   xnfm(par, c), start=(c == 0), stop=(c == 7))
                    if c == 7:
                        ins.then_inc(g("s_pq"), 1)
                grp += 1
            for ek in range(8):
                t.wait_ge(g("s_ks"), 8 * b + ek + 1)
                if ek == 0 and b >= 1:
                    t.wait_ge(g("s_nf"), b)
                ins = t.matmul(psL[0:1, 0:512], ones_b[:, :], ksq(ek),
                               start=(ek == 0), stop=(ek == 7))
                if ek == 7:
                    ins.then_inc(g("s_nm"), 1)
            for eq in range(8):
                gi = gbase + grp
                t.wait_ge(g("s_ev"), gi - 1)
                ps = ps0 if grp % 2 == 0 else ps1
                for c in range(8):
                    ins = t.matmul(ps[:, :], wqkvT(c)[:, 128 * eq:128 * (eq + 1)],
                                   xnfm(par, c), start=(c == 0), stop=(c == 7))
                    if c == 7:
                        ins.then_inc(g("s_pq"), 1)
                grp += 1
            for i in range(8):
                nc4, h = i // 2, i % 2
                gi = gbase + grp
                t.wait_ge(g("s_ev"), gi - 1)
                ps = ps0 if grp % 2 == 0 else ps1
                for c in range(8):
                    ins = t.matmul(ps[:, :], xnfm(par, c)[:, 128 * nc4:128 * (nc4 + 1)],
                                   wqkvT(c)[:, 2048 + 512 * h:2048 + 512 * (h + 1)],
                                   start=(c == 0), stop=(c == 7))
                    if c == 7:
                        ins.then_inc(g("s_pq"), 1)
                grp += 1

        for b in range(NBLK):
            for tt in range(4 * b, 4 * b + 4):
                transpose_tile(tt, 0)
            if b >= 1:
                proj_block(b - 1)
        proj_block(NBLK - 1)

        for ns in range(4):
            t.wait_ge(g("s_qs"), 8 * (ns + 1))
            t.wait_ge(g("s_nf"), 4 + ns)
            for c in range(8):
                ins = t.matmul(psL[0:1, 0:512], ones_b[:, :], ksq(c),
                               start=(c == 0), stop=(c == 7))
                if c == 7:
                    ins.then_inc(g("s_nm"), 1)

        t.wait_ge(g("s_mi"), 2)
        t.matmul(psX[:, 0:1], ones_f[:, :], nstats[0:1, 5:6],
                 start=True, stop=True).then_inc(g("s_mi"), 1)

        def s_part(mb, mc, cpair):
            par = mb % 2
            EG = 4 * mb + mc
            if mc == 0 and cpair == 0:
                t.wait_ge(g("s_kv"), 16 * (mb + 1))
            if cpair == 0 and EG >= 1:
                t.wait_ge(g("s_exp"), 2 * EG)
            for c in (2 * cpair, 2 * cpair + 1):
                lhs = kbuf(par, c)[:, 128 * mc:128 * (mc + 1)]
                for sqs in range(4):
                    ins = t.matmul(_sbanks[sqs], lhs,
                                   qfm[:, 2048 * c + 512 * sqs:2048 * c + 512 * (sqs + 1)],
                                   start=(c == 0), stop=(c == 7))
                    if c == 7 and sqs == 3:
                        ins.then_inc(g("s_smm"), 1)

        def pv_group(mb, qc):
            idx = 16 * mb + qc
            if qc == 0:
                t.wait_ge(g("s_vv"), 16 * (mb + 1))
                t.wait_ge(g("s_exp"), 8 * (mb + 1))
            if idx == 0:
                t.wait_ge(g("s_mi"), 4)
            if idx >= 1:
                t.wait_ge(g("s_la"), idx)
                t.wait_ge(g("s_add"), idx)
            for sub in range(4):
                lhs = PT(mb % 3, sub)[:, 128 * qc:128 * (qc + 1)]
                t.matmul(pvpair[0][:, :], lhs, vbuf(mb % 2, sub)[:, 0:512],
                         start=(sub == 0), stop=(sub == 3))
                t.matmul(psX[:, 0:1], lhs, ones_b[:, :],
                         start=(sub == 0), stop=(sub == 3))
                ins = t.matmul(pvpair[1][:, :], lhs, vbuf(mb % 2, sub)[:, 512:1024],
                               start=(sub == 0), stop=(sub == 3))
                if sub == 3:
                    ins.then_inc(g("s_pv"), 1)

        for mb in range(NMB):
            for u in range(16):
                if mb >= 2:
                    pv_group(mb - 2, u)
                s_part(mb, u // 4, u % 4)
        for mb in (NMB - 2, NMB - 1):
            for qc in range(16):
                pv_group(mb, qc)

        def ffn_group(qc):
            t.wait_ge(g("s_fm"), 128 + 8 * (qc + 1))
            t.wait_ge(g("s_nb"), 32)
            if qc >= 2:
                t.wait_ge(g("s_ev"), 96 + 2 * (qc - 1))
            pair = (ps0, ps1) if qc % 2 == 0 else (psL, psX)
            for half in range(2):
                ps = pair[half]
                wa, wb_ = (wtR, wtIn) if half == 0 else (wtI, wtR)
                bias = sbr if half == 0 else sbi
                for c in range(4):
                    t.matmul(ps[:, :], hfm(c)[:, 128 * qc:128 * (qc + 1)],
                             wa[:, 512 * c:512 * (c + 1)], start=(c == 0), stop=False)
                for c in range(4):
                    t.matmul(ps[:, :], hfm(4 + c)[:, 128 * qc:128 * (qc + 1)],
                             wb_[:, 512 * c:512 * (c + 1)], start=False, stop=False)
                t.matmul(ps[:, :], ones_f[:, :], bias[:, :],
                         start=False, stop=True).then_inc(g("s_pq"), 1)

        transpose_tile(0, 1)
        transpose_tile(1, 1)
        for qc in range(NTILE - 2):
            transpose_tile(qc + 2, 1)
            ffn_group(qc)
        ffn_group(NTILE - 2)
        ffn_group(NTILE - 1)

    @blk.scalar
    def _(sc):
        _acnt = [0]
        def amark(ins):
            ins.then_inc(g("s_act"), 1)
            _acnt[0] += 1
        def ajoin():
            sc.wait_ge(g("s_act"), _acnt[0])
        def ln_stats(t_idx, ph3, src_ap):
            par = t_idx % 2
            off = 16 * ph3
            if ph3:
                sc.wait_ge(g("s_x2"), t_idx + 1)
            else:
                sc.wait_ge(g("s_x"), 16 * (t_idx + 1))
            sc.activation(out=trash(), in_=src_ap, func=AF.Square,
                          accum_out=st_col(par, 4)).then_inc(g("s_sq"), 1)
            sc.wait_ge(g("s_st"), off + t_idx + 1)
            sc.wait_ge(g("s_sq"), off + t_idx + 1)
            amark(sc.activation(out=st_col(par, 5), in_=st_col(par, 4), func=AF.Ln,
                                scale=1.0 / D, bias=st_col(par, 3)))
            ajoin()
            sc.activation(out=st_col(par, 6), in_=st_col(par, 5), func=AF.Exp,
                          scale=-0.5).then_inc(g("s_rs"), 1)

        def evac_block(b):
            gbase = 24 * b
            for ek in range(8):
                sc.wait_ge(g("s_pq"), gbase + ek + 1)
                kk = 8 * b + ek
                if kk >= 2:
                    sc.wait_ge(g("s_store"), 16 * (16 * b + ek))
                if ek == 0 and b >= 1:
                    sc.wait_ge(g("s_nm"), b)
                ps = ps0 if ek % 2 == 0 else ps1
                sc.activation(out=ksq(ek), in_=ps[:, :], func=AF.Square).then_inc(g("s_ks"), 1)
                sc.activation(out=ktile(ek % 2), in_=ps[:, :], func=AF.Copy).then_inc(g("s_ev"), 1)
                sc.wait_ge(g("s_ev"), gbase + ek + 1)
                sc.dma_start(out=kslice.ap()[128 * ek:128 * (ek + 1), 512 * b:512 * (b + 1)],
                             in_=ktile(ek % 2)).then_inc(g("s_store"), 16)
            for eq in range(8):
                sc.wait_ge(g("s_pq"), gbase + 8 + eq + 1)
                ps = ps0 if eq % 2 == 0 else ps1
                sc.activation(out=qfm[:, 2048 * eq + 512 * b:2048 * eq + 512 * (b + 1)],
                              in_=ps[:, :], func=AF.Copy, scale=QSCALE).then_inc(g("s_ev"), 1)
            for i in range(8):
                sc.wait_ge(g("s_pq"), gbase + 16 + i + 1)
                vv = 8 * b + i
                if vv >= 2:
                    sc.wait_ge(g("s_store"), 16 * (16 * b + 8 + i))
                nc4, h = i // 2, i % 2
                ps = ps0 if i % 2 == 0 else ps1
                sc.activation(out=vtile(i % 2), in_=ps[:, :], func=AF.Copy).then_inc(g("s_ev"), 1)
                sc.wait_ge(g("s_ev"), gbase + 16 + i + 1)
                sc.dma_start(out=vslice.ap()[512 * b + 128 * nc4:512 * b + 128 * (nc4 + 1),
                                             512 * h:512 * (h + 1)],
                             in_=vtile(i % 2)).then_inc(g("s_store"), 16)

        for b in range(NBLK):
            for tt in range(4 * b, 4 * b + 4):
                ln_stats(tt, 0, xt_t(tt))
            if b >= 1:
                evac_block(b - 1)
        evac_block(NBLK - 1)

        for ns in range(4):
            sc.wait_ge(g("s_nm"), 4 + ns)
            for c in range(8):
                sc.activation(out=ksq(c), in_=qfm[:, 2048 * c + 512 * ns:2048 * c + 512 * (ns + 1)],
                              func=AF.Square).then_inc(g("s_qs"), 1)

        sc.wait_ge(g("s_mi"), 1)
        amark(sc.activation(out=nstats[0:1, 4:5], in_=nstats[0:1, 3:4], func=AF.Sqrt))
        ajoin()
        sc.activation(out=nstats[0:1, 5:6], in_=nstats[0:1, 4:5], func=AF.Copy,
                      scale=-1.0).then_inc(g("s_mi"), 1)

        ecount = 0
        for mb in range(NMB):
            u = mb % 3
            if mb >= 3:
                sc.wait_ge(g("s_pv"), 16 * (mb - 2))
            for mc in range(4):
                sc.wait_ge(g("s_smm"), ecount + 1)
                if ecount == 0:
                    sc.wait_ge(g("s_mi"), 4)
                sc.activation(out=PT(u, mc)[:, 0:1024], in_=psA[:, :], func=AF.Exp,
                              bias=negm_bc[:, :]).then_inc(g("s_exp"), 1)
                sc.activation(out=PT(u, mc)[:, 1024:2048], in_=psB[:, :], func=AF.Exp,
                              bias=negm_bc[:, :]).then_inc(g("s_exp"), 1)
                ecount += 1

        for tt in range(NTILE):
            ln_stats(tt, 1, oacc_t(tt))
        for qc in range(NTILE):
            sc.wait_ge(g("s_ev"), 96 + 2 * (qc + 1))
            sc.dma_start(out=out_h[128 * qc:128 * (qc + 1), :],
                         in_=oacc_t(qc)).then_inc(g("s_store"), 16)
        sc.wait_ge(g("s_store"), 16 * (KV_STORES + NTILE))

    @blk.vector
    def _(v):
        _dcnt = [0]
        def dmark(ins):
            ins.then_inc(g("s_dve"), 1)
            _dcnt[0] += 1
        def djoin():
            v.wait_ge(g("s_dve"), _dcnt[0])
        v.memset(ones_f[:, :], 1.0)
        v.memset(ones_b[:, :], 1.0)
        v.memset(seps[:, :], EPS)
        for w in range(2):
            for j in range(8):
                v.wait_ge(g("s_wb"), WB[f"mm{w}_{j}"])
                if (w, j) > (0, 1):
                    pj = 16 * (14 * w + j) if j >= 2 else 16 * 14
                    v.wait_ge(g("s_wbd"), pj)
                v.tensor_copy(out=a_tmp[j % 2][:, :], in_=psX[:, 0:256]).then_inc(g("s_wbv"), 1)
            for h in range(2):
                v.wait_ge(g("s_wbs"), WBs[f"read1_{w}_{h}"])
                if not (w == 0 and h == 0):
                    v.wait_ge(g("s_wbd"), WBd[f"write2_{w}_{h - 1}"] if h == 1
                              else WBd["write2_0_1"])
                v.tensor_copy(out=wbA2[:, :].rearrange("p (gg b e) -> p gg b e", gg=16, b=16),
                              in_=wbA[:, :].rearrange("p (b e gg) -> p gg b e", b=16, e=16)) \
                    .then_inc(g("s_wbv"), 1)
        v.wait_ge(g("s_wbd"), WBD_ALL)
        ins = None
        for q in range(4):
            ins = v.tensor_scalar_mul(out=wtIn[:, 512 * q:512 * (q + 1)],
                                      in0=wtI[:, 512 * q:512 * (q + 1)], scalar1=-1.0)
        ins.then_inc(g("s_wbv"), 1)

        def ln_vec(t_idx, ph3, src_ap):
            par = t_idx % 2
            off = 16 * ph3
            if not ph3:
                v.wait_ge(g("s_x"), 16 * (t_idx + 1))
            dmark(v.reduce_sum(st_col(par, 0), src_ap, axis=AX.X))
            djoin()
            dmark(v.tensor_scalar_mul(out=st_col(par, 1), in0=st_col(par, 0), scalar1=1.0 / D))
            djoin()
            dmark(v.scalar_tensor_tensor(out=st_col(par, 2), in0=st_col(par, 1), scalar=-1.0,
                                         in1=st_col(par, 1), op0=ALU.mult, op1=ALU.mult))
            djoin()
            v.tensor_scalar_add(out=st_col(par, 3), in0=st_col(par, 2),
                                scalar1=EPS).then_inc(g("s_st"), 1)
            v.wait_ge(g("s_rs"), off + t_idx + 1)
            dmark(v.scalar_tensor_tensor(out=st_col(par, 7), in0=st_col(par, 1), scalar=-1.0,
                                         in1=st_col(par, 6), op0=ALU.mult, op1=ALU.mult))
            if t_idx >= 2:
                v.wait_ge(g("s_tp"), 128 * ph3 + 8 * (t_idx - 1))
            djoin()
            v.tensor_scalar(out=xnf_t(t_idx), in0=src_ap,
                            scalar1=st_col(par, 6), scalar2=st_col(par, 7),
                            op0=ALU.mult, op1=ALU.add).then_inc(g("s_ln"), 1)

        def evac_tile(t_idx, ph3):
            for c in range(8):
                G = 128 * ph3 + 8 * t_idx + c
                v.wait_ge(g("s_tp"), G + 1)
                if (not ph3) and t_idx % 4 == 0 and c == 0 and t_idx // 4 >= 2:
                    v.wait_ge(g("s_pq"), 24 * (t_idx // 4 - 1))
                if ph3:
                    dst = regA[:, 2048 * c + 128 * t_idx:2048 * c + 128 * (t_idx + 1)]
                else:
                    par = (t_idx // 4) % 2
                    dst = regB[:, 4096 * par + 512 * c + 128 * (t_idx % 4):
                               4096 * par + 512 * c + 128 * (t_idx % 4) + 128]
                v.tensor_copy(out=dst, in_=pT[c % 2][:, 0:128]).then_inc(g("s_fm"), 1)

        def knorm_read(b):
            v.wait_ge(g("s_nm"), b + 1)
            if b == 0:
                v.reduce_max(nstats[0:1, 0:1], psL[0:1, 0:512], axis=AX.X).then_inc(g("s_nf"), 1)
            else:
                dmark(v.reduce_max(nstats[0:1, 2:3], psL[0:1, 0:512], axis=AX.X))
                djoin()
                v.tensor_max(out=nstats[0:1, 0:1], in0=nstats[0:1, 0:1],
                             in1=nstats[0:1, 2:3]).then_inc(g("s_nf"), 1)

        for tt in range(NTILE):
            if tt >= 9 and (tt - 9) % 4 == 0:
                knorm_read((tt - 9) // 4)
            if tt >= 1:
                evac_tile(tt - 1, 0)
            ln_vec(tt, 0, xt_t(tt))
        evac_tile(NTILE - 1, 0)
        knorm_read(NBLK - 2)
        knorm_read(NBLK - 1)
        djoin()
        dmark(v.memset(nrm_sb[:, :], 0.0))
        djoin()
        v.tensor_scalar_add(out=nrm_sb[:, :], in0=nrm_sb[:, :],
                            scalar1=nstats[0:1, 0:1]).then_inc(g("s_nmsb"), 1)
        for ns in range(4):
            v.wait_ge(g("s_nm"), 5 + ns)
            if ns == 0:
                v.reduce_max(nstats[0:1, 1:2], psL[0:1, 0:512], axis=AX.X).then_inc(g("s_nf"), 1)
            else:
                dmark(v.reduce_max(nstats[0:1, 2:3], psL[0:1, 0:512], axis=AX.X))
                djoin()
                v.tensor_max(out=nstats[0:1, 1:2], in0=nstats[0:1, 1:2],
                             in1=nstats[0:1, 2:3]).then_inc(g("s_nf"), 1)
        v.wait_ge(g("s_nb"), 48)
        v.wait_ge(g("s_nf"), 8)
        dmark(v.reduce_max(nstats[0:1, 2:3], nrm_sb[0:1, :], axis=AX.X))
        djoin()
        v.tensor_mul(out=nstats[0:1, 3:4], in0=nstats[0:1, 2:3],
                     in1=nstats[0:1, 1:2]).then_inc(g("s_mi"), 1)
        v.wait_ge(g("s_mi"), 3)
        v.tensor_copy(out=negm_bc[:, :], in_=psX[:, 0:1]).then_inc(g("s_mi"), 1)

        for mb in range(NMB):
            for qc in range(16):
                idx = 16 * mb + qc
                v.wait_ge(g("s_pv"), idx + 1)
                if mb == 0:
                    v.tensor_copy(out=L_sb[:, qc:qc + 1], in_=psX[:, 0:1]).then_inc(g("s_la"), 1)
                    v.tensor_copy(out=oacc[:, 1024 * qc:1024 * qc + 512], in_=pvpair[0][:, :])
                    v.tensor_copy(out=oacc[:, 1024 * qc + 512:1024 * (qc + 1)],
                                  in_=pvpair[1][:, :]).then_inc(g("s_add"), 1)
                else:
                    v.tensor_add(out=L_sb[:, qc:qc + 1], in0=psX[:, 0:1],
                                 in1=L_sb[:, qc:qc + 1]).then_inc(g("s_la"), 1)
                    v.tensor_add(out=oacc[:, 1024 * qc:1024 * qc + 512],
                                 in0=pvpair[0][:, :], in1=oacc[:, 1024 * qc:1024 * qc + 512])
                    v.tensor_add(out=oacc[:, 1024 * qc + 512:1024 * (qc + 1)],
                                 in0=pvpair[1][:, :],
                                 in1=oacc[:, 1024 * qc + 512:1024 * (qc + 1)]).then_inc(g("s_add"), 1)
        v.wait_ge(g("s_la"), NMB * 16)
        v.reciprocal(out=rl_sb[:, :], in_=L_sb[:, :]).then_inc(g("s_lrl"), 1)

        def ffn_evac(qc):
            v.wait_ge(g("s_pq"), 96 + 2 * (qc + 1))
            pair = (ps0, ps1) if qc % 2 == 0 else (psL, psX)
            v.tensor_add(out=oacc[:, 1024 * qc:1024 * qc + 512],
                         in0=pair[0][:, :], in1=oacc[:, 1024 * qc:1024 * qc + 512])
            v.tensor_add(out=oacc[:, 1024 * qc + 512:1024 * (qc + 1)],
                         in0=pair[1][:, :],
                         in1=oacc[:, 1024 * qc + 512:1024 * (qc + 1)]).then_inc(g("s_ev"), 2)

        for tt in range(NTILE):
            v.wait_ge(g("s_x"), 256 + 16 * (tt + 1))
            if tt == 0:
                v.wait_ge(g("s_lrl"), 1)
            v.scalar_tensor_tensor(out=oacc_t(tt), in0=oacc_t(tt),
                                   scalar=rl_sb[:, tt:tt + 1], in1=xt_t(tt),
                                   op0=ALU.mult, op1=ALU.add).then_inc(g("s_x2"), 1)
            if tt >= 1:
                evac_tile(tt - 1, 1)
            v.wait_ge(g("s_x2"), tt + 1)
            ln_vec(tt, 1, oacc_t(tt))
            if tt >= 3:
                ffn_evac(tt - 3)
        evac_tile(NTILE - 1, 1)
        ffn_evac(NTILE - 3)
        ffn_evac(NTILE - 2)
        ffn_evac(NTILE - 1)

    return nc, ctx


_cache = {}


def kernel(L0_attn_x, L0_attn_W_qkv, L0_ffn_G1_R, L0_ffn_G2_R, L0_ffn_G1_I,
           L0_ffn_G2_I, L0_ffn_bias_real, L0_ffn_bias_imag):
    x2d = np.ascontiguousarray(np.asarray(L0_attn_x, dtype=np.float32).reshape(NTOK, D))
    wqkv = np.ascontiguousarray(np.asarray(L0_attn_W_qkv, dtype=np.float32))
    g1r = np.ascontiguousarray(np.asarray(L0_ffn_G1_R, dtype=np.float32))
    g2r = np.ascontiguousarray(np.asarray(L0_ffn_G2_R, dtype=np.float32))
    g1i = np.ascontiguousarray(np.asarray(L0_ffn_G1_I, dtype=np.float32))
    g2i = np.ascontiguousarray(np.asarray(L0_ffn_G2_I, dtype=np.float32))
    br = np.ascontiguousarray(np.asarray(L0_ffn_bias_real, dtype=np.float32))
    bi = np.ascontiguousarray(np.asarray(L0_ffn_bias_imag, dtype=np.float32))

    if "nc" not in _cache:
        _cache["nc"] = build_nc()
    nc, _ctx = _cache["nc"]

    in_maps = [dict(x=x2d[NT * c:NT * (c + 1)], wqkv=wqkv, g1r=g1r, g2r=g2r,
                    g1i=g1i, g2i=g2i, br=br, bi=bi) for c in range(NCORE)]
    res = run_bass_kernel_spmd(nc, in_maps, core_ids=list(range(NCORE)))
    out = np.concatenate([res.results[c]["out"] for c in range(NCORE)], axis=0)
    return out.reshape(1, NTOK, D)


# version 10
# speedup vs baseline: 1.6493x; 1.6493x over previous
import numpy as np
from contextlib import ExitStack

import concourse.bass as bass
import concourse.mybir as mybir
from concourse.bass_utils import run_bass_kernel_spmd

F32 = mybir.dt.float32
BF16 = mybir.dt.bfloat16
AF = mybir.ActivationFunctionType
ALU = mybir.AluOpType
AX = mybir.AxisListType

NCORE = 8
NTOK = 16384
D = 1024
NT = NTOK // NCORE
NTILE = NT // 128
NBLK = NT // 512
MB = 512
NMB = NTOK // MB
NPAIR = NMB // 2
EPS = 1e-5
QSCALE = 1.0 / 32.0


def build_nc():
    nc = bass.Bass("TRN2", target_bir_lowering=False)

    x_h = nc.dram_tensor("x", [NT, D], F32, kind="ExternalInput").ap()
    wqkv_h = nc.dram_tensor("wqkv", [3 * D, D], F32, kind="ExternalInput").ap()
    g1r_h = nc.dram_tensor("g1r", [1, 32, 4, 32], F32, kind="ExternalInput").ap()
    g2r_h = nc.dram_tensor("g2r", [4, 16, 1, 16], F32, kind="ExternalInput").ap()
    g1i_h = nc.dram_tensor("g1i", [1, 32, 4, 32], F32, kind="ExternalInput").ap()
    g2i_h = nc.dram_tensor("g2i", [4, 16, 1, 16], F32, kind="ExternalInput").ap()
    br_h = nc.dram_tensor("br", [512], F32, kind="ExternalInput").ap()
    bi_h = nc.dram_tensor("bi", [512], F32, kind="ExternalInput").ap()
    out_h = nc.dram_tensor("out", [NT, D], F32, kind="ExternalOutput").ap()

    wqkv_bf = nc.dram_tensor("wqkv_bf", [3 * D, D], BF16, kind="Internal")
    kslice = nc.dram_tensor("kslice", [D, NT], BF16, kind="Internal")
    vslice = nc.dram_tensor("vslice", [NT, D], BF16, kind="Internal")
    kfull = nc.dram_tensor("kfull", [NCORE * D, NT], BF16, kind="Internal")
    vfull = nc.dram_tensor("vfull", [NTOK, D], BF16, kind="Internal")
    nrm_in = nc.dram_tensor("nrm_in", [1, 512], F32, kind="Internal")
    nrm_out = nc.dram_tensor("nrm_out", [1, 512], F32, kind="Internal")
    wb_l1 = [nc.dram_tensor(f"wb_l1_{w}", [1024, 256], F32, kind="Internal") for w in range(2)]
    wb_l2 = [nc.dram_tensor(f"wb_l2_{w}", [512, 512], F32, kind="Internal") for w in range(2)]
    eye_h = nc.inline_tensor(np.eye(128, dtype=np.float32), "eyec")

    ctx = ExitStack()

    regA = ctx.enter_context(nc.sbuf_tensor("regA", [128, 24576], BF16))
    regB = ctx.enter_context(nc.sbuf_tensor("regB", [128, 8192], BF16))
    regC = ctx.enter_context(nc.sbuf_tensor("regC", [128, 8192], BF16))
    wtR = ctx.enter_context(nc.sbuf_tensor("wtR", [128, 2048], BF16))
    wtI = ctx.enter_context(nc.sbuf_tensor("wtI", [128, 2048], BF16))
    wtIn = ctx.enter_context(nc.sbuf_tensor("wtIn", [128, 2048], BF16))
    qfm = ctx.enter_context(nc.sbuf_tensor("qfm", [128, 16384], BF16))
    oacc = ctx.enter_context(nc.sbuf_tensor("oacc", [128, 16384], F32))
    xt = ctx.enter_context(nc.sbuf_tensor("xt", [128, 2048], F32))
    xnf = ctx.enter_context(nc.sbuf_tensor("xnf", [128, 2048], BF16))
    eye_sb = ctx.enter_context(nc.sbuf_tensor("eye_sb", [128, 128], BF16))
    stats = ctx.enter_context(nc.sbuf_tensor("stats", [128, 16], F32))
    nstats = ctx.enter_context(nc.sbuf_tensor("nstats", [1, 16], F32))
    seps = ctx.enter_context(nc.sbuf_tensor("seps", [128, 1], F32))
    negm_bc = ctx.enter_context(nc.sbuf_tensor("negm_bc", [128, 1], F32))
    L_sb = ctx.enter_context(nc.sbuf_tensor("L_sb", [128, 16], F32))
    rl_sb = ctx.enter_context(nc.sbuf_tensor("rl_sb", [128, 16], F32))
    nrm_sb = ctx.enter_context(nc.sbuf_tensor("nrm_sb", [1, 512], F32))
    ones_f = ctx.enter_context(nc.sbuf_tensor("ones_f", [1, 128], F32))
    ones_b = ctx.enter_context(nc.sbuf_tensor("ones_b", [128, 1], BF16))
    sbr = ctx.enter_context(nc.sbuf_tensor("sbr", [1, 512], F32))
    sbi = ctx.enter_context(nc.sbuf_tensor("sbi", [1, 512], F32))
    g1cm = oacc[0:4, 0:1024]
    g2s = oacc[0:4, 1024:1280]
    a_tmp = [oacc[:, 1280:1536], oacc[:, 1536:1792]]
    wbA = oacc[0:32, 2048:6144]
    wbA2 = oacc[0:32, 6144:10240]
    assert nc.sbuf_bytes_remaining >= 0

    psA = ctx.enter_context(nc.psum_tensor("psA", [128, 1024], F32))
    psL = ctx.enter_context(nc.psum_tensor("psL", [128, 512], F32))
    psX = ctx.enter_context(nc.psum_tensor("psX", [128, 512], F32))
    ptctx = ExitStack()
    pT0 = ptctx.enter_context(nc.psum_tensor("pT0", [128, 512], F32))
    pT1 = ptctx.enter_context(nc.psum_tensor("pT1", [128, 512], F32))
    ptctx.close()
    psB = ctx.enter_context(nc.psum_tensor("psB", [128, 1024], F32))
    psV10 = ctx.enter_context(nc.psum_tensor("psV10", [128, 512], F32))
    psV11 = ctx.enter_context(nc.psum_tensor("psV11", [128, 512], F32))
    ps0 = psA[:, 0:512]
    ps1 = psA[:, 512:1024]
    _sbanks = [psA[:, 0:512], psA[:, 512:1024], psB[:, 0:512], psB[:, 512:1024]]
    pvpair = (psV10, psV11)
    pT = [pT0, pT1]

    semnames = ("s_dve s_act s_wb s_wbv s_wbs s_wbd s_wq s_wt s_x s_sq s_st s_rs s_ln s_tp s_fm s_pq s_ev "
                "s_ks s_qs s_nm s_nf s_nmsb s_store s_nst s_ccn s_cck s_ccv s_nb s_mi s_kv s_smm "
                "s_exp s_pv s_add s_lrl s_x2 s_la s_vv").split()
    sems = {n: ctx.enter_context(nc.semaphore(n)) for n in semnames}
    g = lambda n: sems[n]

    blk = ctx.enter_context(nc.Block())

    def wqkvT(c):
        return regA[:, 3072 * c:3072 * (c + 1)]
    def PT(u, mc):
        return regA[:, 8192 * u + 2048 * mc:8192 * u + 2048 * (mc + 1)]
    def hfm(c):
        return regA[:, 2048 * c:2048 * (c + 1)]
    def xnfm(par, c):
        return regB[:, 4096 * par + 512 * c:4096 * par + 512 * (c + 1)]
    def kbuf(par, c):
        return regB[:, 4096 * par + 512 * c:4096 * par + 512 * (c + 1)]
    def ktile(j):
        return regC[:, 512 * j:512 * (j + 1)]
    def vtile(j):
        return regC[:, 1024 + 512 * j:1024 + 512 * (j + 1)]
    def ksq(c):
        return regC[:, 2048 + 512 * c:2048 + 512 * (c + 1)]
    def trash():
        return regC[:, 6144:7168]
    def vbuf(par, mc):
        return regC[:, 4096 * par + 1024 * mc:4096 * par + 1024 * (mc + 1)]
    def xnf_t(t_idx):
        return xnf[:, 1024 * (t_idx % 2):1024 * (t_idx % 2 + 1)]
    def xt_t(t_idx):
        return xt[:, 1024 * (t_idx % 2):1024 * (t_idx % 2 + 1)]
    def oacc_t(t_idx):
        return oacc[:, 1024 * t_idx:1024 * (t_idx + 1)]
    def st_col(par, i):
        return stats[:, 8 * par + i:8 * par + i + 1]

    WB, WBv, WBs, WBd = {}, {}, {}, {}
    for w in range(2):
        for j in range(8):
            WB[f"mm{w}_{j}"] = 8 * w + j + 1
            WBv[f"evac{w}_{j}"] = 10 * w + j + 1
            WBd[f"store{w}_{j}"] = 16 * (14 * w + j + 1)
        for h in range(2):
            WBv[f"reorder{w}_{h}"] = 10 * w + 9 + h
            WBs[f"read1_{w}_{h}"] = 80 + 32 * w + 16 * h
            WBd[f"write2_{w}_{h}"] = 16 * (14 * w + 9 + h)
        WBs[f"loads{w}"] = 32 + 32 * w
        for q in range(4):
            WBd[f"read2_{w}_{q}"] = 16 * (14 * w + 11 + q)
    WB_ALL = 16
    WBV_ALL = 20
    WBD_ALL = 16 * 28
    KV_STORES = 16 * NBLK
    ST_ALL_P1 = KV_STORES + 1
    ST_ALL = ST_ALL_P1 + NTILE

    @blk.sync
    def _(sync):
        s = sync
        _qlast = [None]
        def sdma(out, in_, sem, **kw):
            if _qlast[0] is not None:
                s.wait_ge(*_qlast[0])
            ins = s.dma_start(out=out, in_=in_, **kw)
            ins.then_inc(sem, 16)
            _cnt = _qcnt.get(sem.name if hasattr(sem, 'name') else id(sem), 0) + 16
            _qcnt[sem.name if hasattr(sem, 'name') else id(sem)] = _cnt
            _qlast[0] = (sem, _cnt)
            return ins
        _qcnt = {}
        for w in range(2):
            g1h, g2h = (g1r_h, g2r_h) if w == 0 else (g1i_h, g2i_h)
            if w == 1:
                s.wait_ge(g("s_wb"), WB["mm0_7"])
            sdma(g1cm.rearrange("c (b d) -> c b d", b=32),
                 g1h.rearrange("a b c d -> c (a b) d"), g("s_wbs"))
            sdma(g2s, g2h.rearrange("c e f g -> c (e f g)"), g("s_wbs"))

        def xload(t_):
            if t_ >= 2:
                s.wait_ge(g("s_ln"), t_ - 1)
                s.wait_ge(g("s_sq"), t_ - 1)
            sdma(xt_t(t_), x_h[128 * t_:128 * (t_ + 1), :], g("s_x"))

        xload(0)
        xload(1)
        for w in range(2):
            for h in range(2):
                if (w, h) != (0, 0):
                    prevr = WBv[f"reorder{w}_{h - 1}"] if h == 1 else WBv[f"reorder{w - 1}_1"]
                    s.wait_ge(g("s_wbv"), prevr)
                s.wait_ge(g("s_wbd"), WBd[f"store{w}_{4 * h + 3}"])
                sdma(wbA.rearrange("p (b eg) -> p b eg", b=16),
                     wb_l1[w].ap()[512 * h:512 * (h + 1), :]
                     .rearrange("(b d) eg -> d b eg", b=16), g("s_wbs"))
        xload(2)
        xload(3)
        s.wait_ge(g("s_wq"), 32)
        for c in range(8):
            sdma(wqkvT(c), wqkv_bf.ap()[:, 128 * c:128 * (c + 1)], g("s_wt"), transpose=True)
        for t_ in range(4, NTILE):
            xload(t_)
        sdma(sbr[:, :], br_h[None, :], g("s_nb"))
        sdma(sbi[:, :], bi_h[None, :], g("s_nb"))
        s.wait_ge(g("s_ccn"), 1)
        sdma(nrm_sb[:, :], nrm_out.ap()[:, :], g("s_nb"))
        def kload(mb):
            par = mb % 2
            r, lb = mb // 4, mb % 4
            if mb >= 2:
                s.wait_ge(g("s_smm"), 4 * (mb - 1))
            sdma(regB[:, 4096 * par:4096 * (par + 1)].rearrange("p (c m) -> p c m", c=8),
                 kfull.ap()[1024 * r:1024 * (r + 1), 512 * lb:512 * (lb + 1)]
                 .rearrange("(c p) m -> p c m", p=128), g("s_kv"))
        s.wait_ge(g("s_cck"), 1)
        s.wait_ge(g("s_pq"), 96)
        for mb_ in range(NMB):
            kload(mb_)
        for t in range(NTILE):
            if t >= 2:
                s.wait_ge(g("s_x2"), t - 1)
            sdma(xt_t(t), x_h[128 * t:128 * (t + 1), :], g("s_x"))

    @blk.gpsimd
    def _(gp):
        _glast = [None]
        _gcnt = {}
        def gdma(out, in_, sem, **kw):
            if _glast[0] is not None:
                gp.wait_ge(*_glast[0])
            ins = gp.dma_start(out=out, in_=in_, **kw)
            ins.then_inc(sem, 16)
            k = sem.name if hasattr(sem, 'name') else id(sem)
            _gcnt[k] = _gcnt.get(k, 0) + 16
            _glast[0] = (sem, _gcnt[k])
            return ins
        gdma(eye_sb[:, :], eye_h.ap()[:, :], g("s_wq"))
        gdma(wqkv_bf.ap()[:, :], wqkv_h, g("s_wq"))
        for w in range(2):
            for j in range(8):
                gp.wait_ge(g("s_wbv"), WBv[f"evac{w}_{j}"])
                gdma(wb_l1[w].ap()[128 * j:128 * (j + 1), :], a_tmp[j % 2], g("s_wbd"))
            for h in range(2):
                gp.wait_ge(g("s_wbv"), WBv[f"reorder{w}_{h}"])
                gdma(wb_l2[w].ap().rearrange("(d gg) (b e) -> d gg b e", gg=16, b=32)[:, :, 16 * h:16 * (h + 1), :],
                     wbA2.rearrange("p (gg b e) -> p gg b e", gg=16, b=16), g("s_wbd"))
            wt = wtR if w == 0 else wtI
            for q in range(4):
                gdma(wt[:, 512 * q:512 * (q + 1)], wb_l2[w].ap()[128 * q:128 * (q + 1), :], g("s_wbd"))
        gp.wait_ge(g("s_nmsb"), 1)
        gdma(nrm_in.ap()[:, :], nrm_sb[:, :], g("s_nst"))
        gp.wait_ge(g("s_store"), 16 * KV_STORES)
        gp.wait_ge(g("s_nst"), 16)
        _gv_after_cc = True
        rg = [list(range(NCORE))]
        gp.collective_compute("AllReduce", ALU.max, replica_groups=rg,
                              ins=[nrm_in.ap()[:, :]], outs=[nrm_out.ap()[:, :]]).then_inc(g("s_ccn"))
        gp.collective_compute("AllGather", ALU.bypass, replica_groups=rg,
                              ins=[kslice.ap()[:, :]], outs=[kfull.ap()[:, :]]).then_inc(g("s_cck"))
        gp.collective_compute("AllGather", ALU.bypass, replica_groups=rg,
                              ins=[vslice.ap()[:, :]], outs=[vfull.ap()[:, :]]).then_inc(g("s_ccv"))
        _glast[0] = (g("s_ccv"), 1)
        gp.wait_ge(g("s_nm"), 8)
        for mb_ in range(NMB):
            par_ = mb_ % 2
            if mb_ >= 2:
                gp.wait_ge(g("s_pv"), 16 * (mb_ - 1))
            gdma(regC[:, 4096 * par_:4096 * (par_ + 1)].rearrange("p (c m) -> p c m", c=4),
                 vfull.ap()[512 * mb_:512 * (mb_ + 1), :]
                 .rearrange("(c p) m -> p c m", p=128), g("s_vv"))


    @blk.tensor
    def _(t):
        for w in range(2):
            t.wait_ge(g("s_wbs"), WBs[f"loads{w}"])
            for j in range(8):
                if not (w == 0 and j == 0):
                    prev = WBv[f"evac{w}_{j - 1}"] if j > 0 else WBv["evac0_7"]
                    t.wait_ge(g("s_wbv"), prev)
                t.matmul(psX[:, 0:256], g1cm[:, 128 * j:128 * (j + 1)], g2s,
                         start=True, stop=True).then_inc(g("s_wb"), 1)

        t.wait_ge(g("s_wq"), 16)

        def transpose_tile(t_idx, ph3):
            base = 128 * ph3
            for c in range(8):
                G = base + 8 * t_idx + c
                if c == 0:
                    t.wait_ge(g("s_ln"), 16 * ph3 + t_idx + 1)
                if G >= 2:
                    t.wait_ge(g("s_fm"), G - 1)
                t.matmul(pT[c % 2][:, 0:128],
                         xnf_t(t_idx)[:, 128 * c:128 * (c + 1)],
                         eye_sb[:, :], start=True, stop=True).then_inc(g("s_tp"), 1)

        def proj_block(b):
            par = b % 2
            gbase = 24 * b
            if b == 0:
                t.wait_ge(g("s_wt"), 128)
            t.wait_ge(g("s_fm"), 32 * (b + 1))
            grp = 0
            for ek in range(8):
                gi = gbase + grp
                if gi >= 2:
                    t.wait_ge(g("s_ev"), gi - 1)
                ps = ps0 if grp % 2 == 0 else ps1
                for c in range(8):
                    ins = t.matmul(ps[:, :], wqkvT(c)[:, 1024 + 128 * ek:1024 + 128 * (ek + 1)],
                                   xnfm(par, c), start=(c == 0), stop=(c == 7))
                    if c == 7:
                        ins.then_inc(g("s_pq"), 1)
                grp += 1
            for ek in range(8):
                t.wait_ge(g("s_ks"), 8 * b + ek + 1)
                if ek == 0 and b >= 1:
                    t.wait_ge(g("s_nf"), b)
                ins = t.matmul(psL[0:1, 0:512], ones_b[:, :], ksq(ek),
                               start=(ek == 0), stop=(ek == 7))
                if ek == 7:
                    ins.then_inc(g("s_nm"), 1)
            for eq in range(8):
                gi = gbase + grp
                t.wait_ge(g("s_ev"), gi - 1)
                ps = ps0 if grp % 2 == 0 else ps1
                for c in range(8):
                    ins = t.matmul(ps[:, :], wqkvT(c)[:, 128 * eq:128 * (eq + 1)],
                                   xnfm(par, c), start=(c == 0), stop=(c == 7))
                    if c == 7:
                        ins.then_inc(g("s_pq"), 1)
                grp += 1
            for i in range(8):
                nc4, h = i // 2, i % 2
                gi = gbase + grp
                t.wait_ge(g("s_ev"), gi - 1)
                ps = ps0 if grp % 2 == 0 else ps1
                for c in range(8):
                    ins = t.matmul(ps[:, :], xnfm(par, c)[:, 128 * nc4:128 * (nc4 + 1)],
                                   wqkvT(c)[:, 2048 + 512 * h:2048 + 512 * (h + 1)],
                                   start=(c == 0), stop=(c == 7))
                    if c == 7:
                        ins.then_inc(g("s_pq"), 1)
                grp += 1

        for b in range(NBLK):
            for tt in range(4 * b, 4 * b + 4):
                transpose_tile(tt, 0)
            if b >= 1:
                proj_block(b - 1)
        proj_block(NBLK - 1)

        for ns in range(4):
            t.wait_ge(g("s_qs"), 8 * (ns + 1))
            t.wait_ge(g("s_nf"), 4 + ns)
            for c in range(8):
                ins = t.matmul(psL[0:1, 0:512], ones_b[:, :], ksq(c),
                               start=(c == 0), stop=(c == 7))
                if c == 7:
                    ins.then_inc(g("s_nm"), 1)

        t.wait_ge(g("s_mi"), 2)
        t.matmul(psX[:, 0:1], ones_f[:, :], nstats[0:1, 5:6],
                 start=True, stop=True).then_inc(g("s_mi"), 1)

        def s_part(mb, mc, cpair):
            par = mb % 2
            EG = 4 * mb + mc
            if mc == 0 and cpair == 0:
                t.wait_ge(g("s_kv"), 16 * (mb + 1))
            if cpair == 0 and EG >= 1:
                t.wait_ge(g("s_exp"), 2 * EG)
            for c in (2 * cpair, 2 * cpair + 1):
                lhs = kbuf(par, c)[:, 128 * mc:128 * (mc + 1)]
                for sqs in range(4):
                    ins = t.matmul(_sbanks[sqs], lhs,
                                   qfm[:, 2048 * c + 512 * sqs:2048 * c + 512 * (sqs + 1)],
                                   start=(c == 0), stop=(c == 7))
                    if c == 7 and sqs == 3:
                        ins.then_inc(g("s_smm"), 1)

        def pv_group(mb, qc, tail=False):
            idx = 16 * mb + qc
            pr = (psB[:, 0:512], psB[:, 512:1024]) if (tail and idx % 2 == 1) else \
                 (pvpair[0][:, :], pvpair[1][:, :])
            if qc == 0:
                t.wait_ge(g("s_vv"), 16 * (mb + 1))
                t.wait_ge(g("s_exp"), 8 * (mb + 1))
            if idx == 0:
                t.wait_ge(g("s_mi"), 4)
            if idx >= 1:
                t.wait_ge(g("s_la"), idx)
                t.wait_ge(g("s_add"), idx - 1 if (tail and idx > 16 * (NMB - 2)) else idx)
            for sub in range(4):
                lhs = PT(mb % 3, sub)[:, 128 * qc:128 * (qc + 1)]
                t.matmul(pr[0], lhs, vbuf(mb % 2, sub)[:, 0:512],
                         start=(sub == 0), stop=(sub == 3))
                t.matmul(psX[:, 0:1], lhs, ones_b[:, :],
                         start=(sub == 0), stop=(sub == 3))
                ins = t.matmul(pr[1], lhs, vbuf(mb % 2, sub)[:, 512:1024],
                               start=(sub == 0), stop=(sub == 3))
                if sub == 3:
                    ins.then_inc(g("s_pv"), 1)

        for mb in range(NMB):
            for u in range(16):
                if mb >= 2:
                    pv_group(mb - 2, u)
                s_part(mb, u // 4, u % 4)
        t.wait_ge(g("s_exp"), 8 * NMB)
        for mb in (NMB - 2, NMB - 1):
            for qc in range(16):
                pv_group(mb, qc, tail=True)

        def ffn_group(qc):
            t.wait_ge(g("s_fm"), 128 + 8 * (qc + 1))
            t.wait_ge(g("s_nb"), 32)
            if qc >= 2:
                t.wait_ge(g("s_ev"), 96 + 2 * (qc - 1))
            pair = (ps0, ps1) if qc % 2 == 0 else (psL, psX)
            for half in range(2):
                ps = pair[half]
                wa, wb_ = (wtR, wtIn) if half == 0 else (wtI, wtR)
                bias = sbr if half == 0 else sbi
                for c in range(4):
                    t.matmul(ps[:, :], hfm(c)[:, 128 * qc:128 * (qc + 1)],
                             wa[:, 512 * c:512 * (c + 1)], start=(c == 0), stop=False)
                for c in range(4):
                    t.matmul(ps[:, :], hfm(4 + c)[:, 128 * qc:128 * (qc + 1)],
                             wb_[:, 512 * c:512 * (c + 1)], start=False, stop=False)
                t.matmul(ps[:, :], ones_f[:, :], bias[:, :],
                         start=False, stop=True).then_inc(g("s_pq"), 1)

        transpose_tile(0, 1)
        transpose_tile(1, 1)
        for qc in range(NTILE - 2):
            transpose_tile(qc + 2, 1)
            ffn_group(qc)
        ffn_group(NTILE - 2)
        ffn_group(NTILE - 1)

    @blk.scalar
    def _(sc):
        _acnt = [0]
        def amark(ins):
            ins.then_inc(g("s_act"), 1)
            _acnt[0] += 1
        def ajoin():
            sc.wait_ge(g("s_act"), _acnt[0])
        def ln_stats(t_idx, ph3, src_ap):
            par = t_idx % 2
            off = 16 * ph3
            if ph3:
                sc.wait_ge(g("s_x2"), t_idx + 1)
            else:
                sc.wait_ge(g("s_x"), 16 * (t_idx + 1))
            sc.activation(out=trash(), in_=src_ap, func=AF.Square,
                          accum_out=st_col(par, 4)).then_inc(g("s_sq"), 1)
            sc.wait_ge(g("s_st"), off + t_idx + 1)
            sc.wait_ge(g("s_sq"), off + t_idx + 1)
            amark(sc.activation(out=st_col(par, 5), in_=st_col(par, 4), func=AF.Ln,
                                scale=1.0 / D, bias=st_col(par, 3)))
            ajoin()
            sc.activation(out=st_col(par, 6), in_=st_col(par, 5), func=AF.Exp,
                          scale=-0.5).then_inc(g("s_rs"), 1)

        def evac_block(b):
            gbase = 24 * b
            for ek in range(8):
                sc.wait_ge(g("s_pq"), gbase + ek + 1)
                kk = 8 * b + ek
                if kk >= 2:
                    sc.wait_ge(g("s_store"), 16 * (16 * b + ek))
                if ek == 0 and b >= 1:
                    sc.wait_ge(g("s_nm"), b)
                ps = ps0 if ek % 2 == 0 else ps1
                sc.activation(out=ksq(ek), in_=ps[:, :], func=AF.Square).then_inc(g("s_ks"), 1)
                sc.activation(out=ktile(ek % 2), in_=ps[:, :], func=AF.Copy).then_inc(g("s_ev"), 1)
                sc.wait_ge(g("s_ev"), gbase + ek + 1)
                sc.dma_start(out=kslice.ap()[128 * ek:128 * (ek + 1), 512 * b:512 * (b + 1)],
                             in_=ktile(ek % 2)).then_inc(g("s_store"), 16)
            for eq in range(8):
                sc.wait_ge(g("s_pq"), gbase + 8 + eq + 1)
                ps = ps0 if eq % 2 == 0 else ps1
                sc.activation(out=qfm[:, 2048 * eq + 512 * b:2048 * eq + 512 * (b + 1)],
                              in_=ps[:, :], func=AF.Copy, scale=QSCALE).then_inc(g("s_ev"), 1)
            for i in range(8):
                sc.wait_ge(g("s_pq"), gbase + 16 + i + 1)
                vv = 8 * b + i
                if vv >= 2:
                    sc.wait_ge(g("s_store"), 16 * (16 * b + 8 + i))
                nc4, h = i // 2, i % 2
                ps = ps0 if i % 2 == 0 else ps1
                sc.activation(out=vtile(i % 2), in_=ps[:, :], func=AF.Copy).then_inc(g("s_ev"), 1)
                sc.wait_ge(g("s_ev"), gbase + 16 + i + 1)
                sc.dma_start(out=vslice.ap()[512 * b + 128 * nc4:512 * b + 128 * (nc4 + 1),
                                             512 * h:512 * (h + 1)],
                             in_=vtile(i % 2)).then_inc(g("s_store"), 16)

        for b in range(NBLK):
            for tt in range(4 * b, 4 * b + 4):
                ln_stats(tt, 0, xt_t(tt))
            if b >= 1:
                evac_block(b - 1)
        evac_block(NBLK - 1)

        for ns in range(4):
            sc.wait_ge(g("s_nm"), 4 + ns)
            for c in range(8):
                sc.activation(out=ksq(c), in_=qfm[:, 2048 * c + 512 * ns:2048 * c + 512 * (ns + 1)],
                              func=AF.Square).then_inc(g("s_qs"), 1)

        sc.wait_ge(g("s_mi"), 1)
        amark(sc.activation(out=nstats[0:1, 4:5], in_=nstats[0:1, 3:4], func=AF.Sqrt))
        ajoin()
        sc.activation(out=nstats[0:1, 5:6], in_=nstats[0:1, 4:5], func=AF.Copy,
                      scale=-1.0).then_inc(g("s_mi"), 1)

        ecount = 0
        for mb in range(NMB):
            u = mb % 3
            if mb >= 3:
                sc.wait_ge(g("s_pv"), 16 * (mb - 2))
            for mc in range(4):
                sc.wait_ge(g("s_smm"), ecount + 1)
                if ecount == 0:
                    sc.wait_ge(g("s_mi"), 4)
                sc.activation(out=PT(u, mc)[:, 0:1024], in_=psA[:, :], func=AF.Exp,
                              bias=negm_bc[:, :]).then_inc(g("s_exp"), 1)
                sc.activation(out=PT(u, mc)[:, 1024:2048], in_=psB[:, :], func=AF.Exp,
                              bias=negm_bc[:, :]).then_inc(g("s_exp"), 1)
                ecount += 1

        for tt in range(NTILE):
            ln_stats(tt, 1, oacc_t(tt))
        for qc in range(NTILE):
            sc.wait_ge(g("s_ev"), 96 + 2 * (qc + 1))
            sc.dma_start(out=out_h[128 * qc:128 * (qc + 1), :],
                         in_=oacc_t(qc)).then_inc(g("s_store"), 16)
        sc.wait_ge(g("s_store"), 16 * (KV_STORES + NTILE))

    @blk.vector
    def _(v):
        _dcnt = [0]
        def dmark(ins):
            ins.then_inc(g("s_dve"), 1)
            _dcnt[0] += 1
        def djoin():
            v.wait_ge(g("s_dve"), _dcnt[0])
        v.memset(ones_f[:, :], 1.0)
        v.memset(ones_b[:, :], 1.0)
        v.memset(seps[:, :], EPS)
        for w in range(2):
            for j in range(8):
                v.wait_ge(g("s_wb"), WB[f"mm{w}_{j}"])
                if (w, j) > (0, 1):
                    pj = 16 * (14 * w + j) if j >= 2 else 16 * 14
                    v.wait_ge(g("s_wbd"), pj)
                v.tensor_copy(out=a_tmp[j % 2], in_=psX[:, 0:256]).then_inc(g("s_wbv"), 1)
            for h in range(2):
                v.wait_ge(g("s_wbs"), WBs[f"read1_{w}_{h}"])
                if not (w == 0 and h == 0):
                    v.wait_ge(g("s_wbd"), WBd[f"write2_{w}_{h - 1}"] if h == 1
                              else WBd["write2_0_1"])
                v.tensor_copy(out=wbA2.rearrange("p (gg b e) -> p gg b e", gg=16, b=16),
                              in_=wbA.rearrange("p (b e gg) -> p gg b e", b=16, e=16)) \
                    .then_inc(g("s_wbv"), 1)
        v.wait_ge(g("s_wbd"), WBD_ALL)
        ins = None
        for q in range(4):
            ins = v.tensor_scalar_mul(out=wtIn[:, 512 * q:512 * (q + 1)],
                                      in0=wtI[:, 512 * q:512 * (q + 1)], scalar1=-1.0)
        ins.then_inc(g("s_wbv"), 1)

        def ln_vec(t_idx, ph3, src_ap):
            par = t_idx % 2
            off = 16 * ph3
            if not ph3:
                v.wait_ge(g("s_x"), 16 * (t_idx + 1))
            dmark(v.reduce_sum(st_col(par, 0), src_ap, axis=AX.X))
            djoin()
            dmark(v.tensor_scalar_mul(out=st_col(par, 1), in0=st_col(par, 0), scalar1=1.0 / D))
            djoin()
            dmark(v.scalar_tensor_tensor(out=st_col(par, 2), in0=st_col(par, 1), scalar=-1.0,
                                         in1=st_col(par, 1), op0=ALU.mult, op1=ALU.mult))
            djoin()
            v.tensor_scalar_add(out=st_col(par, 3), in0=st_col(par, 2),
                                scalar1=EPS).then_inc(g("s_st"), 1)
            v.wait_ge(g("s_rs"), off + t_idx + 1)
            dmark(v.scalar_tensor_tensor(out=st_col(par, 7), in0=st_col(par, 1), scalar=-1.0,
                                         in1=st_col(par, 6), op0=ALU.mult, op1=ALU.mult))
            if t_idx >= 2:
                v.wait_ge(g("s_tp"), 128 * ph3 + 8 * (t_idx - 1))
            djoin()
            v.tensor_scalar(out=xnf_t(t_idx), in0=src_ap,
                            scalar1=st_col(par, 6), scalar2=st_col(par, 7),
                            op0=ALU.mult, op1=ALU.add).then_inc(g("s_ln"), 1)

        def evac_tile(t_idx, ph3):
            for c in range(8):
                G = 128 * ph3 + 8 * t_idx + c
                v.wait_ge(g("s_tp"), G + 1)
                if (not ph3) and t_idx % 4 == 0 and c == 0 and t_idx // 4 >= 2:
                    v.wait_ge(g("s_pq"), 24 * (t_idx // 4 - 1))
                if ph3:
                    dst = regA[:, 2048 * c + 128 * t_idx:2048 * c + 128 * (t_idx + 1)]
                else:
                    par = (t_idx // 4) % 2
                    dst = regB[:, 4096 * par + 512 * c + 128 * (t_idx % 4):
                               4096 * par + 512 * c + 128 * (t_idx % 4) + 128]
                v.tensor_copy(out=dst, in_=pT[c % 2][:, 0:128]).then_inc(g("s_fm"), 1)

        def knorm_read(b):
            v.wait_ge(g("s_nm"), b + 1)
            if b == 0:
                v.reduce_max(nstats[0:1, 0:1], psL[0:1, 0:512], axis=AX.X).then_inc(g("s_nf"), 1)
            else:
                dmark(v.reduce_max(nstats[0:1, 2:3], psL[0:1, 0:512], axis=AX.X))
                djoin()
                v.tensor_max(out=nstats[0:1, 0:1], in0=nstats[0:1, 0:1],
                             in1=nstats[0:1, 2:3]).then_inc(g("s_nf"), 1)

        for tt in range(NTILE):
            if tt >= 9 and (tt - 9) % 4 == 0:
                knorm_read((tt - 9) // 4)
            if tt >= 1:
                evac_tile(tt - 1, 0)
            ln_vec(tt, 0, xt_t(tt))
        evac_tile(NTILE - 1, 0)
        knorm_read(NBLK - 2)
        knorm_read(NBLK - 1)
        djoin()
        dmark(v.memset(nrm_sb[:, :], 0.0))
        djoin()
        v.tensor_scalar_add(out=nrm_sb[:, :], in0=nrm_sb[:, :],
                            scalar1=nstats[0:1, 0:1]).then_inc(g("s_nmsb"), 1)
        for ns in range(4):
            v.wait_ge(g("s_nm"), 5 + ns)
            if ns == 0:
                v.reduce_max(nstats[0:1, 1:2], psL[0:1, 0:512], axis=AX.X).then_inc(g("s_nf"), 1)
            else:
                dmark(v.reduce_max(nstats[0:1, 2:3], psL[0:1, 0:512], axis=AX.X))
                djoin()
                v.tensor_max(out=nstats[0:1, 1:2], in0=nstats[0:1, 1:2],
                             in1=nstats[0:1, 2:3]).then_inc(g("s_nf"), 1)
        v.wait_ge(g("s_nb"), 48)
        v.wait_ge(g("s_nf"), 8)
        dmark(v.reduce_max(nstats[0:1, 2:3], nrm_sb[0:1, :], axis=AX.X))
        djoin()
        v.tensor_mul(out=nstats[0:1, 3:4], in0=nstats[0:1, 2:3],
                     in1=nstats[0:1, 1:2]).then_inc(g("s_mi"), 1)
        v.wait_ge(g("s_mi"), 3)
        v.tensor_copy(out=negm_bc[:, :], in_=psX[:, 0:1]).then_inc(g("s_mi"), 1)

        v.wait_ge(g("s_wbd"), WBD_ALL)
        for mb in range(NMB):
            for qc in range(16):
                idx = 16 * mb + qc
                tail = mb >= NMB - 2
                pva, pvb = (psB[:, 0:512], psB[:, 512:1024]) if (tail and idx % 2 == 1) else \
                           (pvpair[0][:, :], pvpair[1][:, :])
                v.wait_ge(g("s_pv"), idx + 1)
                if mb == 0:
                    v.tensor_copy(out=L_sb[:, qc:qc + 1], in_=psX[:, 0:1]).then_inc(g("s_la"), 1)
                    v.tensor_copy(out=oacc[:, 1024 * qc:1024 * qc + 512], in_=pva)
                    v.tensor_copy(out=oacc[:, 1024 * qc + 512:1024 * (qc + 1)],
                                  in_=pvb).then_inc(g("s_add"), 1)
                else:
                    v.tensor_add(out=L_sb[:, qc:qc + 1], in0=psX[:, 0:1],
                                 in1=L_sb[:, qc:qc + 1]).then_inc(g("s_la"), 1)
                    v.tensor_add(out=oacc[:, 1024 * qc:1024 * qc + 512],
                                 in0=pva, in1=oacc[:, 1024 * qc:1024 * qc + 512])
                    v.tensor_add(out=oacc[:, 1024 * qc + 512:1024 * (qc + 1)],
                                 in0=pvb,
                                 in1=oacc[:, 1024 * qc + 512:1024 * (qc + 1)]).then_inc(g("s_add"), 1)
        v.wait_ge(g("s_la"), NMB * 16)
        v.reciprocal(out=rl_sb[:, :], in_=L_sb[:, :]).then_inc(g("s_lrl"), 1)

        def ffn_evac(qc):
            v.wait_ge(g("s_pq"), 96 + 2 * (qc + 1))
            pair = (ps0, ps1) if qc % 2 == 0 else (psL, psX)
            v.tensor_add(out=oacc[:, 1024 * qc:1024 * qc + 512],
                         in0=pair[0][:, :], in1=oacc[:, 1024 * qc:1024 * qc + 512])
            v.tensor_add(out=oacc[:, 1024 * qc + 512:1024 * (qc + 1)],
                         in0=pair[1][:, :],
                         in1=oacc[:, 1024 * qc + 512:1024 * (qc + 1)]).then_inc(g("s_ev"), 2)

        for tt in range(NTILE):
            v.wait_ge(g("s_x"), 256 + 16 * (tt + 1))
            if tt == 0:
                v.wait_ge(g("s_lrl"), 1)
            v.scalar_tensor_tensor(out=oacc_t(tt), in0=oacc_t(tt),
                                   scalar=rl_sb[:, tt:tt + 1], in1=xt_t(tt),
                                   op0=ALU.mult, op1=ALU.add).then_inc(g("s_x2"), 1)
            if tt >= 1:
                evac_tile(tt - 1, 1)
            v.wait_ge(g("s_x2"), tt + 1)
            ln_vec(tt, 1, oacc_t(tt))
            if tt >= 3:
                ffn_evac(tt - 3)
        evac_tile(NTILE - 1, 1)
        ffn_evac(NTILE - 3)
        ffn_evac(NTILE - 2)
        ffn_evac(NTILE - 1)

    return nc, ctx


_cache = {}


def kernel(L0_attn_x, L0_attn_W_qkv, L0_ffn_G1_R, L0_ffn_G2_R, L0_ffn_G1_I,
           L0_ffn_G2_I, L0_ffn_bias_real, L0_ffn_bias_imag):
    x2d = np.ascontiguousarray(np.asarray(L0_attn_x, dtype=np.float32).reshape(NTOK, D))
    wqkv = np.ascontiguousarray(np.asarray(L0_attn_W_qkv, dtype=np.float32))
    g1r = np.ascontiguousarray(np.asarray(L0_ffn_G1_R, dtype=np.float32))
    g2r = np.ascontiguousarray(np.asarray(L0_ffn_G2_R, dtype=np.float32))
    g1i = np.ascontiguousarray(np.asarray(L0_ffn_G1_I, dtype=np.float32))
    g2i = np.ascontiguousarray(np.asarray(L0_ffn_G2_I, dtype=np.float32))
    br = np.ascontiguousarray(np.asarray(L0_ffn_bias_real, dtype=np.float32))
    bi = np.ascontiguousarray(np.asarray(L0_ffn_bias_imag, dtype=np.float32))

    if "nc" not in _cache:
        _cache["nc"] = build_nc()
    nc, _ctx = _cache["nc"]

    in_maps = [dict(x=x2d[NT * c:NT * (c + 1)], wqkv=wqkv, g1r=g1r, g2r=g2r,
                    g1i=g1i, g2i=g2i, br=br, bi=bi) for c in range(NCORE)]
    res = run_bass_kernel_spmd(nc, in_maps, core_ids=list(range(NCORE)))
    out = np.concatenate([res.results[c]["out"] for c in range(NCORE)], axis=0)
    return out.reshape(1, NTOK, D)


# version 11
# speedup vs baseline: 1.7891x; 1.0847x over previous
import numpy as np
from contextlib import ExitStack

import concourse.bass as bass
import concourse.mybir as mybir
from concourse.bass_utils import run_bass_kernel_spmd

F32 = mybir.dt.float32
BF16 = mybir.dt.bfloat16
AF = mybir.ActivationFunctionType
ALU = mybir.AluOpType
AX = mybir.AxisListType

NCORE = 8
NTOK = 16384
D = 1024
NT = NTOK // NCORE
NTILE = NT // 128
NBLK = NT // 512
MB = 512
NMB = NTOK // MB
NPAIR = NMB // 2
EPS = 1e-5
QSCALE = 1.0 / 32.0


def build_nc():
    nc = bass.Bass("TRN2", target_bir_lowering=False)

    x_h = nc.dram_tensor("x", [NT, D], F32, kind="ExternalInput").ap()
    wqkv_h = nc.dram_tensor("wqkv", [3 * D, D], F32, kind="ExternalInput").ap()
    g1r_h = nc.dram_tensor("g1r", [1, 32, 4, 32], F32, kind="ExternalInput").ap()
    g2r_h = nc.dram_tensor("g2r", [4, 16, 1, 16], F32, kind="ExternalInput").ap()
    g1i_h = nc.dram_tensor("g1i", [1, 32, 4, 32], F32, kind="ExternalInput").ap()
    g2i_h = nc.dram_tensor("g2i", [4, 16, 1, 16], F32, kind="ExternalInput").ap()
    br_h = nc.dram_tensor("br", [512], F32, kind="ExternalInput").ap()
    bi_h = nc.dram_tensor("bi", [512], F32, kind="ExternalInput").ap()
    out_h = nc.dram_tensor("out", [NT, D], F32, kind="ExternalOutput").ap()

    wqkv_bf = nc.dram_tensor("wqkv_bf", [3 * D, D], BF16, kind="Internal")
    kslice = [nc.dram_tensor(f"kslice{j}", [D, 512], BF16, kind="Internal") for j in range(4)]
    vslice = [nc.dram_tensor(f"vslice{j}", [512, D], BF16, kind="Internal") for j in range(4)]
    kfull = [nc.dram_tensor(f"kfull{j}", [NCORE * D, 512], BF16, kind="Internal") for j in range(4)]
    vfull = [nc.dram_tensor(f"vfull{j}", [NCORE * 512, D], BF16, kind="Internal") for j in range(4)]
    nrm_in = nc.dram_tensor("nrm_in", [1, 512], F32, kind="Internal")
    nrm_out = nc.dram_tensor("nrm_out", [1, 512], F32, kind="Internal")
    wb_l1 = [nc.dram_tensor(f"wb_l1_{w}", [1024, 256], F32, kind="Internal") for w in range(2)]
    wb_l2 = [nc.dram_tensor(f"wb_l2_{w}", [512, 512], F32, kind="Internal") for w in range(2)]
    eye_h = nc.inline_tensor(np.eye(128, dtype=np.float32), "eyec")

    ctx = ExitStack()

    regA = ctx.enter_context(nc.sbuf_tensor("regA", [128, 24576], BF16))
    regB = ctx.enter_context(nc.sbuf_tensor("regB", [128, 8192], BF16))
    regC = ctx.enter_context(nc.sbuf_tensor("regC", [128, 8192], BF16))
    wtR = ctx.enter_context(nc.sbuf_tensor("wtR", [128, 2048], BF16))
    wtI = ctx.enter_context(nc.sbuf_tensor("wtI", [128, 2048], BF16))
    wtIn = ctx.enter_context(nc.sbuf_tensor("wtIn", [128, 2048], BF16))
    qfm = ctx.enter_context(nc.sbuf_tensor("qfm", [128, 16384], BF16))
    oacc = ctx.enter_context(nc.sbuf_tensor("oacc", [128, 16384], F32))
    xt = ctx.enter_context(nc.sbuf_tensor("xt", [128, 2048], F32))
    xnf = ctx.enter_context(nc.sbuf_tensor("xnf", [128, 2048], BF16))
    eye_sb = ctx.enter_context(nc.sbuf_tensor("eye_sb", [128, 128], BF16))
    stats = ctx.enter_context(nc.sbuf_tensor("stats", [128, 16], F32))
    nstats = ctx.enter_context(nc.sbuf_tensor("nstats", [1, 16], F32))
    seps = ctx.enter_context(nc.sbuf_tensor("seps", [128, 1], F32))
    negm_bc = ctx.enter_context(nc.sbuf_tensor("negm_bc", [128, 1], F32))
    L_sb = ctx.enter_context(nc.sbuf_tensor("L_sb", [128, 16], F32))
    rl_sb = ctx.enter_context(nc.sbuf_tensor("rl_sb", [128, 16], F32))
    nrm_sb = ctx.enter_context(nc.sbuf_tensor("nrm_sb", [1, 512], F32))
    ones_f = ctx.enter_context(nc.sbuf_tensor("ones_f", [1, 128], F32))
    ones_b = ctx.enter_context(nc.sbuf_tensor("ones_b", [128, 1], BF16))
    sbr = ctx.enter_context(nc.sbuf_tensor("sbr", [1, 512], F32))
    sbi = ctx.enter_context(nc.sbuf_tensor("sbi", [1, 512], F32))
    g1cm = oacc[0:4, 0:1024]
    g2s = oacc[0:4, 1024:1280]
    a_tmp = [oacc[:, 1280:1536], oacc[:, 1536:1792]]
    wbA = oacc[0:32, 2048:6144]
    wbA2 = oacc[0:32, 6144:10240]
    assert nc.sbuf_bytes_remaining >= 0

    psA = ctx.enter_context(nc.psum_tensor("psA", [128, 1024], F32))
    psL = ctx.enter_context(nc.psum_tensor("psL", [128, 512], F32))
    psX = ctx.enter_context(nc.psum_tensor("psX", [128, 512], F32))
    ptctx = ExitStack()
    pT0 = ptctx.enter_context(nc.psum_tensor("pT0", [128, 512], F32))
    pT1 = ptctx.enter_context(nc.psum_tensor("pT1", [128, 512], F32))
    ptctx.close()
    psB = ctx.enter_context(nc.psum_tensor("psB", [128, 1024], F32))
    psV10 = ctx.enter_context(nc.psum_tensor("psV10", [128, 512], F32))
    psV11 = ctx.enter_context(nc.psum_tensor("psV11", [128, 512], F32))
    ps0 = psA[:, 0:512]
    ps1 = psA[:, 512:1024]
    _sbanks = [psA[:, 0:512], psA[:, 512:1024], psB[:, 0:512], psB[:, 512:1024]]
    pvpair = (psV10, psV11)
    pT = [pT0, pT1]

    semnames = ("s_dve s_act s_wb s_wbv s_wbs s_wbd s_wq s_wt s_x s_sq s_st s_rs s_ln s_tp s_fm s_pq s_ev "
                "s_ks s_qs s_nm s_nf s_nmsb s_store s_kst s_vst s_nst s_ccn s_cck s_ccv s_nb s_mi s_kv s_smm "
                "s_exp s_pv s_add s_lrl s_x2 s_la s_vv").split()
    sems = {n: ctx.enter_context(nc.semaphore(n)) for n in semnames}
    g = lambda n: sems[n]

    blk = ctx.enter_context(nc.Block())

    def wqkvT(c):
        return regA[:, 3072 * c:3072 * (c + 1)]
    def PT(u, mc):
        return regA[:, 8192 * u + 2048 * mc:8192 * u + 2048 * (mc + 1)]
    def hfm(c):
        return regA[:, 2048 * c:2048 * (c + 1)]
    def xnfm(par, c):
        return regB[:, 4096 * par + 512 * c:4096 * par + 512 * (c + 1)]
    def kbuf(par, c):
        return regB[:, 4096 * par + 512 * c:4096 * par + 512 * (c + 1)]
    def ktile(j):
        return regC[:, 512 * j:512 * (j + 1)]
    def vtile(j):
        return regC[:, 1024 + 512 * j:1024 + 512 * (j + 1)]
    def ksq(c):
        return regC[:, 2048 + 512 * c:2048 + 512 * (c + 1)]
    def trash():
        return regC[:, 6144:7168]
    def vbuf(par, mc):
        return regC[:, 4096 * par + 1024 * mc:4096 * par + 1024 * (mc + 1)]
    def xnf_t(t_idx):
        return xnf[:, 1024 * (t_idx % 2):1024 * (t_idx % 2 + 1)]
    def xt_t(t_idx):
        return xt[:, 1024 * (t_idx % 2):1024 * (t_idx % 2 + 1)]
    def oacc_t(t_idx):
        return oacc[:, 1024 * t_idx:1024 * (t_idx + 1)]
    def st_col(par, i):
        return stats[:, 8 * par + i:8 * par + i + 1]

    WB, WBv, WBs, WBd = {}, {}, {}, {}
    for w in range(2):
        for j in range(8):
            WB[f"mm{w}_{j}"] = 8 * w + j + 1
            WBv[f"evac{w}_{j}"] = 10 * w + j + 1
            WBd[f"store{w}_{j}"] = 16 * (14 * w + j + 1)
        for h in range(2):
            WBv[f"reorder{w}_{h}"] = 10 * w + 9 + h
            WBs[f"read1_{w}_{h}"] = 80 + 32 * w + 16 * h
            WBd[f"write2_{w}_{h}"] = 16 * (14 * w + 9 + h)
        WBs[f"loads{w}"] = 32 + 32 * w
        for q in range(4):
            WBd[f"read2_{w}_{q}"] = 16 * (14 * w + 11 + q)
    WB_ALL = 16
    WBV_ALL = 20
    WBD_ALL = 16 * 28
    KV_STORES = 16 * NBLK
    ST_ALL_P1 = KV_STORES + 1
    ST_ALL = ST_ALL_P1 + NTILE

    @blk.sync
    def _(sync):
        s = sync
        _qlast = [None]
        def sdma(out, in_, sem, **kw):
            if _qlast[0] is not None:
                s.wait_ge(*_qlast[0])
            ins = s.dma_start(out=out, in_=in_, **kw)
            ins.then_inc(sem, 16)
            _cnt = _qcnt.get(sem.name if hasattr(sem, 'name') else id(sem), 0) + 16
            _qcnt[sem.name if hasattr(sem, 'name') else id(sem)] = _cnt
            _qlast[0] = (sem, _cnt)
            return ins
        _qcnt = {}
        for w in range(2):
            g1h, g2h = (g1r_h, g2r_h) if w == 0 else (g1i_h, g2i_h)
            if w == 1:
                s.wait_ge(g("s_wb"), WB["mm0_7"])
            sdma(g1cm.rearrange("c (b d) -> c b d", b=32),
                 g1h.rearrange("a b c d -> c (a b) d"), g("s_wbs"))
            sdma(g2s, g2h.rearrange("c e f g -> c (e f g)"), g("s_wbs"))

        def xload(t_):
            if t_ >= 2:
                s.wait_ge(g("s_ln"), t_ - 1)
                s.wait_ge(g("s_sq"), t_ - 1)
            sdma(xt_t(t_), x_h[128 * t_:128 * (t_ + 1), :], g("s_x"))

        xload(0)
        xload(1)
        for w in range(2):
            for h in range(2):
                if (w, h) != (0, 0):
                    prevr = WBv[f"reorder{w}_{h - 1}"] if h == 1 else WBv[f"reorder{w - 1}_1"]
                    s.wait_ge(g("s_wbv"), prevr)
                s.wait_ge(g("s_wbd"), WBd[f"store{w}_{4 * h + 3}"])
                sdma(wbA.rearrange("p (b eg) -> p b eg", b=16),
                     wb_l1[w].ap()[512 * h:512 * (h + 1), :]
                     .rearrange("(b d) eg -> d b eg", b=16), g("s_wbs"))
        xload(2)
        xload(3)
        s.wait_ge(g("s_wq"), 32)
        for c in range(8):
            sdma(wqkvT(c), wqkv_bf.ap()[:, 128 * c:128 * (c + 1)], g("s_wt"), transpose=True)
        for t_ in range(4, NTILE):
            xload(t_)
        sdma(sbr[:, :], br_h[None, :], g("s_nb"))
        sdma(sbi[:, :], bi_h[None, :], g("s_nb"))
        def kload(mb):
            par = mb % 2
            r, lb = mb // 4, mb % 4
            if mb < 4:
                s.wait_ge(g("s_cck"), lb + 1)
            if mb >= 2:
                s.wait_ge(g("s_smm"), 4 * (mb - 1))
            sdma(regB[:, 4096 * par:4096 * (par + 1)].rearrange("p (c m) -> p c m", c=8),
                 kfull[lb].ap()[1024 * r:1024 * (r + 1), :]
                 .rearrange("(c p) m -> p c m", p=128), g("s_kv"))
        s.wait_ge(g("s_pq"), 96)
        kload(0)
        kload(1)
        s.wait_ge(g("s_ccn"), 1)
        sdma(nrm_sb[:, :], nrm_out.ap()[:, :], g("s_nb"))
        for mb_ in range(2, NMB):
            kload(mb_)
        for t in range(NTILE):
            if t >= 2:
                s.wait_ge(g("s_x2"), t - 1)
            sdma(xt_t(t), x_h[128 * t:128 * (t + 1), :], g("s_x"))

    @blk.gpsimd
    def _(gp):
        _glast = [None]
        _gcnt = {}
        def gdma(out, in_, sem, **kw):
            if _glast[0] is not None:
                gp.wait_ge(*_glast[0])
            ins = gp.dma_start(out=out, in_=in_, **kw)
            ins.then_inc(sem, 16)
            k = sem.name if hasattr(sem, 'name') else id(sem)
            _gcnt[k] = _gcnt.get(k, 0) + 16
            _glast[0] = (sem, _gcnt[k])
            return ins
        gdma(eye_sb[:, :], eye_h.ap()[:, :], g("s_wq"))
        gdma(wqkv_bf.ap()[:, :], wqkv_h, g("s_wq"))
        for w in range(2):
            for j in range(8):
                gp.wait_ge(g("s_wbv"), WBv[f"evac{w}_{j}"])
                gdma(wb_l1[w].ap()[128 * j:128 * (j + 1), :], a_tmp[j % 2], g("s_wbd"))
            for h in range(2):
                gp.wait_ge(g("s_wbv"), WBv[f"reorder{w}_{h}"])
                gdma(wb_l2[w].ap().rearrange("(d gg) (b e) -> d gg b e", gg=16, b=32)[:, :, 16 * h:16 * (h + 1), :],
                     wbA2.rearrange("p (gg b e) -> p gg b e", gg=16, b=16), g("s_wbd"))
            wt = wtR if w == 0 else wtI
            for q in range(4):
                gdma(wt[:, 512 * q:512 * (q + 1)], wb_l2[w].ap()[128 * q:128 * (q + 1), :], g("s_wbd"))
        rg = [list(range(NCORE))]
        for b2 in range(4):
            gp.wait_ge(g("s_kst"), 16 * 8 * (b2 + 1))
            gp.collective_compute("AllGather", ALU.bypass, replica_groups=rg,
                                  ins=[kslice[b2].ap()[:, :]], outs=[kfull[b2].ap()[:, :]]).then_inc(g("s_cck"))
        gp.wait_ge(g("s_nmsb"), 1)
        gdma(nrm_in.ap()[:, :], nrm_sb[:, :], g("s_nst"))
        gp.wait_ge(g("s_nst"), 16)
        gp.collective_compute("AllReduce", ALU.max, replica_groups=rg,
                              ins=[nrm_in.ap()[:, :]], outs=[nrm_out.ap()[:, :]]).then_inc(g("s_ccn"))
        for b2 in range(4):
            gp.wait_ge(g("s_vst"), 16 * 8 * (b2 + 1))
            gp.collective_compute("AllGather", ALU.bypass, replica_groups=rg,
                                  ins=[vslice[b2].ap()[:, :]], outs=[vfull[b2].ap()[:, :]]).then_inc(g("s_ccv"))
        _glast[0] = (g("s_ccv"), 1)
        gp.wait_ge(g("s_nm"), 8)
        for mb_ in range(NMB):
            par_ = mb_ % 2
            r_, lb_ = mb_ // 4, mb_ % 4
            if mb_ < 4:
                gp.wait_ge(g("s_ccv"), lb_ + 1)
            if mb_ >= 2:
                gp.wait_ge(g("s_pv"), 16 * (mb_ - 1))
            gdma(regC[:, 4096 * par_:4096 * (par_ + 1)].rearrange("p (c m) -> p c m", c=4),
                 vfull[lb_].ap()[512 * r_:512 * (r_ + 1), :]
                 .rearrange("(c p) m -> p c m", p=128), g("s_vv"))


    @blk.tensor
    def _(t):
        for w in range(2):
            t.wait_ge(g("s_wbs"), WBs[f"loads{w}"])
            for j in range(8):
                if not (w == 0 and j == 0):
                    prev = WBv[f"evac{w}_{j - 1}"] if j > 0 else WBv["evac0_7"]
                    t.wait_ge(g("s_wbv"), prev)
                t.matmul(psX[:, 0:256], g1cm[:, 128 * j:128 * (j + 1)], g2s,
                         start=True, stop=True).then_inc(g("s_wb"), 1)

        t.wait_ge(g("s_wq"), 16)

        def transpose_tile(t_idx, ph3):
            base = 128 * ph3
            for c in range(8):
                G = base + 8 * t_idx + c
                if c == 0:
                    t.wait_ge(g("s_ln"), 16 * ph3 + t_idx + 1)
                if G >= 2:
                    t.wait_ge(g("s_fm"), G - 1)
                t.matmul(pT[c % 2][:, 0:128],
                         xnf_t(t_idx)[:, 128 * c:128 * (c + 1)],
                         eye_sb[:, :], start=True, stop=True).then_inc(g("s_tp"), 1)

        def proj_block(b):
            par = b % 2
            gbase = 24 * b
            if b == 0:
                t.wait_ge(g("s_wt"), 128)
            t.wait_ge(g("s_fm"), 32 * (b + 1))
            grp = 0
            for ek in range(8):
                gi = gbase + grp
                if gi >= 2:
                    t.wait_ge(g("s_ev"), gi - 1)
                ps = ps0 if grp % 2 == 0 else ps1
                for c in range(8):
                    ins = t.matmul(ps[:, :], wqkvT(c)[:, 1024 + 128 * ek:1024 + 128 * (ek + 1)],
                                   xnfm(par, c), start=(c == 0), stop=(c == 7))
                    if c == 7:
                        ins.then_inc(g("s_pq"), 1)
                grp += 1
            for ek in range(8):
                t.wait_ge(g("s_ks"), 8 * b + ek + 1)
                if ek == 0 and b >= 1:
                    t.wait_ge(g("s_nf"), b)
                ins = t.matmul(psL[0:1, 0:512], ones_b[:, :], ksq(ek),
                               start=(ek == 0), stop=(ek == 7))
                if ek == 7:
                    ins.then_inc(g("s_nm"), 1)
            for eq in range(8):
                gi = gbase + grp
                t.wait_ge(g("s_ev"), gi - 1)
                ps = ps0 if grp % 2 == 0 else ps1
                for c in range(8):
                    ins = t.matmul(ps[:, :], wqkvT(c)[:, 128 * eq:128 * (eq + 1)],
                                   xnfm(par, c), start=(c == 0), stop=(c == 7))
                    if c == 7:
                        ins.then_inc(g("s_pq"), 1)
                grp += 1
            for i in range(8):
                nc4, h = i // 2, i % 2
                gi = gbase + grp
                t.wait_ge(g("s_ev"), gi - 1)
                ps = ps0 if grp % 2 == 0 else ps1
                for c in range(8):
                    ins = t.matmul(ps[:, :], xnfm(par, c)[:, 128 * nc4:128 * (nc4 + 1)],
                                   wqkvT(c)[:, 2048 + 512 * h:2048 + 512 * (h + 1)],
                                   start=(c == 0), stop=(c == 7))
                    if c == 7:
                        ins.then_inc(g("s_pq"), 1)
                grp += 1

        for b in range(NBLK):
            for tt in range(4 * b, 4 * b + 4):
                transpose_tile(tt, 0)
            if b >= 1:
                proj_block(b - 1)
        proj_block(NBLK - 1)

        for ns in range(4):
            t.wait_ge(g("s_qs"), 8 * (ns + 1))
            t.wait_ge(g("s_nf"), 4 + ns)
            for c in range(8):
                ins = t.matmul(psL[0:1, 0:512], ones_b[:, :], ksq(c),
                               start=(c == 0), stop=(c == 7))
                if c == 7:
                    ins.then_inc(g("s_nm"), 1)

        t.wait_ge(g("s_mi"), 2)
        t.matmul(psX[:, 0:1], ones_f[:, :], nstats[0:1, 5:6],
                 start=True, stop=True).then_inc(g("s_mi"), 1)

        def s_part(mb, mc, cpair):
            par = mb % 2
            EG = 4 * mb + mc
            if mc == 0 and cpair == 0:
                t.wait_ge(g("s_kv"), 16 * (mb + 1))
            if EG == 0 and cpair == 0:
                t.wait_ge(g("s_ev"), 96)
            if cpair == 0 and EG >= 1:
                t.wait_ge(g("s_exp"), 2 * EG)
            for c in (2 * cpair, 2 * cpair + 1):
                lhs = kbuf(par, c)[:, 128 * mc:128 * (mc + 1)]
                for sqs in range(4):
                    ins = t.matmul(_sbanks[sqs], lhs,
                                   qfm[:, 2048 * c + 512 * sqs:2048 * c + 512 * (sqs + 1)],
                                   start=(c == 0), stop=(c == 7))
                    if c == 7 and sqs == 3:
                        ins.then_inc(g("s_smm"), 1)

        def pv_group(mb, qc, tail=False):
            idx = 16 * mb + qc
            pr = (psB[:, 0:512], psB[:, 512:1024]) if (tail and idx % 2 == 1) else \
                 (pvpair[0][:, :], pvpair[1][:, :])
            if qc == 0:
                t.wait_ge(g("s_vv"), 16 * (mb + 1))
                t.wait_ge(g("s_exp"), 8 * (mb + 1))
            if idx == 0:
                t.wait_ge(g("s_mi"), 4)
            if idx >= 1:
                t.wait_ge(g("s_la"), idx)
                t.wait_ge(g("s_add"), idx - 1 if (tail and idx > 16 * (NMB - 2)) else idx)
            for sub in range(4):
                lhs = PT(mb % 3, sub)[:, 128 * qc:128 * (qc + 1)]
                t.matmul(pr[0], lhs, vbuf(mb % 2, sub)[:, 0:512],
                         start=(sub == 0), stop=(sub == 3))
                t.matmul(psX[:, 0:1], lhs, ones_b[:, :],
                         start=(sub == 0), stop=(sub == 3))
                ins = t.matmul(pr[1], lhs, vbuf(mb % 2, sub)[:, 512:1024],
                               start=(sub == 0), stop=(sub == 3))
                if sub == 3:
                    ins.then_inc(g("s_pv"), 1)

        for mb in range(NMB):
            for u in range(16):
                if mb >= 2:
                    pv_group(mb - 2, u)
                s_part(mb, u // 4, u % 4)
        t.wait_ge(g("s_exp"), 8 * NMB)
        for mb in (NMB - 2, NMB - 1):
            for qc in range(16):
                pv_group(mb, qc, tail=True)

        def ffn_group(qc):
            t.wait_ge(g("s_fm"), 128 + 8 * (qc + 1))
            t.wait_ge(g("s_nb"), 32)
            if qc >= 2:
                t.wait_ge(g("s_ev"), 96 + 2 * (qc - 1))
            pair = (ps0, ps1) if qc % 2 == 0 else (psL, psX)
            for half in range(2):
                ps = pair[half]
                wa, wb_ = (wtR, wtIn) if half == 0 else (wtI, wtR)
                bias = sbr if half == 0 else sbi
                for c in range(4):
                    t.matmul(ps[:, :], hfm(c)[:, 128 * qc:128 * (qc + 1)],
                             wa[:, 512 * c:512 * (c + 1)], start=(c == 0), stop=False)
                for c in range(4):
                    t.matmul(ps[:, :], hfm(4 + c)[:, 128 * qc:128 * (qc + 1)],
                             wb_[:, 512 * c:512 * (c + 1)], start=False, stop=False)
                t.matmul(ps[:, :], ones_f[:, :], bias[:, :],
                         start=False, stop=True).then_inc(g("s_pq"), 1)

        transpose_tile(0, 1)
        transpose_tile(1, 1)
        for qc in range(NTILE - 2):
            transpose_tile(qc + 2, 1)
            ffn_group(qc)
        ffn_group(NTILE - 2)
        ffn_group(NTILE - 1)

    @blk.scalar
    def _(sc):
        _acnt = [0]
        def amark(ins):
            ins.then_inc(g("s_act"), 1)
            _acnt[0] += 1
        def ajoin():
            sc.wait_ge(g("s_act"), _acnt[0])
        def ln_stats(t_idx, ph3, src_ap):
            par = t_idx % 2
            off = 16 * ph3
            if ph3:
                sc.wait_ge(g("s_x2"), t_idx + 1)
            else:
                sc.wait_ge(g("s_x"), 16 * (t_idx + 1))
            sc.activation(out=trash(), in_=src_ap, func=AF.Square,
                          accum_out=st_col(par, 4)).then_inc(g("s_sq"), 1)
            sc.wait_ge(g("s_st"), off + t_idx + 1)
            sc.wait_ge(g("s_sq"), off + t_idx + 1)
            amark(sc.activation(out=st_col(par, 5), in_=st_col(par, 4), func=AF.Ln,
                                scale=1.0 / D, bias=st_col(par, 3)))
            ajoin()
            sc.activation(out=st_col(par, 6), in_=st_col(par, 5), func=AF.Exp,
                          scale=-0.5).then_inc(g("s_rs"), 1)

        def evac_block(b):
            gbase = 24 * b
            for ek in range(8):
                sc.wait_ge(g("s_pq"), gbase + ek + 1)
                kk = 8 * b + ek
                if kk >= 2:
                    sc.wait_ge(g("s_kst"), 16 * kk)
                if ek == 0 and b >= 1:
                    sc.wait_ge(g("s_nm"), b)
                ps = ps0 if ek % 2 == 0 else ps1
                sc.activation(out=ksq(ek), in_=ps[:, :], func=AF.Square).then_inc(g("s_ks"), 1)
                sc.activation(out=ktile(ek % 2), in_=ps[:, :], func=AF.Copy).then_inc(g("s_ev"), 1)
                sc.wait_ge(g("s_ev"), gbase + ek + 1)
                sc.dma_start(out=kslice[b].ap()[128 * ek:128 * (ek + 1), :],
                             in_=ktile(ek % 2)).then_inc(g("s_kst"), 16)
            for eq in range(8):
                sc.wait_ge(g("s_pq"), gbase + 8 + eq + 1)
                ps = ps0 if eq % 2 == 0 else ps1
                sc.activation(out=qfm[:, 2048 * eq + 512 * b:2048 * eq + 512 * (b + 1)],
                              in_=ps[:, :], func=AF.Copy, scale=QSCALE).then_inc(g("s_ev"), 1)
            for i in range(8):
                sc.wait_ge(g("s_pq"), gbase + 16 + i + 1)
                vv = 8 * b + i
                if vv >= 2:
                    sc.wait_ge(g("s_vst"), 16 * vv)
                nc4, h = i // 2, i % 2
                ps = ps0 if i % 2 == 0 else ps1
                sc.activation(out=vtile(i % 2), in_=ps[:, :], func=AF.Copy).then_inc(g("s_ev"), 1)
                sc.wait_ge(g("s_ev"), gbase + 16 + i + 1)
                sc.dma_start(out=vslice[b].ap()[128 * nc4:128 * (nc4 + 1),
                                                512 * h:512 * (h + 1)],
                             in_=vtile(i % 2)).then_inc(g("s_vst"), 16)

        for b in range(NBLK):
            for tt in range(4 * b, 4 * b + 4):
                ln_stats(tt, 0, xt_t(tt))
            if b >= 1:
                evac_block(b - 1)
        evac_block(NBLK - 1)

        for ns in range(4):
            sc.wait_ge(g("s_nm"), 4 + ns)
            for c in range(8):
                sc.activation(out=ksq(c), in_=qfm[:, 2048 * c + 512 * ns:2048 * c + 512 * (ns + 1)],
                              func=AF.Square).then_inc(g("s_qs"), 1)

        sc.wait_ge(g("s_mi"), 1)
        amark(sc.activation(out=nstats[0:1, 4:5], in_=nstats[0:1, 3:4], func=AF.Sqrt))
        ajoin()
        sc.activation(out=nstats[0:1, 5:6], in_=nstats[0:1, 4:5], func=AF.Copy,
                      scale=-1.0).then_inc(g("s_mi"), 1)

        ecount = 0
        for mb in range(NMB):
            u = mb % 3
            if mb >= 3:
                sc.wait_ge(g("s_pv"), 16 * (mb - 2))
            for mc in range(4):
                sc.wait_ge(g("s_smm"), ecount + 1)
                if ecount == 0:
                    sc.wait_ge(g("s_mi"), 4)
                sc.activation(out=PT(u, mc)[:, 0:1024], in_=psA[:, :], func=AF.Exp,
                              bias=negm_bc[:, :]).then_inc(g("s_exp"), 1)
                sc.activation(out=PT(u, mc)[:, 1024:2048], in_=psB[:, :], func=AF.Exp,
                              bias=negm_bc[:, :]).then_inc(g("s_exp"), 1)
                ecount += 1

        for tt in range(NTILE):
            ln_stats(tt, 1, oacc_t(tt))
        for qc in range(NTILE):
            sc.wait_ge(g("s_ev"), 96 + 2 * (qc + 1))
            sc.dma_start(out=out_h[128 * qc:128 * (qc + 1), :],
                         in_=oacc_t(qc)).then_inc(g("s_store"), 16)
        sc.wait_ge(g("s_store"), 16 * NTILE)

    @blk.vector
    def _(v):
        _dcnt = [0]
        def dmark(ins):
            ins.then_inc(g("s_dve"), 1)
            _dcnt[0] += 1
        def djoin():
            v.wait_ge(g("s_dve"), _dcnt[0])
        v.memset(ones_f[:, :], 1.0)
        v.memset(ones_b[:, :], 1.0)
        v.memset(seps[:, :], EPS)
        for w in range(2):
            for j in range(8):
                v.wait_ge(g("s_wb"), WB[f"mm{w}_{j}"])
                if (w, j) > (0, 1):
                    pj = 16 * (14 * w + j) if j >= 2 else 16 * 14
                    v.wait_ge(g("s_wbd"), pj)
                v.tensor_copy(out=a_tmp[j % 2], in_=psX[:, 0:256]).then_inc(g("s_wbv"), 1)
            for h in range(2):
                v.wait_ge(g("s_wbs"), WBs[f"read1_{w}_{h}"])
                if not (w == 0 and h == 0):
                    v.wait_ge(g("s_wbd"), WBd[f"write2_{w}_{h - 1}"] if h == 1
                              else WBd["write2_0_1"])
                v.tensor_copy(out=wbA2.rearrange("p (gg b e) -> p gg b e", gg=16, b=16),
                              in_=wbA.rearrange("p (b e gg) -> p gg b e", b=16, e=16)) \
                    .then_inc(g("s_wbv"), 1)
        v.wait_ge(g("s_wbd"), WBD_ALL)
        ins = None
        for q in range(4):
            ins = v.tensor_scalar_mul(out=wtIn[:, 512 * q:512 * (q + 1)],
                                      in0=wtI[:, 512 * q:512 * (q + 1)], scalar1=-1.0)
        ins.then_inc(g("s_wbv"), 1)

        def ln_vec(t_idx, ph3, src_ap):
            par = t_idx % 2
            off = 16 * ph3
            if not ph3:
                v.wait_ge(g("s_x"), 16 * (t_idx + 1))
            dmark(v.reduce_sum(st_col(par, 0), src_ap, axis=AX.X))
            djoin()
            dmark(v.tensor_scalar_mul(out=st_col(par, 1), in0=st_col(par, 0), scalar1=1.0 / D))
            djoin()
            dmark(v.scalar_tensor_tensor(out=st_col(par, 2), in0=st_col(par, 1), scalar=-1.0,
                                         in1=st_col(par, 1), op0=ALU.mult, op1=ALU.mult))
            djoin()
            v.tensor_scalar_add(out=st_col(par, 3), in0=st_col(par, 2),
                                scalar1=EPS).then_inc(g("s_st"), 1)
            v.wait_ge(g("s_rs"), off + t_idx + 1)
            dmark(v.scalar_tensor_tensor(out=st_col(par, 7), in0=st_col(par, 1), scalar=-1.0,
                                         in1=st_col(par, 6), op0=ALU.mult, op1=ALU.mult))
            if t_idx >= 2:
                v.wait_ge(g("s_tp"), 128 * ph3 + 8 * (t_idx - 1))
            djoin()
            v.tensor_scalar(out=xnf_t(t_idx), in0=src_ap,
                            scalar1=st_col(par, 6), scalar2=st_col(par, 7),
                            op0=ALU.mult, op1=ALU.add).then_inc(g("s_ln"), 1)

        def evac_tile(t_idx, ph3):
            for c in range(8):
                G = 128 * ph3 + 8 * t_idx + c
                v.wait_ge(g("s_tp"), G + 1)
                if (not ph3) and t_idx % 4 == 0 and c == 0 and t_idx // 4 >= 2:
                    v.wait_ge(g("s_pq"), 24 * (t_idx // 4 - 1))
                if ph3:
                    dst = regA[:, 2048 * c + 128 * t_idx:2048 * c + 128 * (t_idx + 1)]
                else:
                    par = (t_idx // 4) % 2
                    dst = regB[:, 4096 * par + 512 * c + 128 * (t_idx % 4):
                               4096 * par + 512 * c + 128 * (t_idx % 4) + 128]
                v.tensor_copy(out=dst, in_=pT[c % 2][:, 0:128]).then_inc(g("s_fm"), 1)

        def knorm_read(b):
            v.wait_ge(g("s_nm"), b + 1)
            if b == 0:
                v.reduce_max(nstats[0:1, 0:1], psL[0:1, 0:512], axis=AX.X).then_inc(g("s_nf"), 1)
            else:
                dmark(v.reduce_max(nstats[0:1, 2:3], psL[0:1, 0:512], axis=AX.X))
                djoin()
                v.tensor_max(out=nstats[0:1, 0:1], in0=nstats[0:1, 0:1],
                             in1=nstats[0:1, 2:3]).then_inc(g("s_nf"), 1)

        for tt in range(NTILE):
            if tt >= 9 and (tt - 9) % 4 == 0:
                knorm_read((tt - 9) // 4)
            if tt >= 1:
                evac_tile(tt - 1, 0)
            ln_vec(tt, 0, xt_t(tt))
        evac_tile(NTILE - 1, 0)
        knorm_read(NBLK - 2)
        knorm_read(NBLK - 1)
        djoin()
        dmark(v.memset(nrm_sb[:, :], 0.0))
        djoin()
        v.tensor_scalar_add(out=nrm_sb[:, :], in0=nrm_sb[:, :],
                            scalar1=nstats[0:1, 0:1]).then_inc(g("s_nmsb"), 1)
        for ns in range(4):
            v.wait_ge(g("s_nm"), 5 + ns)
            if ns == 0:
                v.reduce_max(nstats[0:1, 1:2], psL[0:1, 0:512], axis=AX.X).then_inc(g("s_nf"), 1)
            else:
                dmark(v.reduce_max(nstats[0:1, 2:3], psL[0:1, 0:512], axis=AX.X))
                djoin()
                v.tensor_max(out=nstats[0:1, 1:2], in0=nstats[0:1, 1:2],
                             in1=nstats[0:1, 2:3]).then_inc(g("s_nf"), 1)
        v.wait_ge(g("s_nb"), 48)
        v.wait_ge(g("s_nf"), 8)
        dmark(v.reduce_max(nstats[0:1, 2:3], nrm_sb[0:1, :], axis=AX.X))
        djoin()
        v.tensor_mul(out=nstats[0:1, 3:4], in0=nstats[0:1, 2:3],
                     in1=nstats[0:1, 1:2]).then_inc(g("s_mi"), 1)
        v.wait_ge(g("s_mi"), 3)
        v.tensor_copy(out=negm_bc[:, :], in_=psX[:, 0:1]).then_inc(g("s_mi"), 1)

        v.wait_ge(g("s_wbd"), WBD_ALL)
        for mb in range(NMB):
            for qc in range(16):
                idx = 16 * mb + qc
                tail = mb >= NMB - 2
                pva, pvb = (psB[:, 0:512], psB[:, 512:1024]) if (tail and idx % 2 == 1) else \
                           (pvpair[0][:, :], pvpair[1][:, :])
                v.wait_ge(g("s_pv"), idx + 1)
                if mb == 0:
                    v.tensor_copy(out=L_sb[:, qc:qc + 1], in_=psX[:, 0:1]).then_inc(g("s_la"), 1)
                    v.tensor_copy(out=oacc[:, 1024 * qc:1024 * qc + 512], in_=pva)
                    v.tensor_copy(out=oacc[:, 1024 * qc + 512:1024 * (qc + 1)],
                                  in_=pvb).then_inc(g("s_add"), 1)
                else:
                    v.tensor_add(out=L_sb[:, qc:qc + 1], in0=psX[:, 0:1],
                                 in1=L_sb[:, qc:qc + 1]).then_inc(g("s_la"), 1)
                    v.tensor_add(out=oacc[:, 1024 * qc:1024 * qc + 512],
                                 in0=pva, in1=oacc[:, 1024 * qc:1024 * qc + 512])
                    v.tensor_add(out=oacc[:, 1024 * qc + 512:1024 * (qc + 1)],
                                 in0=pvb,
                                 in1=oacc[:, 1024 * qc + 512:1024 * (qc + 1)]).then_inc(g("s_add"), 1)
        v.wait_ge(g("s_la"), NMB * 16)
        v.reciprocal(out=rl_sb[:, :], in_=L_sb[:, :]).then_inc(g("s_lrl"), 1)

        def ffn_evac(qc):
            v.wait_ge(g("s_pq"), 96 + 2 * (qc + 1))
            pair = (ps0, ps1) if qc % 2 == 0 else (psL, psX)
            v.tensor_add(out=oacc[:, 1024 * qc:1024 * qc + 512],
                         in0=pair[0][:, :], in1=oacc[:, 1024 * qc:1024 * qc + 512])
            v.tensor_add(out=oacc[:, 1024 * qc + 512:1024 * (qc + 1)],
                         in0=pair[1][:, :],
                         in1=oacc[:, 1024 * qc + 512:1024 * (qc + 1)]).then_inc(g("s_ev"), 2)

        for tt in range(NTILE):
            v.wait_ge(g("s_x"), 256 + 16 * (tt + 1))
            if tt == 0:
                v.wait_ge(g("s_lrl"), 1)
            v.scalar_tensor_tensor(out=oacc_t(tt), in0=oacc_t(tt),
                                   scalar=rl_sb[:, tt:tt + 1], in1=xt_t(tt),
                                   op0=ALU.mult, op1=ALU.add).then_inc(g("s_x2"), 1)
            if tt >= 1:
                evac_tile(tt - 1, 1)
            v.wait_ge(g("s_x2"), tt + 1)
            ln_vec(tt, 1, oacc_t(tt))
            if tt >= 3:
                ffn_evac(tt - 3)
        evac_tile(NTILE - 1, 1)
        ffn_evac(NTILE - 3)
        ffn_evac(NTILE - 2)
        ffn_evac(NTILE - 1)

    return nc, ctx


_cache = {}


def kernel(L0_attn_x, L0_attn_W_qkv, L0_ffn_G1_R, L0_ffn_G2_R, L0_ffn_G1_I,
           L0_ffn_G2_I, L0_ffn_bias_real, L0_ffn_bias_imag):
    x2d = np.ascontiguousarray(np.asarray(L0_attn_x, dtype=np.float32).reshape(NTOK, D))
    wqkv = np.ascontiguousarray(np.asarray(L0_attn_W_qkv, dtype=np.float32))
    g1r = np.ascontiguousarray(np.asarray(L0_ffn_G1_R, dtype=np.float32))
    g2r = np.ascontiguousarray(np.asarray(L0_ffn_G2_R, dtype=np.float32))
    g1i = np.ascontiguousarray(np.asarray(L0_ffn_G1_I, dtype=np.float32))
    g2i = np.ascontiguousarray(np.asarray(L0_ffn_G2_I, dtype=np.float32))
    br = np.ascontiguousarray(np.asarray(L0_ffn_bias_real, dtype=np.float32))
    bi = np.ascontiguousarray(np.asarray(L0_ffn_bias_imag, dtype=np.float32))

    if "nc" not in _cache:
        _cache["nc"] = build_nc()
    nc, _ctx = _cache["nc"]

    in_maps = [dict(x=x2d[NT * c:NT * (c + 1)], wqkv=wqkv, g1r=g1r, g2r=g2r,
                    g1i=g1i, g2i=g2i, br=br, bi=bi) for c in range(NCORE)]
    res = run_bass_kernel_spmd(nc, in_maps, core_ids=list(range(NCORE)))
    out = np.concatenate([res.results[c]["out"] for c in range(NCORE)], axis=0)
    return out.reshape(1, NTOK, D)


# version 12
# speedup vs baseline: 1.8154x; 1.0147x over previous
import numpy as np
from contextlib import ExitStack

import concourse.bass as bass
import concourse.mybir as mybir
from concourse.bass_utils import run_bass_kernel_spmd

F32 = mybir.dt.float32
BF16 = mybir.dt.bfloat16
AF = mybir.ActivationFunctionType
ALU = mybir.AluOpType
AX = mybir.AxisListType

NCORE = 8
NTOK = 16384
D = 1024
NT = NTOK // NCORE
NTILE = NT // 128
NBLK = NT // 512
MB = 512
NMB = NTOK // MB
NPAIR = NMB // 2
EPS = 1e-5
QSCALE = 1.0 / 32.0


def build_nc():
    nc = bass.Bass("TRN2", target_bir_lowering=False)

    x_h = nc.dram_tensor("x", [NT, D], F32, kind="ExternalInput").ap()
    wqkv_h = nc.dram_tensor("wqkv", [3 * D, D], F32, kind="ExternalInput").ap()
    g1r_h = nc.dram_tensor("g1r", [1, 32, 4, 32], F32, kind="ExternalInput").ap()
    g2r_h = nc.dram_tensor("g2r", [4, 16, 1, 16], F32, kind="ExternalInput").ap()
    g1i_h = nc.dram_tensor("g1i", [1, 32, 4, 32], F32, kind="ExternalInput").ap()
    g2i_h = nc.dram_tensor("g2i", [4, 16, 1, 16], F32, kind="ExternalInput").ap()
    br_h = nc.dram_tensor("br", [512], F32, kind="ExternalInput").ap()
    bi_h = nc.dram_tensor("bi", [512], F32, kind="ExternalInput").ap()
    out_h = nc.dram_tensor("out", [NT, D], F32, kind="ExternalOutput").ap()

    wqkv_bf = nc.dram_tensor("wqkv_bf", [3 * D, D], BF16, kind="Internal")
    kslice = [nc.dram_tensor(f"kslice{j}", [D, 512], BF16, kind="Internal") for j in range(4)]
    vslice = [nc.dram_tensor(f"vslice{j}", [512, D], BF16, kind="Internal") for j in range(4)]
    kfull = [nc.dram_tensor(f"kfull{j}", [NCORE * D, 512], BF16, kind="Internal") for j in range(4)]
    vfull = [nc.dram_tensor(f"vfull{j}", [NCORE * 512, D], BF16, kind="Internal") for j in range(4)]
    nrm_in = nc.dram_tensor("nrm_in", [1, 512], F32, kind="Internal")
    nrm_out = nc.dram_tensor("nrm_out", [1, 512], F32, kind="Internal")
    wb_l1 = [nc.dram_tensor(f"wb_l1_{w}", [1024, 256], F32, kind="Internal") for w in range(2)]
    wb_l2 = [nc.dram_tensor(f"wb_l2_{w}", [512, 512], F32, kind="Internal") for w in range(2)]
    eye_h = nc.inline_tensor(np.eye(128, dtype=np.float32), "eyec")

    ctx = ExitStack()

    regA = ctx.enter_context(nc.sbuf_tensor("regA", [128, 24576], BF16))
    regB = ctx.enter_context(nc.sbuf_tensor("regB", [128, 8192], BF16))
    regC = ctx.enter_context(nc.sbuf_tensor("regC", [128, 8192], BF16))
    wtR = ctx.enter_context(nc.sbuf_tensor("wtR", [128, 2048], BF16))
    wtI = ctx.enter_context(nc.sbuf_tensor("wtI", [128, 2048], BF16))
    wtIn = ctx.enter_context(nc.sbuf_tensor("wtIn", [128, 2048], BF16))
    qfm = ctx.enter_context(nc.sbuf_tensor("qfm", [128, 16384], BF16))
    oacc = ctx.enter_context(nc.sbuf_tensor("oacc", [128, 16384], F32))
    xt = ctx.enter_context(nc.sbuf_tensor("xt", [128, 2048], F32))
    xnf = ctx.enter_context(nc.sbuf_tensor("xnf", [128, 2048], BF16))
    eye_sb = ctx.enter_context(nc.sbuf_tensor("eye_sb", [128, 128], BF16))
    stats = ctx.enter_context(nc.sbuf_tensor("stats", [128, 16], F32))
    nstats = ctx.enter_context(nc.sbuf_tensor("nstats", [1, 16], F32))
    seps = ctx.enter_context(nc.sbuf_tensor("seps", [128, 1], F32))
    negm_bc = ctx.enter_context(nc.sbuf_tensor("negm_bc", [128, 1], F32))
    L_sb = ctx.enter_context(nc.sbuf_tensor("L_sb", [128, 16], F32))
    rl_sb = ctx.enter_context(nc.sbuf_tensor("rl_sb", [128, 16], F32))
    nrm_sb = ctx.enter_context(nc.sbuf_tensor("nrm_sb", [1, 512], F32))
    ones_f = ctx.enter_context(nc.sbuf_tensor("ones_f", [1, 128], F32))
    ones_b = ctx.enter_context(nc.sbuf_tensor("ones_b", [128, 1], BF16))
    sbr = ctx.enter_context(nc.sbuf_tensor("sbr", [1, 512], F32))
    sbi = ctx.enter_context(nc.sbuf_tensor("sbi", [1, 512], F32))
    g1cm = oacc[0:4, 0:1024]
    g2s = oacc[0:4, 1024:1280]
    a_tmp = [oacc[:, 1280:1536], oacc[:, 1536:1792]]
    wbA = oacc[0:32, 2048:6144]
    wbA2 = oacc[0:32, 6144:10240]
    assert nc.sbuf_bytes_remaining >= 0

    psA = ctx.enter_context(nc.psum_tensor("psA", [128, 1024], F32))
    psL = ctx.enter_context(nc.psum_tensor("psL", [128, 512], F32))
    psX = ctx.enter_context(nc.psum_tensor("psX", [128, 512], F32))
    ptctx = ExitStack()
    pT0 = ptctx.enter_context(nc.psum_tensor("pT0", [128, 512], F32))
    pT1 = ptctx.enter_context(nc.psum_tensor("pT1", [128, 512], F32))
    ptctx.close()
    psB = ctx.enter_context(nc.psum_tensor("psB", [128, 1024], F32))
    psV10 = ctx.enter_context(nc.psum_tensor("psV10", [128, 512], F32))
    psV11 = ctx.enter_context(nc.psum_tensor("psV11", [128, 512], F32))
    ps0 = psA[:, 0:512]
    ps1 = psA[:, 512:1024]
    _sbanks = [psA[:, 0:512], psA[:, 512:1024], psB[:, 0:512], psB[:, 512:1024]]
    pvpair = (psV10, psV11)
    pT = [pT0, pT1]

    semnames = ("s_dve s_act s_wb s_wbv s_wbs s_wbd s_wl2 s_wq s_wt s_x s_sq s_st s_rs s_ln s_tp s_fm s_pq s_ev "
                "s_ks s_qs s_nm s_nf s_nmsb s_store s_kst s_vst s_nst s_ccn s_cck s_ccv s_nb s_mi s_kv s_smm "
                "s_exp s_pv s_add s_lrl s_x2 s_la s_vv").split()
    sems = {n: ctx.enter_context(nc.semaphore(n)) for n in semnames}
    g = lambda n: sems[n]

    blk = ctx.enter_context(nc.Block())

    def wqkvT(c):
        return regA[:, 3072 * c:3072 * (c + 1)]
    def PT(u, mc):
        return regA[:, 8192 * u + 2048 * mc:8192 * u + 2048 * (mc + 1)]
    def hfm(c):
        return regA[:, 2048 * c:2048 * (c + 1)]
    def xnfm(par, c):
        return regB[:, 4096 * par + 512 * c:4096 * par + 512 * (c + 1)]
    def kbuf(par, c):
        return regB[:, 4096 * par + 512 * c:4096 * par + 512 * (c + 1)]
    def ktile(j):
        return regC[:, 512 * j:512 * (j + 1)]
    def vtile(j):
        return regC[:, 1024 + 512 * j:1024 + 512 * (j + 1)]
    def ksq(c):
        return regC[:, 2048 + 512 * c:2048 + 512 * (c + 1)]
    def trash():
        return regC[:, 6144:7168]
    def vbuf(par, mc):
        return regC[:, 4096 * par + 1024 * mc:4096 * par + 1024 * (mc + 1)]
    def xnf_t(t_idx):
        return xnf[:, 1024 * (t_idx % 2):1024 * (t_idx % 2 + 1)]
    def xt_t(t_idx):
        return xt[:, 1024 * (t_idx % 2):1024 * (t_idx % 2 + 1)]
    def oacc_t(t_idx):
        return oacc[:, 1024 * t_idx:1024 * (t_idx + 1)]
    def st_col(par, i):
        return stats[:, 8 * par + i:8 * par + i + 1]

    WB, WBv, WBs, WBd, WL2 = {}, {}, {}, {}, {}
    for w in range(2):
        for j in range(8):
            WB[f"mm{w}_{j}"] = 8 * w + j + 1
            WBv[f"evac{w}_{j}"] = 10 * w + j + 1
            WBd[f"store{w}_{j}"] = 16 * (8 * w + j + 1)
        for h in range(2):
            WBv[f"reorder{w}_{h}"] = 10 * w + 9 + h
            WBs[f"read1_{w}_{h}"] = 80 + 32 * w + 16 * h
            WL2[f"write2_{w}_{h}"] = 16 * (6 * w + h + 1)
        WBs[f"loads{w}"] = 32 + 32 * w
        for q in range(4):
            WL2[f"read2_{w}_{q}"] = 16 * (6 * w + 3 + q)
    WB_ALL = 16
    WBV_ALL = 20
    WBD_ALL = 16 * 16
    WL2_ALL = 16 * 12
    KV_STORES = 16 * NBLK
    ST_ALL_P1 = KV_STORES + 1
    ST_ALL = ST_ALL_P1 + NTILE

    @blk.sync
    def _(sync):
        s = sync
        _qlast = [None]
        def sdma(out, in_, sem, **kw):
            if _qlast[0] is not None:
                s.wait_ge(*_qlast[0])
            ins = s.dma_start(out=out, in_=in_, **kw)
            ins.then_inc(sem, 16)
            _cnt = _qcnt.get(sem.name if hasattr(sem, 'name') else id(sem), 0) + 16
            _qcnt[sem.name if hasattr(sem, 'name') else id(sem)] = _cnt
            _qlast[0] = (sem, _cnt)
            return ins
        _qcnt = {}
        def xload(t_):
            if t_ >= 2:
                s.wait_ge(g("s_ln"), t_ - 1)
                s.wait_ge(g("s_sq"), t_ - 1)
            sdma(xt_t(t_), x_h[128 * t_:128 * (t_ + 1), :], g("s_x"))

        xload(0)
        xload(1)
        for w in range(2):
            g1h, g2h = (g1r_h, g2r_h) if w == 0 else (g1i_h, g2i_h)
            if w == 1:
                s.wait_ge(g("s_wb"), WB["mm0_7"])
            sdma(g1cm.rearrange("c (b d) -> c b d", b=32),
                 g1h.rearrange("a b c d -> c (a b) d"), g("s_wbs"))
            sdma(g2s, g2h.rearrange("c e f g -> c (e f g)"), g("s_wbs"))
        for w in range(2):
            for h in range(2):
                if (w, h) != (0, 0):
                    prevr = WBv[f"reorder{w}_{h - 1}"] if h == 1 else WBv[f"reorder{w - 1}_1"]
                    s.wait_ge(g("s_wbv"), prevr)
                s.wait_ge(g("s_wbd"), WBd[f"store{w}_{4 * h + 3}"])
                sdma(wbA.rearrange("p (b eg) -> p b eg", b=16),
                     wb_l1[w].ap()[512 * h:512 * (h + 1), :]
                     .rearrange("(b d) eg -> d b eg", b=16), g("s_wbs"))
        s.wait_ge(g("s_wq"), 32)
        for c in range(8):
            sdma(wqkvT(c), wqkv_bf.ap()[:, 128 * c:128 * (c + 1)], g("s_wt"), transpose=True)
        for t_ in range(2, NTILE):
            xload(t_)
        sdma(sbr[:, :], br_h[None, :], g("s_nb"))
        sdma(sbi[:, :], bi_h[None, :], g("s_nb"))
        def kload(mb):
            par = mb % 2
            r, lb = mb // 4, mb % 4
            if mb < 4:
                s.wait_ge(g("s_cck"), lb + 1)
            if mb >= 2:
                s.wait_ge(g("s_smm"), 4 * (mb - 1))
            sdma(regB[:, 4096 * par:4096 * (par + 1)].rearrange("p (c m) -> p c m", c=8),
                 kfull[lb].ap()[1024 * r:1024 * (r + 1), :]
                 .rearrange("(c p) m -> p c m", p=128), g("s_kv"))
        s.wait_ge(g("s_pq"), 96)
        kload(0)
        kload(1)
        s.wait_ge(g("s_ccn"), 1)
        sdma(nrm_sb[:, :], nrm_out.ap()[:, :], g("s_nb"))
        for mb_ in range(2, NMB):
            kload(mb_)
        for t in range(NTILE):
            if t >= 2:
                s.wait_ge(g("s_x2"), t - 1)
            sdma(xt_t(t), x_h[128 * t:128 * (t + 1), :], g("s_x"))

    @blk.gpsimd
    def _(gp):
        _glast = [None]
        _gcnt = {}
        def gdma(out, in_, sem, **kw):
            if _glast[0] is not None:
                gp.wait_ge(*_glast[0])
            ins = gp.dma_start(out=out, in_=in_, **kw)
            ins.then_inc(sem, 16)
            k = sem.name if hasattr(sem, 'name') else id(sem)
            _gcnt[k] = _gcnt.get(k, 0) + 16
            _glast[0] = (sem, _gcnt[k])
            return ins
        gdma(eye_sb[:, :], eye_h.ap()[:, :], g("s_wq"))
        gdma(wqkv_bf.ap()[:, :], wqkv_h, g("s_wq"))
        for w in range(2):
            for h in range(2):
                gp.wait_ge(g("s_wbv"), WBv[f"reorder{w}_{h}"])
                gdma(wb_l2[w].ap().rearrange("(d gg) (b e) -> d gg b e", gg=16, b=32)[:, :, 16 * h:16 * (h + 1), :],
                     wbA2.rearrange("p (gg b e) -> p gg b e", gg=16, b=16), g("s_wl2"))
            wt = wtR if w == 0 else wtI
            for q in range(4):
                gdma(wt[:, 512 * q:512 * (q + 1)], wb_l2[w].ap()[128 * q:128 * (q + 1), :], g("s_wl2"))
        rg = [list(range(NCORE))]
        for b2 in range(3):
            gp.wait_ge(g("s_kst"), 16 * 8 * (b2 + 1))
            gp.collective_compute("AllGather", ALU.bypass, replica_groups=rg,
                                  ins=[kslice[b2].ap()[:, :]], outs=[kfull[b2].ap()[:, :]]).then_inc(g("s_cck"))
        gp.wait_ge(g("s_nmsb"), 1)
        gdma(nrm_in.ap()[:, :], nrm_sb[:, :], g("s_nst"))
        gp.wait_ge(g("s_nst"), 16)
        gp.collective_compute("AllReduce", ALU.max, replica_groups=rg,
                              ins=[nrm_in.ap()[:, :]], outs=[nrm_out.ap()[:, :]]).then_inc(g("s_ccn"))
        gp.wait_ge(g("s_kst"), 16 * 8 * 4)
        gp.collective_compute("AllGather", ALU.bypass, replica_groups=rg,
                              ins=[kslice[3].ap()[:, :]], outs=[kfull[3].ap()[:, :]]).then_inc(g("s_cck"))
        for b2 in range(4):
            gp.wait_ge(g("s_vst"), 16 * 8 * (b2 + 1))
            gp.collective_compute("AllGather", ALU.bypass, replica_groups=rg,
                                  ins=[vslice[b2].ap()[:, :]], outs=[vfull[b2].ap()[:, :]]).then_inc(g("s_ccv"))
        _glast[0] = (g("s_ccv"), 1)
        gp.wait_ge(g("s_nm"), 8)
        for mb_ in range(NMB):
            par_ = mb_ % 2
            r_, lb_ = mb_ // 4, mb_ % 4
            if mb_ < 4:
                gp.wait_ge(g("s_ccv"), lb_ + 1)
            if mb_ >= 2:
                gp.wait_ge(g("s_pv"), 16 * (mb_ - 1))
            gdma(regC[:, 4096 * par_:4096 * (par_ + 1)].rearrange("p (c m) -> p c m", c=4),
                 vfull[lb_].ap()[512 * r_:512 * (r_ + 1), :]
                 .rearrange("(c p) m -> p c m", p=128), g("s_vv"))


    @blk.tensor
    def _(t):
        for w in range(2):
            t.wait_ge(g("s_wbs"), WBs[f"loads{w}"])
            for j in range(8):
                if not (w == 0 and j == 0):
                    prev = WBv[f"evac{w}_{j - 1}"] if j > 0 else WBv["evac0_7"]
                    t.wait_ge(g("s_wbv"), prev)
                t.matmul(psX[:, 0:256], g1cm[:, 128 * j:128 * (j + 1)], g2s,
                         start=True, stop=True).then_inc(g("s_wb"), 1)

        t.wait_ge(g("s_wq"), 16)

        def transpose_tile(t_idx, ph3):
            base = 128 * ph3
            for c in range(8):
                G = base + 8 * t_idx + c
                if c == 0:
                    t.wait_ge(g("s_ln"), 16 * ph3 + t_idx + 1)
                if G >= 2:
                    t.wait_ge(g("s_fm"), G - 1)
                t.matmul(pT[c % 2][:, 0:128],
                         xnf_t(t_idx)[:, 128 * c:128 * (c + 1)],
                         eye_sb[:, :], start=True, stop=True).then_inc(g("s_tp"), 1)

        def proj_block(b):
            par = b % 2
            gbase = 24 * b
            if b == 0:
                t.wait_ge(g("s_wt"), 128)
            t.wait_ge(g("s_fm"), 32 * (b + 1))
            grp = 0
            for ek in range(8):
                gi = gbase + grp
                if gi >= 2:
                    t.wait_ge(g("s_ev"), gi - 1)
                ps = ps0 if grp % 2 == 0 else ps1
                for c in range(8):
                    ins = t.matmul(ps[:, :], wqkvT(c)[:, 1024 + 128 * ek:1024 + 128 * (ek + 1)],
                                   xnfm(par, c), start=(c == 0), stop=(c == 7))
                    if c == 7:
                        ins.then_inc(g("s_pq"), 1)
                grp += 1
            for ek in range(8):
                t.wait_ge(g("s_ks"), 8 * b + ek + 1)
                if ek == 0 and b >= 1:
                    t.wait_ge(g("s_nf"), b)
                ins = t.matmul(psL[0:1, 0:512], ones_b[:, :], ksq(ek),
                               start=(ek == 0), stop=(ek == 7))
                if ek == 7:
                    ins.then_inc(g("s_nm"), 1)
            for eq in range(8):
                gi = gbase + grp
                t.wait_ge(g("s_ev"), gi - 1)
                ps = ps0 if grp % 2 == 0 else ps1
                for c in range(8):
                    ins = t.matmul(ps[:, :], wqkvT(c)[:, 128 * eq:128 * (eq + 1)],
                                   xnfm(par, c), start=(c == 0), stop=(c == 7))
                    if c == 7:
                        ins.then_inc(g("s_pq"), 1)
                grp += 1
            for i in range(8):
                nc4, h = i // 2, i % 2
                gi = gbase + grp
                t.wait_ge(g("s_ev"), gi - 1)
                ps = ps0 if grp % 2 == 0 else ps1
                for c in range(8):
                    ins = t.matmul(ps[:, :], xnfm(par, c)[:, 128 * nc4:128 * (nc4 + 1)],
                                   wqkvT(c)[:, 2048 + 512 * h:2048 + 512 * (h + 1)],
                                   start=(c == 0), stop=(c == 7))
                    if c == 7:
                        ins.then_inc(g("s_pq"), 1)
                grp += 1

        for b in range(NBLK):
            for tt in range(4 * b, 4 * b + 4):
                transpose_tile(tt, 0)
            if b >= 1:
                proj_block(b - 1)
        proj_block(NBLK - 1)

        for ns in range(4):
            t.wait_ge(g("s_qs"), 8 * (ns + 1))
            t.wait_ge(g("s_nf"), 4 + ns)
            for c in range(8):
                ins = t.matmul(psL[0:1, 0:512], ones_b[:, :], ksq(c),
                               start=(c == 0), stop=(c == 7))
                if c == 7:
                    ins.then_inc(g("s_nm"), 1)

        t.wait_ge(g("s_mi"), 2)
        t.matmul(psX[:, 0:1], ones_f[:, :], nstats[0:1, 5:6],
                 start=True, stop=True).then_inc(g("s_mi"), 1)

        def s_part(mb, mc, cpair):
            par = mb % 2
            EG = 4 * mb + mc
            if mc == 0 and cpair == 0:
                t.wait_ge(g("s_kv"), 16 * (mb + 1))
            if EG == 0 and cpair == 0:
                t.wait_ge(g("s_ev"), 96)
            if cpair == 0 and EG >= 1:
                t.wait_ge(g("s_exp"), 2 * EG)
            for c in (2 * cpair, 2 * cpair + 1):
                lhs = kbuf(par, c)[:, 128 * mc:128 * (mc + 1)]
                for sqs in range(4):
                    ins = t.matmul(_sbanks[sqs], lhs,
                                   qfm[:, 2048 * c + 512 * sqs:2048 * c + 512 * (sqs + 1)],
                                   start=(c == 0), stop=(c == 7))
                    if c == 7 and sqs == 3:
                        ins.then_inc(g("s_smm"), 1)

        def pv_group(mb, qc, tail=False):
            idx = 16 * mb + qc
            pr = (psB[:, 0:512], psB[:, 512:1024]) if (tail and idx % 2 == 1) else \
                 (pvpair[0][:, :], pvpair[1][:, :])
            if qc == 0:
                t.wait_ge(g("s_vv"), 16 * (mb + 1))
                t.wait_ge(g("s_exp"), 8 * (mb + 1))
            if idx == 0:
                t.wait_ge(g("s_mi"), 4)
            if idx >= 1:
                t.wait_ge(g("s_la"), idx)
                t.wait_ge(g("s_add"), idx - 1 if (tail and idx > 16 * (NMB - 2)) else idx)
            for sub in range(4):
                lhs = PT(mb % 3, sub)[:, 128 * qc:128 * (qc + 1)]
                t.matmul(pr[0], lhs, vbuf(mb % 2, sub)[:, 0:512],
                         start=(sub == 0), stop=(sub == 3))
                t.matmul(psX[:, 0:1], lhs, ones_b[:, :],
                         start=(sub == 0), stop=(sub == 3))
                ins = t.matmul(pr[1], lhs, vbuf(mb % 2, sub)[:, 512:1024],
                               start=(sub == 0), stop=(sub == 3))
                if sub == 3:
                    ins.then_inc(g("s_pv"), 1)

        for mb in range(NMB):
            for u in range(16):
                if mb >= 2:
                    pv_group(mb - 2, u)
                s_part(mb, u // 4, u % 4)
        t.wait_ge(g("s_exp"), 8 * NMB)
        for mb in (NMB - 2, NMB - 1):
            for qc in range(16):
                pv_group(mb, qc, tail=True)

        def ffn_group(qc):
            t.wait_ge(g("s_fm"), 128 + 8 * (qc + 1))
            t.wait_ge(g("s_nb"), 32)
            if qc >= 2:
                t.wait_ge(g("s_ev"), 96 + 2 * (qc - 1))
            pair = (ps0, ps1) if qc % 2 == 0 else (psL, psX)
            for half in range(2):
                ps = pair[half]
                wa, wb_ = (wtR, wtIn) if half == 0 else (wtI, wtR)
                bias = sbr if half == 0 else sbi
                for c in range(4):
                    t.matmul(ps[:, :], hfm(c)[:, 128 * qc:128 * (qc + 1)],
                             wa[:, 512 * c:512 * (c + 1)], start=(c == 0), stop=False)
                for c in range(4):
                    t.matmul(ps[:, :], hfm(4 + c)[:, 128 * qc:128 * (qc + 1)],
                             wb_[:, 512 * c:512 * (c + 1)], start=False, stop=False)
                t.matmul(ps[:, :], ones_f[:, :], bias[:, :],
                         start=False, stop=True).then_inc(g("s_pq"), 1)

        transpose_tile(0, 1)
        transpose_tile(1, 1)
        for qc in range(NTILE - 2):
            transpose_tile(qc + 2, 1)
            ffn_group(qc)
        ffn_group(NTILE - 2)
        ffn_group(NTILE - 1)

    @blk.scalar
    def _(sc):
        _acnt = [0]
        def amark(ins):
            ins.then_inc(g("s_act"), 1)
            _acnt[0] += 1
        def ajoin():
            sc.wait_ge(g("s_act"), _acnt[0])
        for w in range(2):
            for j in range(8):
                sc.wait_ge(g("s_wbv"), WBv[f"evac{w}_{j}"])
                if 8 * w + j >= 1:
                    sc.wait_ge(g("s_wbd"), 16 * (8 * w + j))
                sc.dma_start(out=wb_l1[w].ap()[128 * j:128 * (j + 1), :],
                             in_=a_tmp[j % 2]).then_inc(g("s_wbd"), 16)
        def ln_stats(t_idx, ph3, src_ap):
            par = t_idx % 2
            off = 16 * ph3
            if ph3:
                sc.wait_ge(g("s_x2"), t_idx + 1)
            else:
                sc.wait_ge(g("s_x"), 16 * (t_idx + 1))
            sc.activation(out=trash(), in_=src_ap, func=AF.Square,
                          accum_out=st_col(par, 4)).then_inc(g("s_sq"), 1)
            sc.wait_ge(g("s_st"), off + t_idx + 1)
            sc.wait_ge(g("s_sq"), off + t_idx + 1)
            amark(sc.activation(out=st_col(par, 5), in_=st_col(par, 4), func=AF.Ln,
                                scale=1.0 / D, bias=st_col(par, 3)))
            ajoin()
            sc.activation(out=st_col(par, 6), in_=st_col(par, 5), func=AF.Exp,
                          scale=-0.5).then_inc(g("s_rs"), 1)

        def evac_block(b):
            gbase = 24 * b
            for ek in range(8):
                sc.wait_ge(g("s_pq"), gbase + ek + 1)
                kk = 8 * b + ek
                if kk >= 2:
                    sc.wait_ge(g("s_kst"), 16 * kk)
                if ek == 0 and b >= 1:
                    sc.wait_ge(g("s_nm"), b)
                ps = ps0 if ek % 2 == 0 else ps1
                sc.activation(out=ksq(ek), in_=ps[:, :], func=AF.Square).then_inc(g("s_ks"), 1)
                sc.activation(out=ktile(ek % 2), in_=ps[:, :], func=AF.Copy).then_inc(g("s_ev"), 1)
                sc.wait_ge(g("s_ev"), gbase + ek + 1)
                sc.dma_start(out=kslice[b].ap()[128 * ek:128 * (ek + 1), :],
                             in_=ktile(ek % 2)).then_inc(g("s_kst"), 16)
            for eq in range(8):
                sc.wait_ge(g("s_pq"), gbase + 8 + eq + 1)
                ps = ps0 if eq % 2 == 0 else ps1
                sc.activation(out=qfm[:, 2048 * eq + 512 * b:2048 * eq + 512 * (b + 1)],
                              in_=ps[:, :], func=AF.Copy, scale=QSCALE).then_inc(g("s_ev"), 1)
            for i in range(8):
                sc.wait_ge(g("s_pq"), gbase + 16 + i + 1)
                vv = 8 * b + i
                if vv >= 2:
                    sc.wait_ge(g("s_vst"), 16 * vv)
                nc4, h = i // 2, i % 2
                ps = ps0 if i % 2 == 0 else ps1
                sc.activation(out=vtile(i % 2), in_=ps[:, :], func=AF.Copy).then_inc(g("s_ev"), 1)
                sc.wait_ge(g("s_ev"), gbase + 16 + i + 1)
                sc.dma_start(out=vslice[b].ap()[128 * nc4:128 * (nc4 + 1),
                                                512 * h:512 * (h + 1)],
                             in_=vtile(i % 2)).then_inc(g("s_vst"), 16)

        for b in range(NBLK):
            for tt in range(4 * b, 4 * b + 4):
                ln_stats(tt, 0, xt_t(tt))
            if b >= 1:
                evac_block(b - 1)
        evac_block(NBLK - 1)

        for ns in range(4):
            sc.wait_ge(g("s_nm"), 4 + ns)
            for c in range(8):
                sc.activation(out=ksq(c), in_=qfm[:, 2048 * c + 512 * ns:2048 * c + 512 * (ns + 1)],
                              func=AF.Square).then_inc(g("s_qs"), 1)

        sc.wait_ge(g("s_mi"), 1)
        amark(sc.activation(out=nstats[0:1, 4:5], in_=nstats[0:1, 3:4], func=AF.Sqrt))
        ajoin()
        sc.activation(out=nstats[0:1, 5:6], in_=nstats[0:1, 4:5], func=AF.Copy,
                      scale=-1.0).then_inc(g("s_mi"), 1)

        ecount = 0
        for mb in range(NMB):
            u = mb % 3
            if mb >= 3:
                sc.wait_ge(g("s_pv"), 16 * (mb - 2))
            for mc in range(4):
                sc.wait_ge(g("s_smm"), ecount + 1)
                if ecount == 0:
                    sc.wait_ge(g("s_mi"), 4)
                sc.activation(out=PT(u, mc)[:, 0:1024], in_=psA[:, :], func=AF.Exp,
                              bias=negm_bc[:, :]).then_inc(g("s_exp"), 1)
                sc.activation(out=PT(u, mc)[:, 1024:2048], in_=psB[:, :], func=AF.Exp,
                              bias=negm_bc[:, :]).then_inc(g("s_exp"), 1)
                ecount += 1

        for tt in range(NTILE):
            ln_stats(tt, 1, oacc_t(tt))
        for qc in range(NTILE):
            sc.wait_ge(g("s_ev"), 96 + 2 * (qc + 1))
            sc.dma_start(out=out_h[128 * qc:128 * (qc + 1), :],
                         in_=oacc_t(qc)).then_inc(g("s_store"), 16)
        sc.wait_ge(g("s_store"), 16 * NTILE)

    @blk.vector
    def _(v):
        _dcnt = [0]
        def dmark(ins):
            ins.then_inc(g("s_dve"), 1)
            _dcnt[0] += 1
        def djoin():
            v.wait_ge(g("s_dve"), _dcnt[0])
        v.memset(ones_f[:, :], 1.0)
        v.memset(ones_b[:, :], 1.0)
        v.memset(seps[:, :], EPS)
        for w in range(2):
            for j in range(8):
                v.wait_ge(g("s_wb"), WB[f"mm{w}_{j}"])
                if (w, j) > (0, 1):
                    v.wait_ge(g("s_wbd"), 16 * (8 * w + j - 1))
                v.tensor_copy(out=a_tmp[j % 2], in_=psX[:, 0:256]).then_inc(g("s_wbv"), 1)
            for h in range(2):
                v.wait_ge(g("s_wbs"), WBs[f"read1_{w}_{h}"])
                if not (w == 0 and h == 0):
                    v.wait_ge(g("s_wl2"), WL2[f"write2_{w}_{h - 1}"] if h == 1
                              else WL2["write2_0_1"])
                v.tensor_copy(out=wbA2.rearrange("p (gg b e) -> p gg b e", gg=16, b=16),
                              in_=wbA.rearrange("p (b e gg) -> p gg b e", b=16, e=16)) \
                    .then_inc(g("s_wbv"), 1)
        v.wait_ge(g("s_wl2"), WL2_ALL)
        ins = None
        for q in range(4):
            ins = v.tensor_scalar_mul(out=wtIn[:, 512 * q:512 * (q + 1)],
                                      in0=wtI[:, 512 * q:512 * (q + 1)], scalar1=-1.0)
        ins.then_inc(g("s_wbv"), 1)

        def ln_vec(t_idx, ph3, src_ap):
            par = t_idx % 2
            off = 16 * ph3
            if not ph3:
                v.wait_ge(g("s_x"), 16 * (t_idx + 1))
            dmark(v.reduce_sum(st_col(par, 0), src_ap, axis=AX.X))
            djoin()
            dmark(v.tensor_scalar_mul(out=st_col(par, 1), in0=st_col(par, 0), scalar1=1.0 / D))
            djoin()
            dmark(v.scalar_tensor_tensor(out=st_col(par, 2), in0=st_col(par, 1), scalar=-1.0,
                                         in1=st_col(par, 1), op0=ALU.mult, op1=ALU.mult))
            djoin()
            v.tensor_scalar_add(out=st_col(par, 3), in0=st_col(par, 2),
                                scalar1=EPS).then_inc(g("s_st"), 1)
            v.wait_ge(g("s_rs"), off + t_idx + 1)
            dmark(v.scalar_tensor_tensor(out=st_col(par, 7), in0=st_col(par, 1), scalar=-1.0,
                                         in1=st_col(par, 6), op0=ALU.mult, op1=ALU.mult))
            if t_idx >= 2:
                v.wait_ge(g("s_tp"), 128 * ph3 + 8 * (t_idx - 1))
            djoin()
            v.tensor_scalar(out=xnf_t(t_idx), in0=src_ap,
                            scalar1=st_col(par, 6), scalar2=st_col(par, 7),
                            op0=ALU.mult, op1=ALU.add).then_inc(g("s_ln"), 1)

        def evac_tile(t_idx, ph3):
            for c in range(8):
                G = 128 * ph3 + 8 * t_idx + c
                v.wait_ge(g("s_tp"), G + 1)
                if (not ph3) and t_idx % 4 == 0 and c == 0 and t_idx // 4 >= 2:
                    v.wait_ge(g("s_pq"), 24 * (t_idx // 4 - 1))
                if ph3:
                    dst = regA[:, 2048 * c + 128 * t_idx:2048 * c + 128 * (t_idx + 1)]
                else:
                    par = (t_idx // 4) % 2
                    dst = regB[:, 4096 * par + 512 * c + 128 * (t_idx % 4):
                               4096 * par + 512 * c + 128 * (t_idx % 4) + 128]
                v.tensor_copy(out=dst, in_=pT[c % 2][:, 0:128]).then_inc(g("s_fm"), 1)

        def knorm_read(b):
            v.wait_ge(g("s_nm"), b + 1)
            if b == 0:
                v.reduce_max(nstats[0:1, 0:1], psL[0:1, 0:512], axis=AX.X).then_inc(g("s_nf"), 1)
            else:
                dmark(v.reduce_max(nstats[0:1, 2:3], psL[0:1, 0:512], axis=AX.X))
                djoin()
                v.tensor_max(out=nstats[0:1, 0:1], in0=nstats[0:1, 0:1],
                             in1=nstats[0:1, 2:3]).then_inc(g("s_nf"), 1)

        for tt in range(NTILE):
            if tt >= 9 and (tt - 9) % 4 == 0:
                knorm_read((tt - 9) // 4)
            if tt >= 1:
                evac_tile(tt - 1, 0)
            ln_vec(tt, 0, xt_t(tt))
        evac_tile(NTILE - 1, 0)
        knorm_read(NBLK - 2)
        knorm_read(NBLK - 1)
        djoin()
        dmark(v.memset(nrm_sb[:, :], 0.0))
        djoin()
        v.tensor_scalar_add(out=nrm_sb[:, :], in0=nrm_sb[:, :],
                            scalar1=nstats[0:1, 0:1]).then_inc(g("s_nmsb"), 1)
        for ns in range(4):
            v.wait_ge(g("s_nm"), 5 + ns)
            if ns == 0:
                v.reduce_max(nstats[0:1, 1:2], psL[0:1, 0:512], axis=AX.X).then_inc(g("s_nf"), 1)
            else:
                dmark(v.reduce_max(nstats[0:1, 2:3], psL[0:1, 0:512], axis=AX.X))
                djoin()
                v.tensor_max(out=nstats[0:1, 1:2], in0=nstats[0:1, 1:2],
                             in1=nstats[0:1, 2:3]).then_inc(g("s_nf"), 1)
        v.wait_ge(g("s_nb"), 48)
        v.wait_ge(g("s_nf"), 8)
        dmark(v.reduce_max(nstats[0:1, 2:3], nrm_sb[0:1, :], axis=AX.X))
        djoin()
        v.tensor_mul(out=nstats[0:1, 3:4], in0=nstats[0:1, 2:3],
                     in1=nstats[0:1, 1:2]).then_inc(g("s_mi"), 1)
        v.wait_ge(g("s_mi"), 3)
        v.tensor_copy(out=negm_bc[:, :], in_=psX[:, 0:1]).then_inc(g("s_mi"), 1)

        v.wait_ge(g("s_wbd"), WBD_ALL)
        v.wait_ge(g("s_wl2"), WL2_ALL)
        for mb in range(NMB):
            for qc in range(16):
                idx = 16 * mb + qc
                tail = mb >= NMB - 2
                pva, pvb = (psB[:, 0:512], psB[:, 512:1024]) if (tail and idx % 2 == 1) else \
                           (pvpair[0][:, :], pvpair[1][:, :])
                v.wait_ge(g("s_pv"), idx + 1)
                if mb == 0:
                    v.tensor_copy(out=L_sb[:, qc:qc + 1], in_=psX[:, 0:1]).then_inc(g("s_la"), 1)
                    v.tensor_copy(out=oacc[:, 1024 * qc:1024 * qc + 512], in_=pva)
                    v.tensor_copy(out=oacc[:, 1024 * qc + 512:1024 * (qc + 1)],
                                  in_=pvb).then_inc(g("s_add"), 1)
                else:
                    v.tensor_add(out=L_sb[:, qc:qc + 1], in0=psX[:, 0:1],
                                 in1=L_sb[:, qc:qc + 1]).then_inc(g("s_la"), 1)
                    v.tensor_add(out=oacc[:, 1024 * qc:1024 * qc + 512],
                                 in0=pva, in1=oacc[:, 1024 * qc:1024 * qc + 512])
                    v.tensor_add(out=oacc[:, 1024 * qc + 512:1024 * (qc + 1)],
                                 in0=pvb,
                                 in1=oacc[:, 1024 * qc + 512:1024 * (qc + 1)]).then_inc(g("s_add"), 1)
        v.wait_ge(g("s_la"), NMB * 16)
        v.reciprocal(out=rl_sb[:, :], in_=L_sb[:, :]).then_inc(g("s_lrl"), 1)

        def ffn_evac(qc):
            v.wait_ge(g("s_pq"), 96 + 2 * (qc + 1))
            pair = (ps0, ps1) if qc % 2 == 0 else (psL, psX)
            v.tensor_add(out=oacc[:, 1024 * qc:1024 * qc + 512],
                         in0=pair[0][:, :], in1=oacc[:, 1024 * qc:1024 * qc + 512])
            v.tensor_add(out=oacc[:, 1024 * qc + 512:1024 * (qc + 1)],
                         in0=pair[1][:, :],
                         in1=oacc[:, 1024 * qc + 512:1024 * (qc + 1)]).then_inc(g("s_ev"), 2)

        for tt in range(NTILE):
            v.wait_ge(g("s_x"), 256 + 16 * (tt + 1))
            if tt == 0:
                v.wait_ge(g("s_lrl"), 1)
            v.scalar_tensor_tensor(out=oacc_t(tt), in0=oacc_t(tt),
                                   scalar=rl_sb[:, tt:tt + 1], in1=xt_t(tt),
                                   op0=ALU.mult, op1=ALU.add).then_inc(g("s_x2"), 1)
            if tt >= 1:
                evac_tile(tt - 1, 1)
            v.wait_ge(g("s_x2"), tt + 1)
            ln_vec(tt, 1, oacc_t(tt))
            if tt >= 3:
                ffn_evac(tt - 3)
        evac_tile(NTILE - 1, 1)
        ffn_evac(NTILE - 3)
        ffn_evac(NTILE - 2)
        ffn_evac(NTILE - 1)

    return nc, ctx


_cache = {}


def kernel(L0_attn_x, L0_attn_W_qkv, L0_ffn_G1_R, L0_ffn_G2_R, L0_ffn_G1_I,
           L0_ffn_G2_I, L0_ffn_bias_real, L0_ffn_bias_imag):
    x2d = np.ascontiguousarray(np.asarray(L0_attn_x, dtype=np.float32).reshape(NTOK, D))
    wqkv = np.ascontiguousarray(np.asarray(L0_attn_W_qkv, dtype=np.float32))
    g1r = np.ascontiguousarray(np.asarray(L0_ffn_G1_R, dtype=np.float32))
    g2r = np.ascontiguousarray(np.asarray(L0_ffn_G2_R, dtype=np.float32))
    g1i = np.ascontiguousarray(np.asarray(L0_ffn_G1_I, dtype=np.float32))
    g2i = np.ascontiguousarray(np.asarray(L0_ffn_G2_I, dtype=np.float32))
    br = np.ascontiguousarray(np.asarray(L0_ffn_bias_real, dtype=np.float32))
    bi = np.ascontiguousarray(np.asarray(L0_ffn_bias_imag, dtype=np.float32))

    if "nc" not in _cache:
        _cache["nc"] = build_nc()
    nc, _ctx = _cache["nc"]

    in_maps = [dict(x=x2d[NT * c:NT * (c + 1)], wqkv=wqkv, g1r=g1r, g2r=g2r,
                    g1i=g1i, g2i=g2i, br=br, bi=bi) for c in range(NCORE)]
    res = run_bass_kernel_spmd(nc, in_maps, core_ids=list(range(NCORE)))
    out = np.concatenate([res.results[c]["out"] for c in range(NCORE)], axis=0)
    return out.reshape(1, NTOK, D)


# version 13
# speedup vs baseline: 1.8467x; 1.0173x over previous
import numpy as np
from contextlib import ExitStack

import concourse.bass as bass
import concourse.mybir as mybir
from concourse.bass_utils import run_bass_kernel_spmd

F32 = mybir.dt.float32
BF16 = mybir.dt.bfloat16
AF = mybir.ActivationFunctionType
ALU = mybir.AluOpType
AX = mybir.AxisListType

NCORE = 8
NTOK = 16384
D = 1024
NT = NTOK // NCORE
NTILE = NT // 128
NBLK = NT // 512
MB = 512
NMB = NTOK // MB
NPAIR = NMB // 2
EPS = 1e-5
QSCALE = 1.0 / 32.0


def build_nc():
    nc = bass.Bass("TRN2", target_bir_lowering=False)

    x_h = nc.dram_tensor("x", [NT, D], F32, kind="ExternalInput").ap()
    wqkv_h = nc.dram_tensor("wqkv", [3 * D, D], F32, kind="ExternalInput").ap()
    g1r_h = nc.dram_tensor("g1r", [1, 32, 4, 32], F32, kind="ExternalInput").ap()
    g2r_h = nc.dram_tensor("g2r", [4, 16, 1, 16], F32, kind="ExternalInput").ap()
    g1i_h = nc.dram_tensor("g1i", [1, 32, 4, 32], F32, kind="ExternalInput").ap()
    g2i_h = nc.dram_tensor("g2i", [4, 16, 1, 16], F32, kind="ExternalInput").ap()
    br_h = nc.dram_tensor("br", [512], F32, kind="ExternalInput").ap()
    bi_h = nc.dram_tensor("bi", [512], F32, kind="ExternalInput").ap()
    out_h = nc.dram_tensor("out", [NT, D], F32, kind="ExternalOutput").ap()

    wqkv_bf = nc.dram_tensor("wqkv_bf", [3 * D, D], BF16, kind="Internal")
    kslice = [nc.dram_tensor(f"kslice{j}", [D, 512], BF16, kind="Internal") for j in range(4)]
    vslice = [nc.dram_tensor(f"vslice{j}", [512, D], BF16, kind="Internal") for j in range(4)]
    kfull = [nc.dram_tensor(f"kfull{j}", [NCORE * D, 512], BF16, kind="Internal",
                            addr_space="Shared") for j in range(4)]
    vfull = [nc.dram_tensor(f"vfull{j}", [NCORE * 512, D], BF16, kind="Internal",
                            addr_space="Shared") for j in range(4)]
    nrm_in = nc.dram_tensor("nrm_in", [1, 512], F32, kind="Internal")
    nrm_out = nc.dram_tensor("nrm_out", [1, 512], F32, kind="Internal")
    wb_l1 = [nc.dram_tensor(f"wb_l1_{w}", [1024, 256], F32, kind="Internal") for w in range(2)]
    wb_l2 = [nc.dram_tensor(f"wb_l2_{w}", [512, 512], F32, kind="Internal") for w in range(2)]
    eye_h = nc.inline_tensor(np.eye(128, dtype=np.float32), "eyec")

    ctx = ExitStack()

    regA = ctx.enter_context(nc.sbuf_tensor("regA", [128, 24576], BF16))
    regB = ctx.enter_context(nc.sbuf_tensor("regB", [128, 8192], BF16))
    regC = ctx.enter_context(nc.sbuf_tensor("regC", [128, 8192], BF16))
    wtR = ctx.enter_context(nc.sbuf_tensor("wtR", [128, 2048], BF16))
    wtI = ctx.enter_context(nc.sbuf_tensor("wtI", [128, 2048], BF16))
    wtIn = ctx.enter_context(nc.sbuf_tensor("wtIn", [128, 2048], BF16))
    qfm = ctx.enter_context(nc.sbuf_tensor("qfm", [128, 16384], BF16))
    oacc = ctx.enter_context(nc.sbuf_tensor("oacc", [128, 16384], F32))
    xt = ctx.enter_context(nc.sbuf_tensor("xt", [128, 2048], F32))
    xnf = ctx.enter_context(nc.sbuf_tensor("xnf", [128, 2048], BF16))
    eye_sb = ctx.enter_context(nc.sbuf_tensor("eye_sb", [128, 128], BF16))
    stats = ctx.enter_context(nc.sbuf_tensor("stats", [128, 16], F32))
    nstats = ctx.enter_context(nc.sbuf_tensor("nstats", [1, 16], F32))
    seps = ctx.enter_context(nc.sbuf_tensor("seps", [128, 1], F32))
    negm_bc = ctx.enter_context(nc.sbuf_tensor("negm_bc", [128, 1], F32))
    L_sb = ctx.enter_context(nc.sbuf_tensor("L_sb", [128, 16], F32))
    rl_sb = ctx.enter_context(nc.sbuf_tensor("rl_sb", [128, 16], F32))
    nrm_sb = ctx.enter_context(nc.sbuf_tensor("nrm_sb", [1, 512], F32))
    ones_f = ctx.enter_context(nc.sbuf_tensor("ones_f", [1, 128], F32))
    ones_b = ctx.enter_context(nc.sbuf_tensor("ones_b", [128, 1], BF16))
    sbr = ctx.enter_context(nc.sbuf_tensor("sbr", [1, 512], F32))
    sbi = ctx.enter_context(nc.sbuf_tensor("sbi", [1, 512], F32))
    g1cm = oacc[0:4, 0:1024]
    g2s = oacc[0:4, 1024:1280]
    a_tmp = [oacc[:, 1280:1536], oacc[:, 1536:1792]]
    wbA = oacc[0:32, 2048:6144]
    wbA2 = oacc[0:32, 6144:10240]
    assert nc.sbuf_bytes_remaining >= 0

    psA = ctx.enter_context(nc.psum_tensor("psA", [128, 1024], F32))
    psL = ctx.enter_context(nc.psum_tensor("psL", [128, 512], F32))
    psX = ctx.enter_context(nc.psum_tensor("psX", [128, 512], F32))
    ptctx = ExitStack()
    pT0 = ptctx.enter_context(nc.psum_tensor("pT0", [128, 512], F32))
    pT1 = ptctx.enter_context(nc.psum_tensor("pT1", [128, 512], F32))
    ptctx.close()
    psB = ctx.enter_context(nc.psum_tensor("psB", [128, 1024], F32))
    psV10 = ctx.enter_context(nc.psum_tensor("psV10", [128, 512], F32))
    psV11 = ctx.enter_context(nc.psum_tensor("psV11", [128, 512], F32))
    ps0 = psA[:, 0:512]
    ps1 = psA[:, 512:1024]
    _sbanks = [psA[:, 0:512], psA[:, 512:1024], psB[:, 0:512], psB[:, 512:1024]]
    pvpair = (psV10, psV11)
    pT = [pT0, pT1]

    semnames = ("s_dve s_act s_wb s_wbv s_wbs s_wbd s_wl2 s_wq s_wt s_x s_sq s_st s_rs s_ln s_tp s_fm s_pq s_ev "
                "s_ks s_qs s_nm s_nf s_nmsb s_store s_kst s_vst s_nst s_ccn s_cck s_ccv s_nb s_mi s_kv s_smm "
                "s_exp s_pv s_add s_lrl s_x2 s_la s_vv").split()
    sems = {n: ctx.enter_context(nc.semaphore(n)) for n in semnames}
    g = lambda n: sems[n]

    blk = ctx.enter_context(nc.Block())

    def wqkvT(c):
        return regA[:, 3072 * c:3072 * (c + 1)]
    def PT(u, mc):
        return regA[:, 8192 * u + 2048 * mc:8192 * u + 2048 * (mc + 1)]
    def hfm(c):
        return regA[:, 2048 * c:2048 * (c + 1)]
    def xnfm(par, c):
        return regB[:, 4096 * par + 512 * c:4096 * par + 512 * (c + 1)]
    def kbuf(par, c):
        return regB[:, 4096 * par + 512 * c:4096 * par + 512 * (c + 1)]
    def ktile(j):
        return regC[:, 512 * j:512 * (j + 1)]
    def vtile(j):
        return regC[:, 1024 + 512 * j:1024 + 512 * (j + 1)]
    def ksq(c):
        return regC[:, 2048 + 512 * c:2048 + 512 * (c + 1)]
    def trash():
        return regC[:, 6144:7168]
    def vbuf(par, mc):
        return regC[:, 4096 * par + 1024 * mc:4096 * par + 1024 * (mc + 1)]
    def xnf_t(t_idx):
        return xnf[:, 1024 * (t_idx % 2):1024 * (t_idx % 2 + 1)]
    def xt_t(t_idx):
        return xt[:, 1024 * (t_idx % 2):1024 * (t_idx % 2 + 1)]
    def oacc_t(t_idx):
        return oacc[:, 1024 * t_idx:1024 * (t_idx + 1)]
    def st_col(par, i):
        return stats[:, 8 * par + i:8 * par + i + 1]

    WB, WBv, WBs, WBd, WL2 = {}, {}, {}, {}, {}
    for w in range(2):
        for j in range(8):
            WB[f"mm{w}_{j}"] = 8 * w + j + 1
            WBv[f"evac{w}_{j}"] = 10 * w + j + 1
            WBd[f"store{w}_{j}"] = 16 * (8 * w + j + 1)
        for h in range(2):
            WBv[f"reorder{w}_{h}"] = 10 * w + 9 + h
            WBs[f"read1_{w}_{h}"] = 80 + 32 * w + 16 * h
            WL2[f"write2_{w}_{h}"] = 16 * (6 * w + h + 1)
        WBs[f"loads{w}"] = 32 + 32 * w
        for q in range(4):
            WL2[f"read2_{w}_{q}"] = 16 * (6 * w + 3 + q)
    WB_ALL = 16
    WBV_ALL = 20
    WBD_ALL = 16 * 16
    WL2_ALL = 16 * 12
    KV_STORES = 16 * NBLK
    ST_ALL_P1 = KV_STORES + 1
    ST_ALL = ST_ALL_P1 + NTILE

    @blk.sync
    def _(sync):
        s = sync
        _qlast = [None]
        def sdma(out, in_, sem, **kw):
            if _qlast[0] is not None:
                s.wait_ge(*_qlast[0])
            ins = s.dma_start(out=out, in_=in_, **kw)
            ins.then_inc(sem, 16)
            _cnt = _qcnt.get(sem.name if hasattr(sem, 'name') else id(sem), 0) + 16
            _qcnt[sem.name if hasattr(sem, 'name') else id(sem)] = _cnt
            _qlast[0] = (sem, _cnt)
            return ins
        _qcnt = {}
        def xload(t_):
            if t_ >= 2:
                s.wait_ge(g("s_ln"), t_ - 1)
                s.wait_ge(g("s_sq"), t_ - 1)
            sdma(xt_t(t_), x_h[128 * t_:128 * (t_ + 1), :], g("s_x"))

        xload(0)
        xload(1)
        for w in range(2):
            g1h, g2h = (g1r_h, g2r_h) if w == 0 else (g1i_h, g2i_h)
            if w == 1:
                s.wait_ge(g("s_wb"), WB["mm0_7"])
            sdma(g1cm.rearrange("c (b d) -> c b d", b=32),
                 g1h.rearrange("a b c d -> c (a b) d"), g("s_wbs"))
            sdma(g2s, g2h.rearrange("c e f g -> c (e f g)"), g("s_wbs"))
        for w in range(2):
            for h in range(2):
                if (w, h) != (0, 0):
                    prevr = WBv[f"reorder{w}_{h - 1}"] if h == 1 else WBv[f"reorder{w - 1}_1"]
                    s.wait_ge(g("s_wbv"), prevr)
                s.wait_ge(g("s_wbd"), WBd[f"store{w}_{4 * h + 3}"])
                sdma(wbA.rearrange("p (b eg) -> p b eg", b=16),
                     wb_l1[w].ap()[512 * h:512 * (h + 1), :]
                     .rearrange("(b d) eg -> d b eg", b=16), g("s_wbs"))
        s.wait_ge(g("s_wq"), 32)
        for c in range(8):
            sdma(wqkvT(c), wqkv_bf.ap()[:, 128 * c:128 * (c + 1)], g("s_wt"), transpose=True)
        for t_ in range(2, NTILE):
            xload(t_)
        sdma(sbr[:, :], br_h[None, :], g("s_nb"))
        sdma(sbi[:, :], bi_h[None, :], g("s_nb"))
        def kload(mb):
            par = mb % 2
            r, lb = mb // 4, mb % 4
            if mb < 4:
                s.wait_ge(g("s_cck"), lb + 1)
            if mb >= 2:
                s.wait_ge(g("s_smm"), 4 * (mb - 1))
            sdma(regB[:, 4096 * par:4096 * (par + 1)].rearrange("p (c m) -> p c m", c=8),
                 kfull[lb].ap()[1024 * r:1024 * (r + 1), :]
                 .rearrange("(c p) m -> p c m", p=128), g("s_kv"))
        s.wait_ge(g("s_pq"), 96)
        kload(0)
        kload(1)
        s.wait_ge(g("s_ccn"), 1)
        sdma(nrm_sb[:, :], nrm_out.ap()[:, :], g("s_nb"))
        for mb_ in range(2, NMB):
            kload(mb_)
        for t in range(NTILE):
            if t >= 2:
                s.wait_ge(g("s_x2"), t - 1)
            sdma(xt_t(t), x_h[128 * t:128 * (t + 1), :], g("s_x"))

    @blk.gpsimd
    def _(gp):
        _glast = [None]
        _gcnt = {}
        def gdma(out, in_, sem, **kw):
            if _glast[0] is not None:
                gp.wait_ge(*_glast[0])
            ins = gp.dma_start(out=out, in_=in_, **kw)
            ins.then_inc(sem, 16)
            k = sem.name if hasattr(sem, 'name') else id(sem)
            _gcnt[k] = _gcnt.get(k, 0) + 16
            _glast[0] = (sem, _gcnt[k])
            return ins
        gdma(eye_sb[:, :], eye_h.ap()[:, :], g("s_wq"))
        gdma(wqkv_bf.ap()[:, :], wqkv_h, g("s_wq"))
        for w in range(2):
            for h in range(2):
                gp.wait_ge(g("s_wbv"), WBv[f"reorder{w}_{h}"])
                gdma(wb_l2[w].ap().rearrange("(d gg) (b e) -> d gg b e", gg=16, b=32)[:, :, 16 * h:16 * (h + 1), :],
                     wbA2.rearrange("p (gg b e) -> p gg b e", gg=16, b=16), g("s_wl2"))
            wt = wtR if w == 0 else wtI
            for q in range(4):
                gdma(wt[:, 512 * q:512 * (q + 1)], wb_l2[w].ap()[128 * q:128 * (q + 1), :], g("s_wl2"))
        rg = [list(range(NCORE))]
        for b2 in range(3):
            gp.wait_ge(g("s_kst"), 16 * 8 * (b2 + 1))
            gp.collective_compute("AllGather", ALU.bypass, replica_groups=rg,
                                  ins=[kslice[b2].ap()[:, :]], outs=[kfull[b2].ap()[:, :]]).then_inc(g("s_cck"))
        gp.wait_ge(g("s_nmsb"), 1)
        gdma(nrm_in.ap()[:, :], nrm_sb[:, :], g("s_nst"))
        gp.wait_ge(g("s_nst"), 16)
        gp.collective_compute("AllReduce", ALU.max, replica_groups=rg,
                              ins=[nrm_in.ap()[:, :]], outs=[nrm_out.ap()[:, :]]).then_inc(g("s_ccn"))
        gp.wait_ge(g("s_kst"), 16 * 8 * 4)
        gp.collective_compute("AllGather", ALU.bypass, replica_groups=rg,
                              ins=[kslice[3].ap()[:, :]], outs=[kfull[3].ap()[:, :]]).then_inc(g("s_cck"))
        for b2 in range(4):
            gp.wait_ge(g("s_vst"), 16 * 8 * (b2 + 1))
            gp.collective_compute("AllGather", ALU.bypass, replica_groups=rg,
                                  ins=[vslice[b2].ap()[:, :]], outs=[vfull[b2].ap()[:, :]]).then_inc(g("s_ccv"))
        _glast[0] = (g("s_ccv"), 1)
        gp.wait_ge(g("s_nm"), 8)
        for mb_ in range(NMB):
            par_ = mb_ % 2
            r_, lb_ = mb_ // 4, mb_ % 4
            if mb_ < 4:
                gp.wait_ge(g("s_ccv"), lb_ + 1)
            if mb_ >= 2:
                gp.wait_ge(g("s_pv"), 16 * (mb_ - 1))
            gdma(regC[:, 4096 * par_:4096 * (par_ + 1)].rearrange("p (c m) -> p c m", c=4),
                 vfull[lb_].ap()[512 * r_:512 * (r_ + 1), :]
                 .rearrange("(c p) m -> p c m", p=128), g("s_vv"))


    @blk.tensor
    def _(t):
        for w in range(2):
            t.wait_ge(g("s_wbs"), WBs[f"loads{w}"])
            for j in range(8):
                if not (w == 0 and j == 0):
                    prev = WBv[f"evac{w}_{j - 1}"] if j > 0 else WBv["evac0_7"]
                    t.wait_ge(g("s_wbv"), prev)
                t.matmul(psX[:, 0:256], g1cm[:, 128 * j:128 * (j + 1)], g2s,
                         start=True, stop=True).then_inc(g("s_wb"), 1)

        t.wait_ge(g("s_wq"), 16)

        def transpose_tile(t_idx, ph3):
            base = 128 * ph3
            for c in range(8):
                G = base + 8 * t_idx + c
                if c == 0:
                    t.wait_ge(g("s_ln"), 16 * ph3 + t_idx + 1)
                if G >= 2:
                    t.wait_ge(g("s_fm"), G - 1)
                t.matmul(pT[c % 2][:, 0:128],
                         xnf_t(t_idx)[:, 128 * c:128 * (c + 1)],
                         eye_sb[:, :], start=True, stop=True).then_inc(g("s_tp"), 1)

        def proj_block(b):
            par = b % 2
            gbase = 24 * b
            if b == 0:
                t.wait_ge(g("s_wt"), 128)
            t.wait_ge(g("s_fm"), 32 * (b + 1))
            grp = 0
            for ek in range(8):
                gi = gbase + grp
                if gi >= 2:
                    t.wait_ge(g("s_ev"), gi - 1)
                ps = ps0 if grp % 2 == 0 else ps1
                for c in range(8):
                    ins = t.matmul(ps[:, :], wqkvT(c)[:, 1024 + 128 * ek:1024 + 128 * (ek + 1)],
                                   xnfm(par, c), start=(c == 0), stop=(c == 7))
                    if c == 7:
                        ins.then_inc(g("s_pq"), 1)
                grp += 1
            for ek in range(8):
                t.wait_ge(g("s_ks"), 8 * b + ek + 1)
                if ek == 0 and b >= 1:
                    t.wait_ge(g("s_nf"), b)
                ins = t.matmul(psL[0:1, 0:512], ones_b[:, :], ksq(ek),
                               start=(ek == 0), stop=(ek == 7))
                if ek == 7:
                    ins.then_inc(g("s_nm"), 1)
            for eq in range(8):
                gi = gbase + grp
                t.wait_ge(g("s_ev"), gi - 1)
                ps = ps0 if grp % 2 == 0 else ps1
                for c in range(8):
                    ins = t.matmul(ps[:, :], wqkvT(c)[:, 128 * eq:128 * (eq + 1)],
                                   xnfm(par, c), start=(c == 0), stop=(c == 7))
                    if c == 7:
                        ins.then_inc(g("s_pq"), 1)
                grp += 1
            for i in range(8):
                nc4, h = i // 2, i % 2
                gi = gbase + grp
                t.wait_ge(g("s_ev"), gi - 1)
                ps = ps0 if grp % 2 == 0 else ps1
                for c in range(8):
                    ins = t.matmul(ps[:, :], xnfm(par, c)[:, 128 * nc4:128 * (nc4 + 1)],
                                   wqkvT(c)[:, 2048 + 512 * h:2048 + 512 * (h + 1)],
                                   start=(c == 0), stop=(c == 7))
                    if c == 7:
                        ins.then_inc(g("s_pq"), 1)
                grp += 1

        for b in range(NBLK):
            for tt in range(4 * b, 4 * b + 4):
                transpose_tile(tt, 0)
            if b >= 1:
                proj_block(b - 1)
        proj_block(NBLK - 1)

        for ns in range(4):
            t.wait_ge(g("s_qs"), 8 * (ns + 1))
            t.wait_ge(g("s_nf"), 4 + ns)
            for c in range(8):
                ins = t.matmul(psL[0:1, 0:512], ones_b[:, :], ksq(c),
                               start=(c == 0), stop=(c == 7))
                if c == 7:
                    ins.then_inc(g("s_nm"), 1)

        t.wait_ge(g("s_mi"), 2)
        t.matmul(psX[:, 0:1], ones_f[:, :], nstats[0:1, 5:6],
                 start=True, stop=True).then_inc(g("s_mi"), 1)

        def s_part(mb, mc, cpair):
            par = mb % 2
            EG = 4 * mb + mc
            if mc == 0 and cpair == 0:
                t.wait_ge(g("s_kv"), 16 * (mb + 1))
            if EG == 0 and cpair == 0:
                t.wait_ge(g("s_ev"), 96)
            if cpair == 0 and EG >= 1:
                t.wait_ge(g("s_exp"), 2 * EG)
            for c in (2 * cpair, 2 * cpair + 1):
                lhs = kbuf(par, c)[:, 128 * mc:128 * (mc + 1)]
                for sqs in range(4):
                    ins = t.matmul(_sbanks[sqs], lhs,
                                   qfm[:, 2048 * c + 512 * sqs:2048 * c + 512 * (sqs + 1)],
                                   start=(c == 0), stop=(c == 7))
                    if c == 7 and sqs == 3:
                        ins.then_inc(g("s_smm"), 1)

        def pv_group(mb, qc, tail=False):
            idx = 16 * mb + qc
            pr = (psB[:, 0:512], psB[:, 512:1024]) if (tail and idx % 2 == 1) else \
                 (pvpair[0][:, :], pvpair[1][:, :])
            if qc == 0:
                t.wait_ge(g("s_vv"), 16 * (mb + 1))
                t.wait_ge(g("s_exp"), 8 * (mb + 1))
            if idx == 0:
                t.wait_ge(g("s_mi"), 4)
            if idx >= 1:
                t.wait_ge(g("s_la"), idx)
                t.wait_ge(g("s_add"), idx - 1 if (tail and idx > 16 * (NMB - 2)) else idx)
            for sub in range(4):
                lhs = PT(mb % 3, sub)[:, 128 * qc:128 * (qc + 1)]
                t.matmul(pr[0], lhs, vbuf(mb % 2, sub)[:, 0:512],
                         start=(sub == 0), stop=(sub == 3))
                t.matmul(psX[:, 0:1], lhs, ones_b[:, :],
                         start=(sub == 0), stop=(sub == 3))
                ins = t.matmul(pr[1], lhs, vbuf(mb % 2, sub)[:, 512:1024],
                               start=(sub == 0), stop=(sub == 3))
                if sub == 3:
                    ins.then_inc(g("s_pv"), 1)

        for mb in range(NMB):
            for u in range(16):
                if mb >= 2:
                    pv_group(mb - 2, u)
                s_part(mb, u // 4, u % 4)
        t.wait_ge(g("s_exp"), 8 * NMB)
        for mb in (NMB - 2, NMB - 1):
            for qc in range(16):
                pv_group(mb, qc, tail=True)

        def ffn_group(qc):
            t.wait_ge(g("s_fm"), 128 + 8 * (qc + 1))
            t.wait_ge(g("s_nb"), 32)
            if qc >= 2:
                t.wait_ge(g("s_ev"), 96 + 2 * (qc - 1))
            pair = (ps0, ps1) if qc % 2 == 0 else (psL, psX)
            for half in range(2):
                ps = pair[half]
                wa, wb_ = (wtR, wtIn) if half == 0 else (wtI, wtR)
                bias = sbr if half == 0 else sbi
                for c in range(4):
                    t.matmul(ps[:, :], hfm(c)[:, 128 * qc:128 * (qc + 1)],
                             wa[:, 512 * c:512 * (c + 1)], start=(c == 0), stop=False)
                for c in range(4):
                    t.matmul(ps[:, :], hfm(4 + c)[:, 128 * qc:128 * (qc + 1)],
                             wb_[:, 512 * c:512 * (c + 1)], start=False, stop=False)
                t.matmul(ps[:, :], ones_f[:, :], bias[:, :],
                         start=False, stop=True).then_inc(g("s_pq"), 1)

        transpose_tile(0, 1)
        transpose_tile(1, 1)
        for qc in range(NTILE - 2):
            transpose_tile(qc + 2, 1)
            ffn_group(qc)
        ffn_group(NTILE - 2)
        ffn_group(NTILE - 1)

    @blk.scalar
    def _(sc):
        _acnt = [0]
        def amark(ins):
            ins.then_inc(g("s_act"), 1)
            _acnt[0] += 1
        def ajoin():
            sc.wait_ge(g("s_act"), _acnt[0])
        for w in range(2):
            for j in range(8):
                sc.wait_ge(g("s_wbv"), WBv[f"evac{w}_{j}"])
                if 8 * w + j >= 1:
                    sc.wait_ge(g("s_wbd"), 16 * (8 * w + j))
                sc.dma_start(out=wb_l1[w].ap()[128 * j:128 * (j + 1), :],
                             in_=a_tmp[j % 2]).then_inc(g("s_wbd"), 16)
        def ln_stats(t_idx, ph3, src_ap):
            par = t_idx % 2
            off = 16 * ph3
            if ph3:
                sc.wait_ge(g("s_x2"), t_idx + 1)
            else:
                sc.wait_ge(g("s_x"), 16 * (t_idx + 1))
            sc.activation(out=trash(), in_=src_ap, func=AF.Square,
                          accum_out=st_col(par, 4)).then_inc(g("s_sq"), 1)
            sc.wait_ge(g("s_st"), off + t_idx + 1)
            sc.wait_ge(g("s_sq"), off + t_idx + 1)
            amark(sc.activation(out=st_col(par, 5), in_=st_col(par, 4), func=AF.Ln,
                                scale=1.0 / D, bias=st_col(par, 3)))
            ajoin()
            sc.activation(out=st_col(par, 6), in_=st_col(par, 5), func=AF.Exp,
                          scale=-0.5).then_inc(g("s_rs"), 1)

        def evac_block(b):
            gbase = 24 * b
            for ek in range(8):
                sc.wait_ge(g("s_pq"), gbase + ek + 1)
                kk = 8 * b + ek
                if kk >= 2:
                    sc.wait_ge(g("s_kst"), 16 * kk)
                if ek == 0 and b >= 1:
                    sc.wait_ge(g("s_nm"), b)
                ps = ps0 if ek % 2 == 0 else ps1
                sc.activation(out=ksq(ek), in_=ps[:, :], func=AF.Square).then_inc(g("s_ks"), 1)
                sc.activation(out=ktile(ek % 2), in_=ps[:, :], func=AF.Copy).then_inc(g("s_ev"), 1)
                sc.wait_ge(g("s_ev"), gbase + ek + 1)
                sc.dma_start(out=kslice[b].ap()[128 * ek:128 * (ek + 1), :],
                             in_=ktile(ek % 2)).then_inc(g("s_kst"), 16)
            for eq in range(8):
                sc.wait_ge(g("s_pq"), gbase + 8 + eq + 1)
                ps = ps0 if eq % 2 == 0 else ps1
                sc.activation(out=qfm[:, 2048 * eq + 512 * b:2048 * eq + 512 * (b + 1)],
                              in_=ps[:, :], func=AF.Copy, scale=QSCALE).then_inc(g("s_ev"), 1)
            for i in range(8):
                sc.wait_ge(g("s_pq"), gbase + 16 + i + 1)
                vv = 8 * b + i
                if vv >= 2:
                    sc.wait_ge(g("s_vst"), 16 * vv)
                nc4, h = i // 2, i % 2
                ps = ps0 if i % 2 == 0 else ps1
                sc.activation(out=vtile(i % 2), in_=ps[:, :], func=AF.Copy).then_inc(g("s_ev"), 1)
                sc.wait_ge(g("s_ev"), gbase + 16 + i + 1)
                sc.dma_start(out=vslice[b].ap()[128 * nc4:128 * (nc4 + 1),
                                                512 * h:512 * (h + 1)],
                             in_=vtile(i % 2)).then_inc(g("s_vst"), 16)

        for b in range(NBLK):
            for tt in range(4 * b, 4 * b + 4):
                ln_stats(tt, 0, xt_t(tt))
            if b >= 1:
                evac_block(b - 1)
        evac_block(NBLK - 1)

        for ns in range(4):
            sc.wait_ge(g("s_nm"), 4 + ns)
            for c in range(8):
                sc.activation(out=ksq(c), in_=qfm[:, 2048 * c + 512 * ns:2048 * c + 512 * (ns + 1)],
                              func=AF.Square).then_inc(g("s_qs"), 1)

        sc.wait_ge(g("s_mi"), 1)
        amark(sc.activation(out=nstats[0:1, 4:5], in_=nstats[0:1, 3:4], func=AF.Sqrt))
        ajoin()
        sc.activation(out=nstats[0:1, 5:6], in_=nstats[0:1, 4:5], func=AF.Copy,
                      scale=-1.0).then_inc(g("s_mi"), 1)

        ecount = 0
        for mb in range(NMB):
            u = mb % 3
            if mb >= 3:
                sc.wait_ge(g("s_pv"), 16 * (mb - 2))
            for mc in range(4):
                sc.wait_ge(g("s_smm"), ecount + 1)
                if ecount == 0:
                    sc.wait_ge(g("s_mi"), 4)
                sc.activation(out=PT(u, mc)[:, 0:1024], in_=psA[:, :], func=AF.Exp,
                              bias=negm_bc[:, :]).then_inc(g("s_exp"), 1)
                sc.activation(out=PT(u, mc)[:, 1024:2048], in_=psB[:, :], func=AF.Exp,
                              bias=negm_bc[:, :]).then_inc(g("s_exp"), 1)
                ecount += 1

        for tt in range(NTILE):
            ln_stats(tt, 1, oacc_t(tt))
        for qc in range(NTILE):
            sc.wait_ge(g("s_ev"), 96 + 2 * (qc + 1))
            sc.dma_start(out=out_h[128 * qc:128 * (qc + 1), :],
                         in_=oacc_t(qc)).then_inc(g("s_store"), 16)
        sc.wait_ge(g("s_store"), 16 * NTILE)

    @blk.vector
    def _(v):
        _dcnt = [0]
        def dmark(ins):
            ins.then_inc(g("s_dve"), 1)
            _dcnt[0] += 1
        def djoin():
            v.wait_ge(g("s_dve"), _dcnt[0])
        v.memset(ones_f[:, :], 1.0)
        v.memset(ones_b[:, :], 1.0)
        v.memset(seps[:, :], EPS)
        for w in range(2):
            for j in range(8):
                v.wait_ge(g("s_wb"), WB[f"mm{w}_{j}"])
                if (w, j) > (0, 1):
                    v.wait_ge(g("s_wbd"), 16 * (8 * w + j - 1))
                v.tensor_copy(out=a_tmp[j % 2], in_=psX[:, 0:256]).then_inc(g("s_wbv"), 1)
            for h in range(2):
                v.wait_ge(g("s_wbs"), WBs[f"read1_{w}_{h}"])
                if not (w == 0 and h == 0):
                    v.wait_ge(g("s_wl2"), WL2[f"write2_{w}_{h - 1}"] if h == 1
                              else WL2["write2_0_1"])
                v.tensor_copy(out=wbA2.rearrange("p (gg b e) -> p gg b e", gg=16, b=16),
                              in_=wbA.rearrange("p (b e gg) -> p gg b e", b=16, e=16)) \
                    .then_inc(g("s_wbv"), 1)
        v.wait_ge(g("s_wl2"), WL2_ALL)
        ins = None
        for q in range(4):
            ins = v.tensor_scalar_mul(out=wtIn[:, 512 * q:512 * (q + 1)],
                                      in0=wtI[:, 512 * q:512 * (q + 1)], scalar1=-1.0)
        ins.then_inc(g("s_wbv"), 1)

        def ln_vec(t_idx, ph3, src_ap):
            par = t_idx % 2
            off = 16 * ph3
            if not ph3:
                v.wait_ge(g("s_x"), 16 * (t_idx + 1))
            dmark(v.reduce_sum(st_col(par, 0), src_ap, axis=AX.X))
            djoin()
            dmark(v.tensor_scalar_mul(out=st_col(par, 1), in0=st_col(par, 0), scalar1=1.0 / D))
            djoin()
            dmark(v.scalar_tensor_tensor(out=st_col(par, 2), in0=st_col(par, 1), scalar=-1.0,
                                         in1=st_col(par, 1), op0=ALU.mult, op1=ALU.mult))
            djoin()
            v.tensor_scalar_add(out=st_col(par, 3), in0=st_col(par, 2),
                                scalar1=EPS).then_inc(g("s_st"), 1)
            v.wait_ge(g("s_rs"), off + t_idx + 1)
            dmark(v.scalar_tensor_tensor(out=st_col(par, 7), in0=st_col(par, 1), scalar=-1.0,
                                         in1=st_col(par, 6), op0=ALU.mult, op1=ALU.mult))
            if t_idx >= 2:
                v.wait_ge(g("s_tp"), 128 * ph3 + 8 * (t_idx - 1))
            djoin()
            v.tensor_scalar(out=xnf_t(t_idx), in0=src_ap,
                            scalar1=st_col(par, 6), scalar2=st_col(par, 7),
                            op0=ALU.mult, op1=ALU.add).then_inc(g("s_ln"), 1)

        def evac_tile(t_idx, ph3):
            for c in range(8):
                G = 128 * ph3 + 8 * t_idx + c
                v.wait_ge(g("s_tp"), G + 1)
                if (not ph3) and t_idx % 4 == 0 and c == 0 and t_idx // 4 >= 2:
                    v.wait_ge(g("s_pq"), 24 * (t_idx // 4 - 1))
                if ph3:
                    dst = regA[:, 2048 * c + 128 * t_idx:2048 * c + 128 * (t_idx + 1)]
                else:
                    par = (t_idx // 4) % 2
                    dst = regB[:, 4096 * par + 512 * c + 128 * (t_idx % 4):
                               4096 * par + 512 * c + 128 * (t_idx % 4) + 128]
                v.tensor_copy(out=dst, in_=pT[c % 2][:, 0:128]).then_inc(g("s_fm"), 1)

        def knorm_read(b):
            v.wait_ge(g("s_nm"), b + 1)
            if b == 0:
                v.reduce_max(nstats[0:1, 0:1], psL[0:1, 0:512], axis=AX.X).then_inc(g("s_nf"), 1)
            else:
                dmark(v.reduce_max(nstats[0:1, 2:3], psL[0:1, 0:512], axis=AX.X))
                djoin()
                v.tensor_max(out=nstats[0:1, 0:1], in0=nstats[0:1, 0:1],
                             in1=nstats[0:1, 2:3]).then_inc(g("s_nf"), 1)

        for tt in range(NTILE):
            if tt >= 9 and (tt - 9) % 4 == 0:
                knorm_read((tt - 9) // 4)
            if tt >= 1:
                evac_tile(tt - 1, 0)
            ln_vec(tt, 0, xt_t(tt))
        evac_tile(NTILE - 1, 0)
        knorm_read(NBLK - 2)
        knorm_read(NBLK - 1)
        djoin()
        dmark(v.memset(nrm_sb[:, :], 0.0))
        djoin()
        v.tensor_scalar_add(out=nrm_sb[:, :], in0=nrm_sb[:, :],
                            scalar1=nstats[0:1, 0:1]).then_inc(g("s_nmsb"), 1)
        for ns in range(4):
            v.wait_ge(g("s_nm"), 5 + ns)
            if ns == 0:
                v.reduce_max(nstats[0:1, 1:2], psL[0:1, 0:512], axis=AX.X).then_inc(g("s_nf"), 1)
            else:
                dmark(v.reduce_max(nstats[0:1, 2:3], psL[0:1, 0:512], axis=AX.X))
                djoin()
                v.tensor_max(out=nstats[0:1, 1:2], in0=nstats[0:1, 1:2],
                             in1=nstats[0:1, 2:3]).then_inc(g("s_nf"), 1)
        v.wait_ge(g("s_nb"), 48)
        v.wait_ge(g("s_nf"), 8)
        dmark(v.reduce_max(nstats[0:1, 2:3], nrm_sb[0:1, :], axis=AX.X))
        djoin()
        v.tensor_mul(out=nstats[0:1, 3:4], in0=nstats[0:1, 2:3],
                     in1=nstats[0:1, 1:2]).then_inc(g("s_mi"), 1)
        v.wait_ge(g("s_mi"), 3)
        v.tensor_copy(out=negm_bc[:, :], in_=psX[:, 0:1]).then_inc(g("s_mi"), 1)

        v.wait_ge(g("s_wbd"), WBD_ALL)
        v.wait_ge(g("s_wl2"), WL2_ALL)
        for mb in range(NMB):
            for qc in range(16):
                idx = 16 * mb + qc
                tail = mb >= NMB - 2
                pva, pvb = (psB[:, 0:512], psB[:, 512:1024]) if (tail and idx % 2 == 1) else \
                           (pvpair[0][:, :], pvpair[1][:, :])
                v.wait_ge(g("s_pv"), idx + 1)
                if mb == 0:
                    v.tensor_copy(out=L_sb[:, qc:qc + 1], in_=psX[:, 0:1]).then_inc(g("s_la"), 1)
                    v.tensor_copy(out=oacc[:, 1024 * qc:1024 * qc + 512], in_=pva)
                    v.tensor_copy(out=oacc[:, 1024 * qc + 512:1024 * (qc + 1)],
                                  in_=pvb).then_inc(g("s_add"), 1)
                else:
                    v.tensor_add(out=L_sb[:, qc:qc + 1], in0=psX[:, 0:1],
                                 in1=L_sb[:, qc:qc + 1]).then_inc(g("s_la"), 1)
                    v.tensor_add(out=oacc[:, 1024 * qc:1024 * qc + 512],
                                 in0=pva, in1=oacc[:, 1024 * qc:1024 * qc + 512])
                    v.tensor_add(out=oacc[:, 1024 * qc + 512:1024 * (qc + 1)],
                                 in0=pvb,
                                 in1=oacc[:, 1024 * qc + 512:1024 * (qc + 1)]).then_inc(g("s_add"), 1)
        v.wait_ge(g("s_la"), NMB * 16)
        v.reciprocal(out=rl_sb[:, :], in_=L_sb[:, :]).then_inc(g("s_lrl"), 1)

        def ffn_evac(qc):
            v.wait_ge(g("s_pq"), 96 + 2 * (qc + 1))
            pair = (ps0, ps1) if qc % 2 == 0 else (psL, psX)
            v.tensor_add(out=oacc[:, 1024 * qc:1024 * qc + 512],
                         in0=pair[0][:, :], in1=oacc[:, 1024 * qc:1024 * qc + 512])
            v.tensor_add(out=oacc[:, 1024 * qc + 512:1024 * (qc + 1)],
                         in0=pair[1][:, :],
                         in1=oacc[:, 1024 * qc + 512:1024 * (qc + 1)]).then_inc(g("s_ev"), 2)

        for tt in range(NTILE):
            v.wait_ge(g("s_x"), 256 + 16 * (tt + 1))
            if tt == 0:
                v.wait_ge(g("s_lrl"), 1)
            v.scalar_tensor_tensor(out=oacc_t(tt), in0=oacc_t(tt),
                                   scalar=rl_sb[:, tt:tt + 1], in1=xt_t(tt),
                                   op0=ALU.mult, op1=ALU.add).then_inc(g("s_x2"), 1)
            if tt >= 1:
                evac_tile(tt - 1, 1)
            v.wait_ge(g("s_x2"), tt + 1)
            ln_vec(tt, 1, oacc_t(tt))
            if tt >= 3:
                ffn_evac(tt - 3)
        evac_tile(NTILE - 1, 1)
        ffn_evac(NTILE - 3)
        ffn_evac(NTILE - 2)
        ffn_evac(NTILE - 1)

    return nc, ctx


_cache = {}


def kernel(L0_attn_x, L0_attn_W_qkv, L0_ffn_G1_R, L0_ffn_G2_R, L0_ffn_G1_I,
           L0_ffn_G2_I, L0_ffn_bias_real, L0_ffn_bias_imag):
    x2d = np.ascontiguousarray(np.asarray(L0_attn_x, dtype=np.float32).reshape(NTOK, D))
    wqkv = np.ascontiguousarray(np.asarray(L0_attn_W_qkv, dtype=np.float32))
    g1r = np.ascontiguousarray(np.asarray(L0_ffn_G1_R, dtype=np.float32))
    g2r = np.ascontiguousarray(np.asarray(L0_ffn_G2_R, dtype=np.float32))
    g1i = np.ascontiguousarray(np.asarray(L0_ffn_G1_I, dtype=np.float32))
    g2i = np.ascontiguousarray(np.asarray(L0_ffn_G2_I, dtype=np.float32))
    br = np.ascontiguousarray(np.asarray(L0_ffn_bias_real, dtype=np.float32))
    bi = np.ascontiguousarray(np.asarray(L0_ffn_bias_imag, dtype=np.float32))

    if "nc" not in _cache:
        _cache["nc"] = build_nc()
    nc, _ctx = _cache["nc"]

    in_maps = [dict(x=x2d[NT * c:NT * (c + 1)], wqkv=wqkv, g1r=g1r, g2r=g2r,
                    g1i=g1i, g2i=g2i, br=br, bi=bi) for c in range(NCORE)]
    res = run_bass_kernel_spmd(nc, in_maps, core_ids=list(range(NCORE)))
    out = np.concatenate([res.results[c]["out"] for c in range(NCORE)], axis=0)
    return out.reshape(1, NTOK, D)


# version 15
# speedup vs baseline: 1.8617x; 1.0081x over previous
import numpy as np
from contextlib import ExitStack

import concourse.bass as bass
import concourse.mybir as mybir
from concourse.bass_utils import run_bass_kernel_spmd

F32 = mybir.dt.float32
BF16 = mybir.dt.bfloat16
AF = mybir.ActivationFunctionType
ALU = mybir.AluOpType
AX = mybir.AxisListType

NCORE = 8
NTOK = 16384
D = 1024
NT = NTOK // NCORE
NTILE = NT // 128
NBLK = NT // 512
MB = 512
NMB = NTOK // MB
NPAIR = NMB // 2
EPS = 1e-5
QSCALE = 1.0 / 32.0


def build_nc():
    nc = bass.Bass("TRN2", target_bir_lowering=False)

    x_h = nc.dram_tensor("x", [NT, D], F32, kind="ExternalInput").ap()
    wqkv_h = nc.dram_tensor("wqkv", [3 * D, D], F32, kind="ExternalInput").ap()
    g1r_h = nc.dram_tensor("g1r", [1, 32, 4, 32], F32, kind="ExternalInput").ap()
    g2r_h = nc.dram_tensor("g2r", [4, 16, 1, 16], F32, kind="ExternalInput").ap()
    g1i_h = nc.dram_tensor("g1i", [1, 32, 4, 32], F32, kind="ExternalInput").ap()
    g2i_h = nc.dram_tensor("g2i", [4, 16, 1, 16], F32, kind="ExternalInput").ap()
    br_h = nc.dram_tensor("br", [512], F32, kind="ExternalInput").ap()
    bi_h = nc.dram_tensor("bi", [512], F32, kind="ExternalInput").ap()
    out_h = nc.dram_tensor("out", [NT, D], F32, kind="ExternalOutput").ap()

    wqkv_bf = nc.dram_tensor("wqkv_bf", [3 * D, D], BF16, kind="Internal")
    kslice = [nc.dram_tensor(f"kslice{j}", [D, 512], BF16, kind="Internal") for j in range(4)]
    vslice = [nc.dram_tensor(f"vslice{j}", [512, D], BF16, kind="Internal") for j in range(4)]
    kfull = [nc.dram_tensor(f"kfull{j}", [NCORE * D, 512], BF16, kind="Internal",
                            addr_space="Shared") for j in range(4)]
    vfull = [nc.dram_tensor(f"vfull{j}", [NCORE * 512, D], BF16, kind="Internal",
                            addr_space="Shared") for j in range(4)]
    nrm_in = nc.dram_tensor("nrm_in", [1, 512], F32, kind="Internal")
    nrm_out = nc.dram_tensor("nrm_out", [1, 512], F32, kind="Internal")
    wb_l1 = [nc.dram_tensor(f"wb_l1_{w}", [1024, 256], F32, kind="Internal") for w in range(2)]
    wb_l2 = [nc.dram_tensor(f"wb_l2_{w}", [512, 512], F32, kind="Internal") for w in range(2)]
    eye_h = nc.inline_tensor(np.eye(128, dtype=np.float32), "eyec")

    ctx = ExitStack()

    regA = ctx.enter_context(nc.sbuf_tensor("regA", [128, 24576], BF16))
    regB = ctx.enter_context(nc.sbuf_tensor("regB", [128, 8192], BF16))
    regC = ctx.enter_context(nc.sbuf_tensor("regC", [128, 8192], BF16))
    wtR = ctx.enter_context(nc.sbuf_tensor("wtR", [128, 2048], BF16))
    wtI = ctx.enter_context(nc.sbuf_tensor("wtI", [128, 2048], BF16))
    wtIn = ctx.enter_context(nc.sbuf_tensor("wtIn", [128, 2048], BF16))
    qfm = ctx.enter_context(nc.sbuf_tensor("qfm", [128, 16384], BF16))
    oacc = ctx.enter_context(nc.sbuf_tensor("oacc", [128, 16384], F32))
    xt = ctx.enter_context(nc.sbuf_tensor("xt", [128, 2048], F32))
    xnf = ctx.enter_context(nc.sbuf_tensor("xnf", [128, 2048], BF16))
    eye_sb = ctx.enter_context(nc.sbuf_tensor("eye_sb", [128, 128], BF16))
    stats = ctx.enter_context(nc.sbuf_tensor("stats", [128, 16], F32))
    nstats = ctx.enter_context(nc.sbuf_tensor("nstats", [1, 16], F32))
    seps = ctx.enter_context(nc.sbuf_tensor("seps", [128, 1], F32))
    negm_bc = ctx.enter_context(nc.sbuf_tensor("negm_bc", [128, 1], F32))
    L_sb = ctx.enter_context(nc.sbuf_tensor("L_sb", [128, 16], F32))
    rl_sb = ctx.enter_context(nc.sbuf_tensor("rl_sb", [128, 16], F32))
    nrm_sb = ctx.enter_context(nc.sbuf_tensor("nrm_sb", [1, 512], F32))
    ones_f = ctx.enter_context(nc.sbuf_tensor("ones_f", [1, 128], F32))
    ones_b = ctx.enter_context(nc.sbuf_tensor("ones_b", [128, 1], BF16))
    sbr = ctx.enter_context(nc.sbuf_tensor("sbr", [1, 512], F32))
    sbi = ctx.enter_context(nc.sbuf_tensor("sbi", [1, 512], F32))
    g1cm = oacc[0:4, 0:1024]
    g2s = oacc[0:4, 1024:1280]
    a_tmp = [oacc[:, 1280:1536], oacc[:, 1536:1792]]
    wbA = oacc[0:32, 2048:6144]
    wbA2 = oacc[0:32, 6144:10240]
    assert nc.sbuf_bytes_remaining >= 0

    psA = ctx.enter_context(nc.psum_tensor("psA", [128, 1024], F32))
    psL = ctx.enter_context(nc.psum_tensor("psL", [128, 512], F32))
    psX = ctx.enter_context(nc.psum_tensor("psX", [128, 512], F32))
    ptctx = ExitStack()
    pT0 = ptctx.enter_context(nc.psum_tensor("pT0", [128, 512], F32))
    pT1 = ptctx.enter_context(nc.psum_tensor("pT1", [128, 512], F32))
    ptctx.close()
    psB = ctx.enter_context(nc.psum_tensor("psB", [128, 1024], F32))
    psV10 = ctx.enter_context(nc.psum_tensor("psV10", [128, 512], F32))
    psV11 = ctx.enter_context(nc.psum_tensor("psV11", [128, 512], F32))
    ps0 = psA[:, 0:512]
    ps1 = psA[:, 512:1024]
    _sbanks = [psA[:, 0:512], psA[:, 512:1024], psB[:, 0:512], psB[:, 512:1024]]
    pvpair = (psV10, psV11)
    pT = [pT0, pT1]

    semnames = ("s_dve s_act s_wb s_wbv s_wbs s_wbd s_wl2 s_wq s_wt s_wt2 s_x s_sq s_st s_rs s_ln s_tp s_fm s_pq s_ev "
                "s_ks s_qs s_nm s_nf s_nmsb s_store s_kst s_vst s_nst s_ccn s_cck s_ccv s_nb s_mi s_kv s_smm "
                "s_exp s_pv s_add s_lrl s_x2 s_la s_vv").split()
    sems = {n: ctx.enter_context(nc.semaphore(n)) for n in semnames}
    g = lambda n: sems[n]

    blk = ctx.enter_context(nc.Block())

    def wqkvT(c):
        return regA[:, 3072 * c:3072 * (c + 1)]
    def PT(u, mc):
        return regA[:, 8192 * u + 2048 * mc:8192 * u + 2048 * (mc + 1)]
    def hfm(c):
        return regA[:, 2048 * c:2048 * (c + 1)]
    def xnfm(par, c):
        return regB[:, 4096 * par + 512 * c:4096 * par + 512 * (c + 1)]
    def kbuf(par, c):
        return regB[:, 4096 * par + 512 * c:4096 * par + 512 * (c + 1)]
    def ktile(j):
        return regC[:, 512 * j:512 * (j + 1)]
    def vtile(j):
        return regC[:, 1024 + 512 * j:1024 + 512 * (j + 1)]
    def ksq(c):
        return regC[:, 2048 + 512 * c:2048 + 512 * (c + 1)]
    def trash():
        return regC[:, 6144:7168]
    def vbuf(par, mc):
        return regC[:, 4096 * par + 1024 * mc:4096 * par + 1024 * (mc + 1)]
    def xnf_t(t_idx):
        return xnf[:, 1024 * (t_idx % 2):1024 * (t_idx % 2 + 1)]
    def xt_t(t_idx):
        return xt[:, 1024 * (t_idx % 2):1024 * (t_idx % 2 + 1)]
    def oacc_t(t_idx):
        return oacc[:, 1024 * t_idx:1024 * (t_idx + 1)]
    def st_col(par, i):
        return stats[:, 8 * par + i:8 * par + i + 1]

    WB, WBv, WBs, WBd, WL2 = {}, {}, {}, {}, {}
    for w in range(2):
        for j in range(8):
            WB[f"mm{w}_{j}"] = 8 * w + j + 1
            WBv[f"evac{w}_{j}"] = 10 * w + j + 1
            WBd[f"store{w}_{j}"] = 16 * (8 * w + j + 1)
        for h in range(2):
            WBv[f"reorder{w}_{h}"] = 10 * w + 9 + h
            WBs[f"read1_{w}_{h}"] = 80 + 32 * w + 16 * h
            WL2[f"write2_{w}_{h}"] = 16 * (6 * w + h + 1)
        WBs[f"loads{w}"] = 32 + 32 * w
        for q in range(4):
            WL2[f"read2_{w}_{q}"] = 16 * (6 * w + 3 + q)
    WB_ALL = 16
    WBV_ALL = 20
    WBD_ALL = 16 * 16
    WL2_ALL = 16 * 12
    KV_STORES = 16 * NBLK
    ST_ALL_P1 = KV_STORES + 1
    ST_ALL = ST_ALL_P1 + NTILE

    @blk.sync
    def _(sync):
        s = sync
        _qlast = [None]
        def sdma(out, in_, sem, **kw):
            if _qlast[0] is not None:
                s.wait_ge(*_qlast[0])
            ins = s.dma_start(out=out, in_=in_, **kw)
            ins.then_inc(sem, 16)
            _cnt = _qcnt.get(sem.name if hasattr(sem, 'name') else id(sem), 0) + 16
            _qcnt[sem.name if hasattr(sem, 'name') else id(sem)] = _cnt
            _qlast[0] = (sem, _cnt)
            return ins
        _qcnt = {}
        def xload(t_):
            if t_ >= 2:
                s.wait_ge(g("s_ln"), t_ - 1)
                s.wait_ge(g("s_sq"), t_ - 1)
            sdma(xt_t(t_), x_h[128 * t_:128 * (t_ + 1), :], g("s_x"))

        xload(0)
        xload(1)
        for w in range(2):
            g1h, g2h = (g1r_h, g2r_h) if w == 0 else (g1i_h, g2i_h)
            if w == 1:
                s.wait_ge(g("s_wb"), WB["mm0_7"])
            sdma(g1cm.rearrange("c (b d) -> c b d", b=32),
                 g1h.rearrange("a b c d -> c (a b) d"), g("s_wbs"))
            sdma(g2s, g2h.rearrange("c e f g -> c (e f g)"), g("s_wbs"))
        for w in range(2):
            for h in range(2):
                if (w, h) != (0, 0):
                    prevr = WBv[f"reorder{w}_{h - 1}"] if h == 1 else WBv[f"reorder{w - 1}_1"]
                    s.wait_ge(g("s_wbv"), prevr)
                s.wait_ge(g("s_wbd"), WBd[f"store{w}_{4 * h + 3}"])
                sdma(wbA.rearrange("p (b eg) -> p b eg", b=16),
                     wb_l1[w].ap()[512 * h:512 * (h + 1), :]
                     .rearrange("(b d) eg -> d b eg", b=16), g("s_wbs"))
        s.wait_ge(g("s_wq"), 32)
        for c in range(4):
            sdma(wqkvT(c), wqkv_bf.ap()[:, 128 * c:128 * (c + 1)], g("s_wt"), transpose=True)
        s.wait_ge(g("s_wq"), 48)
        for c in range(4, 8):
            sdma(wqkvT(c), wqkv_bf.ap()[:, 128 * c:128 * (c + 1)], g("s_wt"), transpose=True)
        for t_ in range(2, NTILE):
            xload(t_)
        sdma(sbr[:, :], br_h[None, :], g("s_nb"))
        sdma(sbi[:, :], bi_h[None, :], g("s_nb"))
        def kload(mb):
            par = mb % 2
            r, lb = mb // 4, mb % 4
            if mb < 4:
                s.wait_ge(g("s_cck"), lb + 1)
            if mb >= 2:
                s.wait_ge(g("s_smm"), 4 * (mb - 1))
            sdma(regB[:, 4096 * par:4096 * (par + 1)].rearrange("p (c m) -> p c m", c=8),
                 kfull[lb].ap()[1024 * r:1024 * (r + 1), :]
                 .rearrange("(c p) m -> p c m", p=128), g("s_kv"))
        s.wait_ge(g("s_pq"), 96)
        kload(0)
        kload(1)
        s.wait_ge(g("s_ccn"), 1)
        sdma(nrm_sb[:, :], nrm_out.ap()[:, :], g("s_nb"))
        for mb_ in range(2, NMB):
            kload(mb_)
        for t in range(NTILE):
            if t >= 2:
                s.wait_ge(g("s_x2"), t - 1)
            sdma(xt_t(t), x_h[128 * t:128 * (t + 1), :], g("s_x"))

    @blk.gpsimd
    def _(gp):
        _glast = [None]
        _gcnt = {}
        def gdma(out, in_, sem, **kw):
            if _glast[0] is not None:
                gp.wait_ge(*_glast[0])
            ins = gp.dma_start(out=out, in_=in_, **kw)
            ins.then_inc(sem, 16)
            k = sem.name if hasattr(sem, 'name') else id(sem)
            _gcnt[k] = _gcnt.get(k, 0) + 16
            _glast[0] = (sem, _gcnt[k])
            return ins
        gdma(eye_sb[:, :], eye_h.ap()[:, :], g("s_wq"))
        gdma(wqkv_bf.ap()[:, 0:512], wqkv_h[:, 0:512], g("s_wq"))
        gdma(wqkv_bf.ap()[:, 512:1024], wqkv_h[:, 512:1024], g("s_wq"))
        for w in range(2):
            for h in range(2):
                gp.wait_ge(g("s_wbv"), WBv[f"reorder{w}_{h}"])
                gdma(wb_l2[w].ap().rearrange("(d gg) (b e) -> d gg b e", gg=16, b=32)[:, :, 16 * h:16 * (h + 1), :],
                     wbA2.rearrange("p (gg b e) -> p gg b e", gg=16, b=16), g("s_wl2"))
            wt = wtR if w == 0 else wtI
            for q in range(4):
                gdma(wt[:, 512 * q:512 * (q + 1)], wb_l2[w].ap()[128 * q:128 * (q + 1), :], g("s_wl2"))
        rg = [list(range(NCORE))]
        for b2 in range(3):
            gp.wait_ge(g("s_kst"), 16 * 8 * (b2 + 1))
            gp.collective_compute("AllGather", ALU.bypass, replica_groups=rg,
                                  ins=[kslice[b2].ap()[:, :]], outs=[kfull[b2].ap()[:, :]]).then_inc(g("s_cck"))
        gp.wait_ge(g("s_nmsb"), 1)
        gdma(nrm_in.ap()[:, :], nrm_sb[:, :], g("s_nst"))
        gp.wait_ge(g("s_nst"), 16)
        gp.collective_compute("AllReduce", ALU.max, replica_groups=rg,
                              ins=[nrm_in.ap()[:, :]], outs=[nrm_out.ap()[:, :]]).then_inc(g("s_ccn"))
        gp.wait_ge(g("s_kst"), 16 * 8 * 4)
        gp.collective_compute("AllGather", ALU.bypass, replica_groups=rg,
                              ins=[kslice[3].ap()[:, :]], outs=[kfull[3].ap()[:, :]]).then_inc(g("s_cck"))
        for b2 in range(4):
            gp.wait_ge(g("s_vst"), 16 * 8 * (b2 + 1))
            gp.collective_compute("AllGather", ALU.bypass, replica_groups=rg,
                                  ins=[vslice[b2].ap()[:, :]], outs=[vfull[b2].ap()[:, :]]).then_inc(g("s_ccv"))
        _glast[0] = (g("s_ccv"), 1)
        gp.wait_ge(g("s_nm"), 8)
        for mb_ in range(NMB):
            par_ = mb_ % 2
            r_, lb_ = mb_ // 4, mb_ % 4
            if mb_ < 4:
                gp.wait_ge(g("s_ccv"), lb_ + 1)
            if mb_ >= 2:
                gp.wait_ge(g("s_pv"), 16 * (mb_ - 1))
            gdma(regC[:, 4096 * par_:4096 * (par_ + 1)].rearrange("p (c m) -> p c m", c=4),
                 vfull[lb_].ap()[512 * r_:512 * (r_ + 1), :]
                 .rearrange("(c p) m -> p c m", p=128), g("s_vv"))


    @blk.tensor
    def _(t):
        for w in range(2):
            t.wait_ge(g("s_wbs"), WBs[f"loads{w}"])
            for j in range(8):
                if not (w == 0 and j == 0):
                    prev = WBv[f"evac{w}_{j - 1}"] if j > 0 else WBv["evac0_7"]
                    t.wait_ge(g("s_wbv"), prev)
                t.matmul(psX[:, 0:256], g1cm[:, 128 * j:128 * (j + 1)], g2s,
                         start=True, stop=True).then_inc(g("s_wb"), 1)

        t.wait_ge(g("s_wq"), 16)

        def transpose_tile(t_idx, ph3):
            base = 128 * ph3
            for c in range(8):
                G = base + 8 * t_idx + c
                if c == 0:
                    t.wait_ge(g("s_ln"), 16 * ph3 + t_idx + 1)
                if G >= 2:
                    t.wait_ge(g("s_fm"), G - 1)
                t.matmul(pT[c % 2][:, 0:128],
                         xnf_t(t_idx)[:, 128 * c:128 * (c + 1)],
                         eye_sb[:, :], start=True, stop=True).then_inc(g("s_tp"), 1)

        _pj4 = [ps0, ps1, psV10[:, :], psV11[:, :]]
        def proj_block(b):
            par = b % 2
            gbase = 24 * b
            if b == 0:
                t.wait_ge(g("s_wt"), 128)
            t.wait_ge(g("s_fm"), 32 * (b + 1))
            grp = 0
            for ek in range(8):
                gi = gbase + grp
                if gi >= 4:
                    t.wait_ge(g("s_ev"), gi - 3)
                ps = _pj4[grp % 4]
                for c in range(8):
                    ins = t.matmul(ps[:, :], wqkvT(c)[:, 1024 + 128 * ek:1024 + 128 * (ek + 1)],
                                   xnfm(par, c), start=(c == 0), stop=(c == 7))
                    if c == 7:
                        ins.then_inc(g("s_pq"), 1)
                grp += 1
            for ek in range(8):
                t.wait_ge(g("s_ks"), 8 * b + ek + 1)
                if ek == 0 and b >= 1:
                    t.wait_ge(g("s_nf"), b)
                ins = t.matmul(psL[0:1, 0:512], ones_b[:, :], ksq(ek),
                               start=(ek == 0), stop=(ek == 7))
                if ek == 7:
                    ins.then_inc(g("s_nm"), 1)
            for eq in range(8):
                gi = gbase + grp
                t.wait_ge(g("s_ev"), gi - 3)
                ps = _pj4[grp % 4]
                for c in range(8):
                    ins = t.matmul(ps[:, :], wqkvT(c)[:, 128 * eq:128 * (eq + 1)],
                                   xnfm(par, c), start=(c == 0), stop=(c == 7))
                    if c == 7:
                        ins.then_inc(g("s_pq"), 1)
                grp += 1
            for i in range(8):
                nc4, h = i // 2, i % 2
                gi = gbase + grp
                t.wait_ge(g("s_ev"), gi - 3)
                ps = _pj4[grp % 4]
                for c in range(8):
                    ins = t.matmul(ps[:, :], xnfm(par, c)[:, 128 * nc4:128 * (nc4 + 1)],
                                   wqkvT(c)[:, 2048 + 512 * h:2048 + 512 * (h + 1)],
                                   start=(c == 0), stop=(c == 7))
                    if c == 7:
                        ins.then_inc(g("s_pq"), 1)
                grp += 1

        for b in range(NBLK):
            for tt in range(4 * b, 4 * b + 4):
                transpose_tile(tt, 0)
            if b >= 1:
                proj_block(b - 1)
        proj_block(NBLK - 1)

        for ns in range(4):
            t.wait_ge(g("s_qs"), 8 * (ns + 1))
            t.wait_ge(g("s_nf"), 4 + ns)
            for c in range(8):
                ins = t.matmul(psL[0:1, 0:512], ones_b[:, :], ksq(c),
                               start=(c == 0), stop=(c == 7))
                if c == 7:
                    ins.then_inc(g("s_nm"), 1)

        t.wait_ge(g("s_mi"), 2)
        t.matmul(psX[:, 0:1], ones_f[:, :], nstats[0:1, 5:6],
                 start=True, stop=True).then_inc(g("s_mi"), 1)

        def s_part(mb, mc, cpair):
            par = mb % 2
            EG = 4 * mb + mc
            if mc == 0 and cpair == 0:
                t.wait_ge(g("s_kv"), 16 * (mb + 1))
            if EG == 0 and cpair == 0:
                t.wait_ge(g("s_ev"), 96)
            if cpair == 0 and EG >= 1:
                t.wait_ge(g("s_exp"), 2 * EG)
            for c in (2 * cpair, 2 * cpair + 1):
                lhs = kbuf(par, c)[:, 128 * mc:128 * (mc + 1)]
                for sqs in range(4):
                    ins = t.matmul(_sbanks[sqs], lhs,
                                   qfm[:, 2048 * c + 512 * sqs:2048 * c + 512 * (sqs + 1)],
                                   start=(c == 0), stop=(c == 7))
                    if c == 7 and sqs == 3:
                        ins.then_inc(g("s_smm"), 1)

        def pv_group(mb, qc, tail=False):
            idx = 16 * mb + qc
            pr = (psB[:, 0:512], psB[:, 512:1024]) if (tail and idx % 2 == 1) else \
                 (pvpair[0][:, :], pvpair[1][:, :])
            if qc == 0:
                t.wait_ge(g("s_vv"), 16 * (mb + 1))
                t.wait_ge(g("s_exp"), 8 * (mb + 1))
            if idx == 0:
                t.wait_ge(g("s_mi"), 4)
            if idx >= 1:
                t.wait_ge(g("s_la"), idx)
                t.wait_ge(g("s_add"), idx - 1 if (tail and idx > 16 * (NMB - 2)) else idx)
            for sub in range(4):
                lhs = PT(mb % 3, sub)[:, 128 * qc:128 * (qc + 1)]
                t.matmul(pr[0], lhs, vbuf(mb % 2, sub)[:, 0:512],
                         start=(sub == 0), stop=(sub == 3))
                t.matmul(psX[:, 0:1], lhs, ones_b[:, :],
                         start=(sub == 0), stop=(sub == 3))
                ins = t.matmul(pr[1], lhs, vbuf(mb % 2, sub)[:, 512:1024],
                               start=(sub == 0), stop=(sub == 3))
                if sub == 3:
                    ins.then_inc(g("s_pv"), 1)

        for mb in range(NMB):
            for u in range(16):
                if mb >= 2:
                    pv_group(mb - 2, u)
                s_part(mb, u // 4, u % 4)
        t.wait_ge(g("s_exp"), 8 * NMB)
        for mb in (NMB - 2, NMB - 1):
            for qc in range(16):
                pv_group(mb, qc, tail=True)

        def ffn_group(qc):
            t.wait_ge(g("s_fm"), 128 + 8 * (qc + 1))
            t.wait_ge(g("s_nb"), 32)
            if qc >= 2:
                t.wait_ge(g("s_ev"), 96 + 2 * (qc - 1))
            pair = (ps0, ps1) if qc % 2 == 0 else (psL, psX)
            for half in range(2):
                ps = pair[half]
                wa, wb_ = (wtR, wtIn) if half == 0 else (wtI, wtR)
                bias = sbr if half == 0 else sbi
                for c in range(4):
                    t.matmul(ps[:, :], hfm(c)[:, 128 * qc:128 * (qc + 1)],
                             wa[:, 512 * c:512 * (c + 1)], start=(c == 0), stop=False)
                for c in range(4):
                    t.matmul(ps[:, :], hfm(4 + c)[:, 128 * qc:128 * (qc + 1)],
                             wb_[:, 512 * c:512 * (c + 1)], start=False, stop=False)
                t.matmul(ps[:, :], ones_f[:, :], bias[:, :],
                         start=False, stop=True).then_inc(g("s_pq"), 1)

        transpose_tile(0, 1)
        transpose_tile(1, 1)
        for qc in range(NTILE - 2):
            transpose_tile(qc + 2, 1)
            ffn_group(qc)
        ffn_group(NTILE - 2)
        ffn_group(NTILE - 1)

    @blk.scalar
    def _(sc):
        _acnt = [0]
        def amark(ins):
            ins.then_inc(g("s_act"), 1)
            _acnt[0] += 1
        def ajoin():
            sc.wait_ge(g("s_act"), _acnt[0])
        for w in range(2):
            for j in range(8):
                sc.wait_ge(g("s_wbv"), WBv[f"evac{w}_{j}"])
                if 8 * w + j >= 1:
                    sc.wait_ge(g("s_wbd"), 16 * (8 * w + j))
                sc.dma_start(out=wb_l1[w].ap()[128 * j:128 * (j + 1), :],
                             in_=a_tmp[j % 2]).then_inc(g("s_wbd"), 16)

        def ln_stats(t_idx, ph3, src_ap):
            par = t_idx % 2
            off = 16 * ph3
            if ph3:
                sc.wait_ge(g("s_x2"), t_idx + 1)
            else:
                sc.wait_ge(g("s_x"), 16 * (t_idx + 1))
            sc.activation(out=trash(), in_=src_ap, func=AF.Square,
                          accum_out=st_col(par, 4)).then_inc(g("s_sq"), 1)
            sc.wait_ge(g("s_st"), off + t_idx + 1)
            sc.wait_ge(g("s_sq"), off + t_idx + 1)
            amark(sc.activation(out=st_col(par, 5), in_=st_col(par, 4), func=AF.Ln,
                                scale=1.0 / D, bias=st_col(par, 3)))
            ajoin()
            sc.activation(out=st_col(par, 6), in_=st_col(par, 5), func=AF.Exp,
                          scale=-0.5).then_inc(g("s_rs"), 1)

        _aj4 = [ps0, ps1, psV10[:, :], psV11[:, :]]
        def evac_block(b):
            gbase = 24 * b
            for ek in range(8):
                sc.wait_ge(g("s_pq"), gbase + ek + 1)
                kk = 8 * b + ek
                if kk >= 2:
                    sc.wait_ge(g("s_kst"), 16 * kk)
                if ek == 0 and b >= 1:
                    sc.wait_ge(g("s_nm"), b)
                ps = _aj4[ek % 4]
                sc.activation(out=ksq(ek), in_=ps[:, :], func=AF.Square).then_inc(g("s_ks"), 1)
                sc.activation(out=ktile(ek % 2), in_=ps[:, :], func=AF.Copy).then_inc(g("s_ev"), 1)
                sc.wait_ge(g("s_ev"), gbase + ek + 1)
                sc.dma_start(out=kslice[b].ap()[128 * ek:128 * (ek + 1), :],
                             in_=ktile(ek % 2)).then_inc(g("s_kst"), 16)
            for eq in range(8):
                sc.wait_ge(g("s_pq"), gbase + 8 + eq + 1)
                ps = _aj4[eq % 4]
                sc.activation(out=qfm[:, 2048 * eq + 512 * b:2048 * eq + 512 * (b + 1)],
                              in_=ps[:, :], func=AF.Copy, scale=QSCALE).then_inc(g("s_ev"), 1)
            for i in range(8):
                sc.wait_ge(g("s_pq"), gbase + 16 + i + 1)
                vv = 8 * b + i
                if vv >= 2:
                    sc.wait_ge(g("s_vst"), 16 * vv)
                nc4, h = i // 2, i % 2
                ps = _aj4[i % 4]
                sc.activation(out=vtile(i % 2), in_=ps[:, :], func=AF.Copy).then_inc(g("s_ev"), 1)
                sc.wait_ge(g("s_ev"), gbase + 16 + i + 1)
                sc.dma_start(out=vslice[b].ap()[128 * nc4:128 * (nc4 + 1),
                                                512 * h:512 * (h + 1)],
                             in_=vtile(i % 2)).then_inc(g("s_vst"), 16)

        for b in range(NBLK):
            for tt in range(4 * b, 4 * b + 4):
                ln_stats(tt, 0, xt_t(tt))
            if b >= 1:
                evac_block(b - 1)
        evac_block(NBLK - 1)

        for ns in range(4):
            sc.wait_ge(g("s_nm"), 4 + ns)
            for c in range(8):
                sc.activation(out=ksq(c), in_=qfm[:, 2048 * c + 512 * ns:2048 * c + 512 * (ns + 1)],
                              func=AF.Square).then_inc(g("s_qs"), 1)

        sc.wait_ge(g("s_mi"), 1)
        amark(sc.activation(out=nstats[0:1, 4:5], in_=nstats[0:1, 3:4], func=AF.Sqrt))
        ajoin()
        sc.activation(out=nstats[0:1, 5:6], in_=nstats[0:1, 4:5], func=AF.Copy,
                      scale=-1.0).then_inc(g("s_mi"), 1)

        ecount = 0
        for mb in range(NMB):
            u = mb % 3
            if mb >= 3:
                sc.wait_ge(g("s_pv"), 16 * (mb - 2))
            for mc in range(4):
                sc.wait_ge(g("s_smm"), ecount + 1)
                if ecount == 0:
                    sc.wait_ge(g("s_mi"), 4)
                sc.activation(out=PT(u, mc)[:, 0:1024], in_=psA[:, :], func=AF.Exp,
                              bias=negm_bc[:, :]).then_inc(g("s_exp"), 1)
                sc.activation(out=PT(u, mc)[:, 1024:2048], in_=psB[:, :], func=AF.Exp,
                              bias=negm_bc[:, :]).then_inc(g("s_exp"), 1)
                ecount += 1

        for tt in range(NTILE):
            ln_stats(tt, 1, oacc_t(tt))
        for qc in range(NTILE):
            sc.wait_ge(g("s_ev"), 96 + 2 * (qc + 1))
            sc.dma_start(out=out_h[128 * qc:128 * (qc + 1), :],
                         in_=oacc_t(qc)).then_inc(g("s_store"), 16)
        sc.wait_ge(g("s_store"), 16 * NTILE)

    @blk.vector
    def _(v):
        _dcnt = [0]
        def dmark(ins):
            ins.then_inc(g("s_dve"), 1)
            _dcnt[0] += 1
        def djoin():
            v.wait_ge(g("s_dve"), _dcnt[0])
        v.memset(ones_f[:, :], 1.0)
        v.memset(ones_b[:, :], 1.0)
        v.memset(seps[:, :], EPS)
        for w in range(2):
            for j in range(8):
                v.wait_ge(g("s_wb"), WB[f"mm{w}_{j}"])
                if (w, j) > (0, 1):
                    v.wait_ge(g("s_wbd"), 16 * (8 * w + j - 1))
                v.tensor_copy(out=a_tmp[j % 2], in_=psX[:, 0:256]).then_inc(g("s_wbv"), 1)
            for h in range(2):
                v.wait_ge(g("s_wbs"), WBs[f"read1_{w}_{h}"])
                if not (w == 0 and h == 0):
                    v.wait_ge(g("s_wl2"), WL2[f"write2_{w}_{h - 1}"] if h == 1
                              else WL2["write2_0_1"])
                v.tensor_copy(out=wbA2.rearrange("p (gg b e) -> p gg b e", gg=16, b=16),
                              in_=wbA.rearrange("p (b e gg) -> p gg b e", b=16, e=16)) \
                    .then_inc(g("s_wbv"), 1)
        v.wait_ge(g("s_wl2"), WL2_ALL)
        ins = None
        for q in range(4):
            ins = v.tensor_scalar_mul(out=wtIn[:, 512 * q:512 * (q + 1)],
                                      in0=wtI[:, 512 * q:512 * (q + 1)], scalar1=-1.0)
        ins.then_inc(g("s_wbv"), 1)

        def ln_vec(t_idx, ph3, src_ap):
            par = t_idx % 2
            off = 16 * ph3
            if not ph3:
                v.wait_ge(g("s_x"), 16 * (t_idx + 1))
            dmark(v.reduce_sum(st_col(par, 0), src_ap, axis=AX.X))
            djoin()
            dmark(v.tensor_scalar_mul(out=st_col(par, 1), in0=st_col(par, 0), scalar1=1.0 / D))
            djoin()
            dmark(v.scalar_tensor_tensor(out=st_col(par, 2), in0=st_col(par, 1), scalar=-1.0,
                                         in1=st_col(par, 1), op0=ALU.mult, op1=ALU.mult))
            djoin()
            v.tensor_scalar_add(out=st_col(par, 3), in0=st_col(par, 2),
                                scalar1=EPS).then_inc(g("s_st"), 1)
            v.wait_ge(g("s_rs"), off + t_idx + 1)
            dmark(v.scalar_tensor_tensor(out=st_col(par, 7), in0=st_col(par, 1), scalar=-1.0,
                                         in1=st_col(par, 6), op0=ALU.mult, op1=ALU.mult))
            if t_idx >= 2:
                v.wait_ge(g("s_tp"), 128 * ph3 + 8 * (t_idx - 1))
            djoin()
            v.tensor_scalar(out=xnf_t(t_idx), in0=src_ap,
                            scalar1=st_col(par, 6), scalar2=st_col(par, 7),
                            op0=ALU.mult, op1=ALU.add).then_inc(g("s_ln"), 1)

        def evac_tile(t_idx, ph3):
            for c in range(8):
                G = 128 * ph3 + 8 * t_idx + c
                v.wait_ge(g("s_tp"), G + 1)
                if (not ph3) and t_idx % 4 == 0 and c == 0 and t_idx // 4 >= 2:
                    v.wait_ge(g("s_pq"), 24 * (t_idx // 4 - 1))
                if ph3:
                    dst = regA[:, 2048 * c + 128 * t_idx:2048 * c + 128 * (t_idx + 1)]
                else:
                    par = (t_idx // 4) % 2
                    dst = regB[:, 4096 * par + 512 * c + 128 * (t_idx % 4):
                               4096 * par + 512 * c + 128 * (t_idx % 4) + 128]
                v.tensor_copy(out=dst, in_=pT[c % 2][:, 0:128]).then_inc(g("s_fm"), 1)

        def knorm_read(b):
            v.wait_ge(g("s_nm"), b + 1)
            if b == 0:
                v.reduce_max(nstats[0:1, 0:1], psL[0:1, 0:512], axis=AX.X).then_inc(g("s_nf"), 1)
            else:
                dmark(v.reduce_max(nstats[0:1, 2:3], psL[0:1, 0:512], axis=AX.X))
                djoin()
                v.tensor_max(out=nstats[0:1, 0:1], in0=nstats[0:1, 0:1],
                             in1=nstats[0:1, 2:3]).then_inc(g("s_nf"), 1)

        for tt in range(NTILE):
            if tt >= 9 and (tt - 9) % 4 == 0:
                knorm_read((tt - 9) // 4)
            if tt >= 1:
                evac_tile(tt - 1, 0)
            ln_vec(tt, 0, xt_t(tt))
        evac_tile(NTILE - 1, 0)
        knorm_read(NBLK - 2)
        knorm_read(NBLK - 1)
        djoin()
        dmark(v.memset(nrm_sb[:, :], 0.0))
        djoin()
        v.tensor_scalar_add(out=nrm_sb[:, :], in0=nrm_sb[:, :],
                            scalar1=nstats[0:1, 0:1]).then_inc(g("s_nmsb"), 1)
        for ns in range(4):
            v.wait_ge(g("s_nm"), 5 + ns)
            if ns == 0:
                v.reduce_max(nstats[0:1, 1:2], psL[0:1, 0:512], axis=AX.X).then_inc(g("s_nf"), 1)
            else:
                dmark(v.reduce_max(nstats[0:1, 2:3], psL[0:1, 0:512], axis=AX.X))
                djoin()
                v.tensor_max(out=nstats[0:1, 1:2], in0=nstats[0:1, 1:2],
                             in1=nstats[0:1, 2:3]).then_inc(g("s_nf"), 1)
        v.wait_ge(g("s_nb"), 48)
        v.wait_ge(g("s_nf"), 8)
        dmark(v.reduce_max(nstats[0:1, 2:3], nrm_sb[0:1, :], axis=AX.X))
        djoin()
        v.tensor_mul(out=nstats[0:1, 3:4], in0=nstats[0:1, 2:3],
                     in1=nstats[0:1, 1:2]).then_inc(g("s_mi"), 1)
        v.wait_ge(g("s_mi"), 3)
        v.tensor_copy(out=negm_bc[:, :], in_=psX[:, 0:1]).then_inc(g("s_mi"), 1)

        v.wait_ge(g("s_wbd"), WBD_ALL)
        v.wait_ge(g("s_wl2"), WL2_ALL)
        for mb in range(NMB):
            for qc in range(16):
                idx = 16 * mb + qc
                tail = mb >= NMB - 2
                pva, pvb = (psB[:, 0:512], psB[:, 512:1024]) if (tail and idx % 2 == 1) else \
                           (pvpair[0][:, :], pvpair[1][:, :])
                v.wait_ge(g("s_pv"), idx + 1)
                if mb == 0:
                    v.tensor_copy(out=L_sb[:, qc:qc + 1], in_=psX[:, 0:1]).then_inc(g("s_la"), 1)
                    v.tensor_copy(out=oacc[:, 1024 * qc:1024 * qc + 512], in_=pva)
                    v.tensor_copy(out=oacc[:, 1024 * qc + 512:1024 * (qc + 1)],
                                  in_=pvb).then_inc(g("s_add"), 1)
                else:
                    v.tensor_add(out=L_sb[:, qc:qc + 1], in0=psX[:, 0:1],
                                 in1=L_sb[:, qc:qc + 1]).then_inc(g("s_la"), 1)
                    v.tensor_add(out=oacc[:, 1024 * qc:1024 * qc + 512],
                                 in0=pva, in1=oacc[:, 1024 * qc:1024 * qc + 512])
                    v.tensor_add(out=oacc[:, 1024 * qc + 512:1024 * (qc + 1)],
                                 in0=pvb,
                                 in1=oacc[:, 1024 * qc + 512:1024 * (qc + 1)]).then_inc(g("s_add"), 1)
        v.wait_ge(g("s_la"), NMB * 16)
        v.reciprocal(out=rl_sb[:, :], in_=L_sb[:, :]).then_inc(g("s_lrl"), 1)

        def ffn_evac(qc):
            v.wait_ge(g("s_pq"), 96 + 2 * (qc + 1))
            pair = (ps0, ps1) if qc % 2 == 0 else (psL, psX)
            v.tensor_add(out=oacc[:, 1024 * qc:1024 * qc + 512],
                         in0=pair[0][:, :], in1=oacc[:, 1024 * qc:1024 * qc + 512])
            v.tensor_add(out=oacc[:, 1024 * qc + 512:1024 * (qc + 1)],
                         in0=pair[1][:, :],
                         in1=oacc[:, 1024 * qc + 512:1024 * (qc + 1)]).then_inc(g("s_ev"), 2)

        for tt in range(NTILE):
            v.wait_ge(g("s_x"), 256 + 16 * (tt + 1))
            if tt == 0:
                v.wait_ge(g("s_lrl"), 1)
            v.scalar_tensor_tensor(out=oacc_t(tt), in0=oacc_t(tt),
                                   scalar=rl_sb[:, tt:tt + 1], in1=xt_t(tt),
                                   op0=ALU.mult, op1=ALU.add).then_inc(g("s_x2"), 1)
            if tt >= 1:
                evac_tile(tt - 1, 1)
            v.wait_ge(g("s_x2"), tt + 1)
            ln_vec(tt, 1, oacc_t(tt))
            if tt >= 3:
                ffn_evac(tt - 3)
        evac_tile(NTILE - 1, 1)
        ffn_evac(NTILE - 3)
        ffn_evac(NTILE - 2)
        ffn_evac(NTILE - 1)

    return nc, ctx


_cache = {}


def kernel(L0_attn_x, L0_attn_W_qkv, L0_ffn_G1_R, L0_ffn_G2_R, L0_ffn_G1_I,
           L0_ffn_G2_I, L0_ffn_bias_real, L0_ffn_bias_imag):
    x2d = np.ascontiguousarray(np.asarray(L0_attn_x, dtype=np.float32).reshape(NTOK, D))
    wqkv = np.ascontiguousarray(np.asarray(L0_attn_W_qkv, dtype=np.float32))
    g1r = np.ascontiguousarray(np.asarray(L0_ffn_G1_R, dtype=np.float32))
    g2r = np.ascontiguousarray(np.asarray(L0_ffn_G2_R, dtype=np.float32))
    g1i = np.ascontiguousarray(np.asarray(L0_ffn_G1_I, dtype=np.float32))
    g2i = np.ascontiguousarray(np.asarray(L0_ffn_G2_I, dtype=np.float32))
    br = np.ascontiguousarray(np.asarray(L0_ffn_bias_real, dtype=np.float32))
    bi = np.ascontiguousarray(np.asarray(L0_ffn_bias_imag, dtype=np.float32))

    if "nc" not in _cache:
        _cache["nc"] = build_nc()
    nc, _ctx = _cache["nc"]

    in_maps = [dict(x=x2d[NT * c:NT * (c + 1)], wqkv=wqkv, g1r=g1r, g2r=g2r,
                    g1i=g1i, g2i=g2i, br=br, bi=bi) for c in range(NCORE)]
    res = run_bass_kernel_spmd(nc, in_maps, core_ids=list(range(NCORE)))
    out = np.concatenate([res.results[c]["out"] for c in range(NCORE)], axis=0)
    return out.reshape(1, NTOK, D)


# version 16
# speedup vs baseline: 1.8760x; 1.0077x over previous
import numpy as np
from contextlib import ExitStack

import concourse.bass as bass
import concourse.mybir as mybir
from concourse.bass_utils import run_bass_kernel_spmd

F32 = mybir.dt.float32
BF16 = mybir.dt.bfloat16
AF = mybir.ActivationFunctionType
ALU = mybir.AluOpType
AX = mybir.AxisListType

NCORE = 8
NTOK = 16384
D = 1024
NT = NTOK // NCORE
NTILE = NT // 128
NBLK = NT // 512
MB = 512
NMB = NTOK // MB
NPAIR = NMB // 2
EPS = 1e-5
QSCALE = 1.0 / 32.0


def build_nc():
    nc = bass.Bass("TRN2", target_bir_lowering=False)

    x_h = nc.dram_tensor("x", [NT, D], F32, kind="ExternalInput").ap()
    wqkv_h = nc.dram_tensor("wqkv", [3 * D, D], F32, kind="ExternalInput").ap()
    g1r_h = nc.dram_tensor("g1r", [1, 32, 4, 32], F32, kind="ExternalInput").ap()
    g2r_h = nc.dram_tensor("g2r", [4, 16, 1, 16], F32, kind="ExternalInput").ap()
    g1i_h = nc.dram_tensor("g1i", [1, 32, 4, 32], F32, kind="ExternalInput").ap()
    g2i_h = nc.dram_tensor("g2i", [4, 16, 1, 16], F32, kind="ExternalInput").ap()
    br_h = nc.dram_tensor("br", [512], F32, kind="ExternalInput").ap()
    bi_h = nc.dram_tensor("bi", [512], F32, kind="ExternalInput").ap()
    out_h = nc.dram_tensor("out", [NT, D], F32, kind="ExternalOutput").ap()

    wqkv_bf = nc.dram_tensor("wqkv_bf", [3 * D, D], BF16, kind="Internal")
    kslice = [nc.dram_tensor(f"kslice{j}", [D, 512], BF16, kind="Internal") for j in range(4)]
    vslice = [nc.dram_tensor(f"vslice{j}", [512, D], BF16, kind="Internal") for j in range(4)]
    kfull = [nc.dram_tensor(f"kfull{j}", [NCORE * D, 512], BF16, kind="Internal",
                            addr_space="Shared") for j in range(4)]
    vfull = [nc.dram_tensor(f"vfull{j}", [NCORE * 512, D], BF16, kind="Internal",
                            addr_space="Shared") for j in range(4)]
    nrm_in = nc.dram_tensor("nrm_in", [1, 512], F32, kind="Internal")
    nrm_out = nc.dram_tensor("nrm_out", [1, 512], F32, kind="Internal")
    wb_l1 = [nc.dram_tensor(f"wb_l1_{w}", [1024, 256], F32, kind="Internal") for w in range(2)]
    wb_l2 = [nc.dram_tensor(f"wb_l2_{w}", [512, 512], F32, kind="Internal") for w in range(2)]
    eye_h = nc.inline_tensor(np.eye(128, dtype=np.float32), "eyec")

    ctx = ExitStack()

    regA = ctx.enter_context(nc.sbuf_tensor("regA", [128, 24576], BF16))
    regB = ctx.enter_context(nc.sbuf_tensor("regB", [128, 8192], BF16))
    regC = ctx.enter_context(nc.sbuf_tensor("regC", [128, 8192], BF16))
    wtR = ctx.enter_context(nc.sbuf_tensor("wtR", [128, 2048], BF16))
    wtI = ctx.enter_context(nc.sbuf_tensor("wtI", [128, 2048], BF16))
    wtIn = ctx.enter_context(nc.sbuf_tensor("wtIn", [128, 2048], BF16))
    qfm = ctx.enter_context(nc.sbuf_tensor("qfm", [128, 16384], BF16))
    oacc = ctx.enter_context(nc.sbuf_tensor("oacc", [128, 16384], F32))
    xt = ctx.enter_context(nc.sbuf_tensor("xt", [128, 2048], F32))
    xnf = ctx.enter_context(nc.sbuf_tensor("xnf", [128, 2048], BF16))
    eye_sb = ctx.enter_context(nc.sbuf_tensor("eye_sb", [128, 128], BF16))
    stats = ctx.enter_context(nc.sbuf_tensor("stats", [128, 16], F32))
    nstats = ctx.enter_context(nc.sbuf_tensor("nstats", [1, 16], F32))
    seps = ctx.enter_context(nc.sbuf_tensor("seps", [128, 1], F32))
    negm_bc = ctx.enter_context(nc.sbuf_tensor("negm_bc", [128, 1], F32))
    L_sb = ctx.enter_context(nc.sbuf_tensor("L_sb", [128, 16], F32))
    rl_sb = ctx.enter_context(nc.sbuf_tensor("rl_sb", [128, 16], F32))
    nrm_sb = ctx.enter_context(nc.sbuf_tensor("nrm_sb", [1, 512], F32))
    ones_f = ctx.enter_context(nc.sbuf_tensor("ones_f", [1, 128], F32))
    ones_b = ctx.enter_context(nc.sbuf_tensor("ones_b", [128, 1], BF16))
    sbr = ctx.enter_context(nc.sbuf_tensor("sbr", [1, 512], F32))
    sbi = ctx.enter_context(nc.sbuf_tensor("sbi", [1, 512], F32))
    g1cm = oacc[0:4, 0:1024]
    g2s = oacc[0:4, 1024:1280]
    a_tmp = [oacc[:, 1280:1536], oacc[:, 1536:1792]]
    wbA = oacc[0:32, 2048:6144]
    wbA2 = oacc[0:32, 6144:10240]
    assert nc.sbuf_bytes_remaining >= 0

    psA = ctx.enter_context(nc.psum_tensor("psA", [128, 1024], F32))
    psL = ctx.enter_context(nc.psum_tensor("psL", [128, 512], F32))
    psX = ctx.enter_context(nc.psum_tensor("psX", [128, 512], F32))
    ptctx = ExitStack()
    pT0 = ptctx.enter_context(nc.psum_tensor("pT0", [128, 512], F32))
    pT1 = ptctx.enter_context(nc.psum_tensor("pT1", [128, 512], F32))
    ptctx.close()
    psB = ctx.enter_context(nc.psum_tensor("psB", [128, 1024], F32))
    psV10 = ctx.enter_context(nc.psum_tensor("psV10", [128, 512], F32))
    psV11 = ctx.enter_context(nc.psum_tensor("psV11", [128, 512], F32))
    ps0 = psA[:, 0:512]
    ps1 = psA[:, 512:1024]
    _sbanks = [psA[:, 0:512], psA[:, 512:1024], psB[:, 0:512], psB[:, 512:1024]]
    pvpair = (psV10, psV11)
    pT = [pT0, pT1]

    semnames = ("s_dve s_act s_wb s_wbv s_wbs s_wbd s_wl2 s_wq s_wt s_wt2 s_x s_sq s_st s_rs s_ln s_tp s_fm s_pq s_ev "
                "s_ks s_qs s_nm s_nf s_nmsb s_store s_kst s_vst s_nst s_ccn s_cck s_ccv s_nb s_mi s_kv s_smm "
                "s_exp s_pv s_add s_lrl s_x2 s_la s_vv").split()
    sems = {n: ctx.enter_context(nc.semaphore(n)) for n in semnames}
    g = lambda n: sems[n]

    blk = ctx.enter_context(nc.Block())

    def wqkvT(c):
        return regA[:, 3072 * c:3072 * (c + 1)]
    def PT(u, mc):
        return regA[:, 8192 * u + 2048 * mc:8192 * u + 2048 * (mc + 1)]
    def hfm(c):
        return regA[:, 2048 * c:2048 * (c + 1)]
    def xnfm(par, c):
        return regB[:, 4096 * par + 512 * c:4096 * par + 512 * (c + 1)]
    def kbuf(par, c):
        return regB[:, 4096 * par + 512 * c:4096 * par + 512 * (c + 1)]
    def ktile(j):
        return regC[:, 512 * j:512 * (j + 1)]
    def vtile(j):
        return regC[:, 1024 + 512 * j:1024 + 512 * (j + 1)]
    def ksq(c):
        return regC[:, 2048 + 512 * c:2048 + 512 * (c + 1)]
    def trash():
        return regC[:, 6144:7168]
    def vbuf(par, mc):
        return regC[:, 4096 * par + 1024 * mc:4096 * par + 1024 * (mc + 1)]
    def xnf_t(t_idx):
        return xnf[:, 1024 * (t_idx % 2):1024 * (t_idx % 2 + 1)]
    def xt_t(t_idx):
        return xt[:, 1024 * (t_idx % 2):1024 * (t_idx % 2 + 1)]
    def oacc_t(t_idx):
        return oacc[:, 1024 * t_idx:1024 * (t_idx + 1)]
    def st_col(par, i):
        return stats[:, 8 * par + i:8 * par + i + 1]

    WB, WBv, WBs, WBd, WL2 = {}, {}, {}, {}, {}
    for w in range(2):
        for j in range(8):
            WB[f"mm{w}_{j}"] = 8 * w + j + 1
            WBv[f"evac{w}_{j}"] = 10 * w + j + 1
            WBd[f"store{w}_{j}"] = 16 * (8 * w + j + 1)
        for h in range(2):
            WBv[f"reorder{w}_{h}"] = 10 * w + 9 + h
            WBs[f"read1_{w}_{h}"] = 80 + 32 * w + 16 * h
            WL2[f"write2_{w}_{h}"] = 16 * (6 * w + h + 1)
        WBs[f"loads{w}"] = 32 + 32 * w
        for q in range(4):
            WL2[f"read2_{w}_{q}"] = 16 * (6 * w + 3 + q)
    WB_ALL = 16
    WBV_ALL = 20
    WBD_ALL = 16 * 16
    WL2_ALL = 16 * 12
    KV_STORES = 16 * NBLK
    ST_ALL_P1 = KV_STORES + 1
    ST_ALL = ST_ALL_P1 + NTILE

    @blk.sync
    def _(sync):
        s = sync
        _qlast = [None]
        def sdma(out, in_, sem, **kw):
            if _qlast[0] is not None:
                s.wait_ge(*_qlast[0])
            ins = s.dma_start(out=out, in_=in_, **kw)
            ins.then_inc(sem, 16)
            _cnt = _qcnt.get(sem.name if hasattr(sem, 'name') else id(sem), 0) + 16
            _qcnt[sem.name if hasattr(sem, 'name') else id(sem)] = _cnt
            _qlast[0] = (sem, _cnt)
            return ins
        _qcnt = {}
        def xload(t_):
            if t_ >= 2:
                s.wait_ge(g("s_ln"), t_ - 1)
                s.wait_ge(g("s_sq"), t_ - 1)
            sdma(xt_t(t_), x_h[128 * t_:128 * (t_ + 1), :], g("s_x"))

        xload(0)
        xload(1)
        for w in range(2):
            g1h, g2h = (g1r_h, g2r_h) if w == 0 else (g1i_h, g2i_h)
            if w == 1:
                s.wait_ge(g("s_wb"), WB["mm0_7"])
            sdma(g1cm.rearrange("c (b d) -> c b d", b=32),
                 g1h.rearrange("a b c d -> c (a b) d"), g("s_wbs"))
            sdma(g2s, g2h.rearrange("c e f g -> c (e f g)"), g("s_wbs"))
        for w in range(2):
            for h in range(2):
                if (w, h) != (0, 0):
                    prevr = WBv[f"reorder{w}_{h - 1}"] if h == 1 else WBv[f"reorder{w - 1}_1"]
                    s.wait_ge(g("s_wbv"), prevr)
                s.wait_ge(g("s_wbd"), WBd[f"store{w}_{4 * h + 3}"])
                sdma(wbA.rearrange("p (b eg) -> p b eg", b=16),
                     wb_l1[w].ap()[512 * h:512 * (h + 1), :]
                     .rearrange("(b d) eg -> d b eg", b=16), g("s_wbs"))
        s.wait_ge(g("s_wq"), 32)
        for c in range(4):
            sdma(wqkvT(c), wqkv_bf.ap()[:, 128 * c:128 * (c + 1)], g("s_wt"), transpose=True)
        s.wait_ge(g("s_wq"), 48)
        for c in range(4, 8):
            sdma(wqkvT(c), wqkv_bf.ap()[:, 128 * c:128 * (c + 1)], g("s_wt"), transpose=True)
        for t_ in range(2, NTILE):
            xload(t_)
        sdma(sbr[:, :], br_h[None, :], g("s_nb"))
        sdma(sbi[:, :], bi_h[None, :], g("s_nb"))
        def kload(mb):
            par = mb % 2
            r, lb = mb // 4, mb % 4
            if mb < 4:
                s.wait_ge(g("s_cck"), lb + 1)
            if mb >= 2:
                s.wait_ge(g("s_smm"), 4 * (mb - 1))
            sdma(regB[:, 4096 * par:4096 * (par + 1)].rearrange("p (c m) -> p c m", c=8),
                 kfull[lb].ap()[1024 * r:1024 * (r + 1), :]
                 .rearrange("(c p) m -> p c m", p=128), g("s_kv"))
        s.wait_ge(g("s_pq"), 96)
        kload(0)
        kload(1)
        s.wait_ge(g("s_ccn"), 1)
        sdma(nrm_sb[:, :], nrm_out.ap()[:, :], g("s_nb"))
        for mb_ in range(2, NMB):
            kload(mb_)
        for t in range(NTILE):
            if t >= 2:
                s.wait_ge(g("s_x2"), t - 1)
            sdma(xt_t(t), x_h[128 * t:128 * (t + 1), :], g("s_x"))

    @blk.gpsimd
    def _(gp):
        _glast = [None]
        _gcnt = {}
        def gdma(out, in_, sem, **kw):
            if _glast[0] is not None:
                gp.wait_ge(*_glast[0])
            ins = gp.dma_start(out=out, in_=in_, **kw)
            ins.then_inc(sem, 16)
            k = sem.name if hasattr(sem, 'name') else id(sem)
            _gcnt[k] = _gcnt.get(k, 0) + 16
            _glast[0] = (sem, _gcnt[k])
            return ins
        gdma(eye_sb[:, :], eye_h.ap()[:, :], g("s_wq"))
        gdma(wqkv_bf.ap()[:, 0:512], wqkv_h[:, 0:512], g("s_wq"))
        gdma(wqkv_bf.ap()[:, 512:1024], wqkv_h[:, 512:1024], g("s_wq"))
        for w in range(2):
            for h in range(2):
                gp.wait_ge(g("s_wbv"), WBv[f"reorder{w}_{h}"])
                gdma(wb_l2[w].ap().rearrange("(d gg) (b e) -> d gg b e", gg=16, b=32)[:, :, 16 * h:16 * (h + 1), :],
                     wbA2.rearrange("p (gg b e) -> p gg b e", gg=16, b=16), g("s_wl2"))
            wt = wtR if w == 0 else wtI
            for q in range(4):
                gdma(wt[:, 512 * q:512 * (q + 1)], wb_l2[w].ap()[128 * q:128 * (q + 1), :], g("s_wl2"))
        rg = [list(range(NCORE))]
        for b2 in range(3):
            gp.wait_ge(g("s_kst"), 16 * 8 * (b2 + 1))
            gp.collective_compute("AllGather", ALU.bypass, replica_groups=rg,
                                  ins=[kslice[b2].ap()[:, :]], outs=[kfull[b2].ap()[:, :]]).then_inc(g("s_cck"))
        gp.wait_ge(g("s_nmsb"), 1)
        gdma(nrm_in.ap()[:, :], nrm_sb[:, :], g("s_nst"))
        gp.wait_ge(g("s_nst"), 16)
        gp.collective_compute("AllReduce", ALU.max, replica_groups=rg,
                              ins=[nrm_in.ap()[:, :]], outs=[nrm_out.ap()[:, :]]).then_inc(g("s_ccn"))
        gp.wait_ge(g("s_kst"), 16 * 8 * 4)
        gp.collective_compute("AllGather", ALU.bypass, replica_groups=rg,
                              ins=[kslice[3].ap()[:, :]], outs=[kfull[3].ap()[:, :]]).then_inc(g("s_cck"))
        for b2 in range(4):
            gp.wait_ge(g("s_vst"), 16 * 8 * (b2 + 1))
            gp.collective_compute("AllGather", ALU.bypass, replica_groups=rg,
                                  ins=[vslice[b2].ap()[:, :]], outs=[vfull[b2].ap()[:, :]]).then_inc(g("s_ccv"))
        _glast[0] = (g("s_ccv"), 1)
        gp.wait_ge(g("s_nm"), 8)
        for mb_ in range(NMB):
            par_ = mb_ % 2
            r_, lb_ = mb_ // 4, mb_ % 4
            if mb_ < 4:
                gp.wait_ge(g("s_ccv"), lb_ + 1)
            if mb_ >= 2:
                gp.wait_ge(g("s_pv"), 16 * (mb_ - 1))
            gdma(regC[:, 4096 * par_:4096 * (par_ + 1)].rearrange("p (c m) -> p c m", c=4),
                 vfull[lb_].ap()[512 * r_:512 * (r_ + 1), :]
                 .rearrange("(c p) m -> p c m", p=128), g("s_vv"))


    @blk.tensor
    def _(t):
        for w in range(2):
            t.wait_ge(g("s_wbs"), WBs[f"loads{w}"])
            for j in range(8):
                if not (w == 0 and j == 0):
                    prev = WBv[f"evac{w}_{j - 1}"] if j > 0 else WBv["evac0_7"]
                    t.wait_ge(g("s_wbv"), prev)
                t.matmul(psX[:, 0:256], g1cm[:, 128 * j:128 * (j + 1)], g2s,
                         start=True, stop=True).then_inc(g("s_wb"), 1)

        t.wait_ge(g("s_wq"), 16)

        def transpose_tile(t_idx, ph3):
            base = 128 * ph3
            for c in range(8):
                G = base + 8 * t_idx + c
                if c == 0:
                    t.wait_ge(g("s_ln"), 16 * ph3 + t_idx + 1)
                if G >= 2:
                    t.wait_ge(g("s_fm"), G - 1)
                t.matmul(pT[c % 2][:, 0:128],
                         xnf_t(t_idx)[:, 128 * c:128 * (c + 1)],
                         eye_sb[:, :], start=True, stop=True).then_inc(g("s_tp"), 1)

        _pj4 = [ps0, ps1, psV10[:, :], psV11[:, :]]
        def proj_block(b):
            par = b % 2
            gbase = 24 * b
            if b == 0:
                t.wait_ge(g("s_wt"), 128)
            t.wait_ge(g("s_fm"), 32 * (b + 1))
            grp = 0
            for ek in range(8):
                gi = gbase + grp
                if gi >= 4:
                    t.wait_ge(g("s_ev"), gi - 3)
                ps = _pj4[grp % 4]
                for c in range(8):
                    ins = t.matmul(ps[:, :], wqkvT(c)[:, 1024 + 128 * ek:1024 + 128 * (ek + 1)],
                                   xnfm(par, c), start=(c == 0), stop=(c == 7))
                    if c == 7:
                        ins.then_inc(g("s_pq"), 1)
                grp += 1
            if b < NBLK - 1:
                transpose_tile(4 * b + 4, 0)
                transpose_tile(4 * b + 5, 0)
            for ek in range(8):
                t.wait_ge(g("s_ks"), 8 * b + ek + 1)
                if ek == 0 and b >= 1:
                    t.wait_ge(g("s_nf"), b)
                ins = t.matmul(psL[0:1, 0:512], ones_b[:, :], ksq(ek),
                               start=(ek == 0), stop=(ek == 7))
                if ek == 7:
                    ins.then_inc(g("s_nm"), 1)
            for eq in range(8):
                gi = gbase + grp
                t.wait_ge(g("s_ev"), gi - 3)
                ps = _pj4[grp % 4]
                for c in range(8):
                    ins = t.matmul(ps[:, :], wqkvT(c)[:, 128 * eq:128 * (eq + 1)],
                                   xnfm(par, c), start=(c == 0), stop=(c == 7))
                    if c == 7:
                        ins.then_inc(g("s_pq"), 1)
                grp += 1
            if b < NBLK - 1:
                transpose_tile(4 * b + 6, 0)
                transpose_tile(4 * b + 7, 0)
            for i in range(8):
                nc4, h = i // 2, i % 2
                gi = gbase + grp
                t.wait_ge(g("s_ev"), gi - 3)
                ps = _pj4[grp % 4]
                for c in range(8):
                    ins = t.matmul(ps[:, :], xnfm(par, c)[:, 128 * nc4:128 * (nc4 + 1)],
                                   wqkvT(c)[:, 2048 + 512 * h:2048 + 512 * (h + 1)],
                                   start=(c == 0), stop=(c == 7))
                    if c == 7:
                        ins.then_inc(g("s_pq"), 1)
                grp += 1

        for tt in range(4):
            transpose_tile(tt, 0)
        for b in range(NBLK):
            proj_block(b)

        for ns in range(4):
            t.wait_ge(g("s_qs"), 8 * (ns + 1))
            t.wait_ge(g("s_nf"), 4 + ns)
            for c in range(8):
                ins = t.matmul(psL[0:1, 0:512], ones_b[:, :], ksq(c),
                               start=(c == 0), stop=(c == 7))
                if c == 7:
                    ins.then_inc(g("s_nm"), 1)

        t.wait_ge(g("s_mi"), 2)
        t.matmul(psX[:, 0:1], ones_f[:, :], nstats[0:1, 5:6],
                 start=True, stop=True).then_inc(g("s_mi"), 1)

        def s_part(mb, mc, cpair):
            par = mb % 2
            EG = 4 * mb + mc
            if mc == 0 and cpair == 0:
                t.wait_ge(g("s_kv"), 16 * (mb + 1))
            if EG == 0 and cpair == 0:
                t.wait_ge(g("s_ev"), 96)
            if cpair == 0 and EG >= 1:
                t.wait_ge(g("s_exp"), 2 * EG)
            for c in (2 * cpair, 2 * cpair + 1):
                lhs = kbuf(par, c)[:, 128 * mc:128 * (mc + 1)]
                for sqs in range(4):
                    ins = t.matmul(_sbanks[sqs], lhs,
                                   qfm[:, 2048 * c + 512 * sqs:2048 * c + 512 * (sqs + 1)],
                                   start=(c == 0), stop=(c == 7))
                    if c == 7 and sqs == 3:
                        ins.then_inc(g("s_smm"), 1)

        def pv_group(mb, qc, tail=False):
            idx = 16 * mb + qc
            pr = (psB[:, 0:512], psB[:, 512:1024]) if (tail and idx % 2 == 1) else \
                 (pvpair[0][:, :], pvpair[1][:, :])
            if qc == 0:
                t.wait_ge(g("s_vv"), 16 * (mb + 1))
                t.wait_ge(g("s_exp"), 8 * (mb + 1))
            if idx == 0:
                t.wait_ge(g("s_mi"), 4)
            if idx >= 1:
                t.wait_ge(g("s_la"), idx)
                t.wait_ge(g("s_add"), idx - 1 if (tail and idx > 16 * (NMB - 2)) else idx)
            for sub in range(4):
                lhs = PT(mb % 3, sub)[:, 128 * qc:128 * (qc + 1)]
                t.matmul(pr[0], lhs, vbuf(mb % 2, sub)[:, 0:512],
                         start=(sub == 0), stop=(sub == 3))
                t.matmul(psX[:, 0:1], lhs, ones_b[:, :],
                         start=(sub == 0), stop=(sub == 3))
                ins = t.matmul(pr[1], lhs, vbuf(mb % 2, sub)[:, 512:1024],
                               start=(sub == 0), stop=(sub == 3))
                if sub == 3:
                    ins.then_inc(g("s_pv"), 1)

        for mb in range(NMB):
            for u in range(16):
                if mb >= 2:
                    pv_group(mb - 2, u)
                s_part(mb, u // 4, u % 4)
        t.wait_ge(g("s_exp"), 8 * NMB)
        for mb in (NMB - 2, NMB - 1):
            for qc in range(16):
                pv_group(mb, qc, tail=True)

        def ffn_group(qc):
            t.wait_ge(g("s_fm"), 128 + 8 * (qc + 1))
            t.wait_ge(g("s_nb"), 32)
            if qc >= 2:
                t.wait_ge(g("s_ev"), 96 + 2 * (qc - 1))
            pair = (ps0, ps1) if qc % 2 == 0 else (psL, psX)
            for half in range(2):
                ps = pair[half]
                wa, wb_ = (wtR, wtIn) if half == 0 else (wtI, wtR)
                bias = sbr if half == 0 else sbi
                for c in range(4):
                    t.matmul(ps[:, :], hfm(c)[:, 128 * qc:128 * (qc + 1)],
                             wa[:, 512 * c:512 * (c + 1)], start=(c == 0), stop=False)
                for c in range(4):
                    t.matmul(ps[:, :], hfm(4 + c)[:, 128 * qc:128 * (qc + 1)],
                             wb_[:, 512 * c:512 * (c + 1)], start=False, stop=False)
                t.matmul(ps[:, :], ones_f[:, :], bias[:, :],
                         start=False, stop=True).then_inc(g("s_pq"), 1)

        transpose_tile(0, 1)
        transpose_tile(1, 1)
        for qc in range(NTILE - 2):
            transpose_tile(qc + 2, 1)
            ffn_group(qc)
        ffn_group(NTILE - 2)
        ffn_group(NTILE - 1)

    @blk.scalar
    def _(sc):
        _acnt = [0]
        def amark(ins):
            ins.then_inc(g("s_act"), 1)
            _acnt[0] += 1
        def ajoin():
            sc.wait_ge(g("s_act"), _acnt[0])
        for w in range(2):
            for j in range(8):
                sc.wait_ge(g("s_wbv"), WBv[f"evac{w}_{j}"])
                if 8 * w + j >= 1:
                    sc.wait_ge(g("s_wbd"), 16 * (8 * w + j))
                sc.dma_start(out=wb_l1[w].ap()[128 * j:128 * (j + 1), :],
                             in_=a_tmp[j % 2]).then_inc(g("s_wbd"), 16)

        def ln_stats(t_idx, ph3, src_ap):
            par = t_idx % 2
            off = 16 * ph3
            if ph3:
                sc.wait_ge(g("s_x2"), t_idx + 1)
            else:
                sc.wait_ge(g("s_x"), 16 * (t_idx + 1))
            sc.activation(out=trash(), in_=src_ap, func=AF.Square,
                          accum_out=st_col(par, 4)).then_inc(g("s_sq"), 1)
            sc.wait_ge(g("s_st"), off + t_idx + 1)
            sc.wait_ge(g("s_sq"), off + t_idx + 1)
            amark(sc.activation(out=st_col(par, 5), in_=st_col(par, 4), func=AF.Ln,
                                scale=1.0 / D, bias=st_col(par, 3)))
            ajoin()
            sc.activation(out=st_col(par, 6), in_=st_col(par, 5), func=AF.Exp,
                          scale=-0.5).then_inc(g("s_rs"), 1)

        _aj4 = [ps0, ps1, psV10[:, :], psV11[:, :]]
        def evac_block(b, part=None):
            gbase = 24 * b
            ek_r = {None: range(8), 0: range(4), 1: range(4, 8)}.get(part, [])
            eq_r = range(8) if part in (None, 2) else []
            v_r = range(8) if part in (None, 3) else []
            for ek in ek_r:
                sc.wait_ge(g("s_pq"), gbase + ek + 1)
                kk = 8 * b + ek
                if kk >= 2:
                    sc.wait_ge(g("s_kst"), 16 * kk)
                if ek == 0 and b >= 1:
                    sc.wait_ge(g("s_nm"), b)
                ps = _aj4[ek % 4]
                sc.activation(out=ksq(ek), in_=ps[:, :], func=AF.Square).then_inc(g("s_ks"), 1)
                sc.activation(out=ktile(ek % 2), in_=ps[:, :], func=AF.Copy).then_inc(g("s_ev"), 1)
                sc.wait_ge(g("s_ev"), gbase + ek + 1)
                sc.dma_start(out=kslice[b].ap()[128 * ek:128 * (ek + 1), :],
                             in_=ktile(ek % 2)).then_inc(g("s_kst"), 16)
            for eq in eq_r:
                sc.wait_ge(g("s_pq"), gbase + 8 + eq + 1)
                ps = _aj4[eq % 4]
                sc.activation(out=qfm[:, 2048 * eq + 512 * b:2048 * eq + 512 * (b + 1)],
                              in_=ps[:, :], func=AF.Copy, scale=QSCALE).then_inc(g("s_ev"), 1)
            for i in v_r:
                sc.wait_ge(g("s_pq"), gbase + 16 + i + 1)
                vv = 8 * b + i
                if vv >= 2:
                    sc.wait_ge(g("s_vst"), 16 * vv)
                nc4, h = i // 2, i % 2
                ps = _aj4[i % 4]
                sc.activation(out=vtile(i % 2), in_=ps[:, :], func=AF.Copy).then_inc(g("s_ev"), 1)
                sc.wait_ge(g("s_ev"), gbase + 16 + i + 1)
                sc.dma_start(out=vslice[b].ap()[128 * nc4:128 * (nc4 + 1),
                                                512 * h:512 * (h + 1)],
                             in_=vtile(i % 2)).then_inc(g("s_vst"), 16)

        for b in range(NBLK):
            for i, tt in enumerate(range(4 * b, 4 * b + 4)):
                ln_stats(tt, 0, xt_t(tt))
                if b >= 1:
                    evac_block(b - 1, part=i)
        evac_block(NBLK - 1)

        for ns in range(4):
            sc.wait_ge(g("s_nm"), 4 + ns)
            for c in range(8):
                sc.activation(out=ksq(c), in_=qfm[:, 2048 * c + 512 * ns:2048 * c + 512 * (ns + 1)],
                              func=AF.Square).then_inc(g("s_qs"), 1)

        sc.wait_ge(g("s_mi"), 1)
        amark(sc.activation(out=nstats[0:1, 4:5], in_=nstats[0:1, 3:4], func=AF.Sqrt))
        ajoin()
        sc.activation(out=nstats[0:1, 5:6], in_=nstats[0:1, 4:5], func=AF.Copy,
                      scale=-1.0).then_inc(g("s_mi"), 1)

        ecount = 0
        for mb in range(NMB):
            u = mb % 3
            if mb >= 3:
                sc.wait_ge(g("s_pv"), 16 * (mb - 2))
            for mc in range(4):
                sc.wait_ge(g("s_smm"), ecount + 1)
                if ecount == 0:
                    sc.wait_ge(g("s_mi"), 4)
                sc.activation(out=PT(u, mc)[:, 0:1024], in_=psA[:, :], func=AF.Exp,
                              bias=negm_bc[:, :]).then_inc(g("s_exp"), 1)
                sc.activation(out=PT(u, mc)[:, 1024:2048], in_=psB[:, :], func=AF.Exp,
                              bias=negm_bc[:, :]).then_inc(g("s_exp"), 1)
                ecount += 1

        for tt in range(NTILE):
            ln_stats(tt, 1, oacc_t(tt))
        for qc in range(NTILE):
            sc.wait_ge(g("s_ev"), 96 + 2 * (qc + 1))
            sc.dma_start(out=out_h[128 * qc:128 * (qc + 1), :],
                         in_=oacc_t(qc)).then_inc(g("s_store"), 16)
        sc.wait_ge(g("s_store"), 16 * NTILE)

    @blk.vector
    def _(v):
        _dcnt = [0]
        def dmark(ins):
            ins.then_inc(g("s_dve"), 1)
            _dcnt[0] += 1
        def djoin():
            v.wait_ge(g("s_dve"), _dcnt[0])
        v.memset(ones_f[:, :], 1.0)
        v.memset(ones_b[:, :], 1.0)
        v.memset(seps[:, :], EPS)
        for w in range(2):
            for j in range(8):
                v.wait_ge(g("s_wb"), WB[f"mm{w}_{j}"])
                if (w, j) > (0, 1):
                    v.wait_ge(g("s_wbd"), 16 * (8 * w + j - 1))
                v.tensor_copy(out=a_tmp[j % 2], in_=psX[:, 0:256]).then_inc(g("s_wbv"), 1)
            for h in range(2):
                v.wait_ge(g("s_wbs"), WBs[f"read1_{w}_{h}"])
                if not (w == 0 and h == 0):
                    v.wait_ge(g("s_wl2"), WL2[f"write2_{w}_{h - 1}"] if h == 1
                              else WL2["write2_0_1"])
                v.tensor_copy(out=wbA2.rearrange("p (gg b e) -> p gg b e", gg=16, b=16),
                              in_=wbA.rearrange("p (b e gg) -> p gg b e", b=16, e=16)) \
                    .then_inc(g("s_wbv"), 1)
        v.wait_ge(g("s_wl2"), WL2_ALL)
        ins = None
        for q in range(4):
            ins = v.tensor_scalar_mul(out=wtIn[:, 512 * q:512 * (q + 1)],
                                      in0=wtI[:, 512 * q:512 * (q + 1)], scalar1=-1.0)
        ins.then_inc(g("s_wbv"), 1)

        def ln_vec(t_idx, ph3, src_ap):
            par = t_idx % 2
            off = 16 * ph3
            if not ph3:
                v.wait_ge(g("s_x"), 16 * (t_idx + 1))
            dmark(v.reduce_sum(st_col(par, 0), src_ap, axis=AX.X))
            djoin()
            dmark(v.tensor_scalar_mul(out=st_col(par, 1), in0=st_col(par, 0), scalar1=1.0 / D))
            djoin()
            dmark(v.scalar_tensor_tensor(out=st_col(par, 2), in0=st_col(par, 1), scalar=-1.0,
                                         in1=st_col(par, 1), op0=ALU.mult, op1=ALU.mult))
            djoin()
            v.tensor_scalar_add(out=st_col(par, 3), in0=st_col(par, 2),
                                scalar1=EPS).then_inc(g("s_st"), 1)
            v.wait_ge(g("s_rs"), off + t_idx + 1)
            dmark(v.scalar_tensor_tensor(out=st_col(par, 7), in0=st_col(par, 1), scalar=-1.0,
                                         in1=st_col(par, 6), op0=ALU.mult, op1=ALU.mult))
            if t_idx >= 2:
                v.wait_ge(g("s_tp"), 128 * ph3 + 8 * (t_idx - 1))
            djoin()
            v.tensor_scalar(out=xnf_t(t_idx), in0=src_ap,
                            scalar1=st_col(par, 6), scalar2=st_col(par, 7),
                            op0=ALU.mult, op1=ALU.add).then_inc(g("s_ln"), 1)

        def evac_tile(t_idx, ph3):
            for c in range(8):
                G = 128 * ph3 + 8 * t_idx + c
                v.wait_ge(g("s_tp"), G + 1)
                if (not ph3) and t_idx % 4 == 0 and c == 0 and t_idx // 4 >= 2:
                    v.wait_ge(g("s_pq"), 24 * (t_idx // 4 - 1))
                if ph3:
                    dst = regA[:, 2048 * c + 128 * t_idx:2048 * c + 128 * (t_idx + 1)]
                else:
                    par = (t_idx // 4) % 2
                    dst = regB[:, 4096 * par + 512 * c + 128 * (t_idx % 4):
                               4096 * par + 512 * c + 128 * (t_idx % 4) + 128]
                v.tensor_copy(out=dst, in_=pT[c % 2][:, 0:128]).then_inc(g("s_fm"), 1)

        def knorm_read(b):
            v.wait_ge(g("s_nm"), b + 1)
            if b == 0:
                v.reduce_max(nstats[0:1, 0:1], psL[0:1, 0:512], axis=AX.X).then_inc(g("s_nf"), 1)
            else:
                dmark(v.reduce_max(nstats[0:1, 2:3], psL[0:1, 0:512], axis=AX.X))
                djoin()
                v.tensor_max(out=nstats[0:1, 0:1], in0=nstats[0:1, 0:1],
                             in1=nstats[0:1, 2:3]).then_inc(g("s_nf"), 1)

        for tt in range(NTILE):
            ln_vec(tt, 0, xt_t(tt))
            if tt == 10:
                knorm_read(0)
            evac_tile(tt, 0)
            if tt == 13:
                knorm_read(1)
        knorm_read(NBLK - 2)
        knorm_read(NBLK - 1)
        djoin()
        dmark(v.memset(nrm_sb[:, :], 0.0))
        djoin()
        v.tensor_scalar_add(out=nrm_sb[:, :], in0=nrm_sb[:, :],
                            scalar1=nstats[0:1, 0:1]).then_inc(g("s_nmsb"), 1)
        for ns in range(4):
            v.wait_ge(g("s_nm"), 5 + ns)
            if ns == 0:
                v.reduce_max(nstats[0:1, 1:2], psL[0:1, 0:512], axis=AX.X).then_inc(g("s_nf"), 1)
            else:
                dmark(v.reduce_max(nstats[0:1, 2:3], psL[0:1, 0:512], axis=AX.X))
                djoin()
                v.tensor_max(out=nstats[0:1, 1:2], in0=nstats[0:1, 1:2],
                             in1=nstats[0:1, 2:3]).then_inc(g("s_nf"), 1)
        v.wait_ge(g("s_nb"), 48)
        v.wait_ge(g("s_nf"), 8)
        dmark(v.reduce_max(nstats[0:1, 2:3], nrm_sb[0:1, :], axis=AX.X))
        djoin()
        v.tensor_mul(out=nstats[0:1, 3:4], in0=nstats[0:1, 2:3],
                     in1=nstats[0:1, 1:2]).then_inc(g("s_mi"), 1)
        v.wait_ge(g("s_mi"), 3)
        v.tensor_copy(out=negm_bc[:, :], in_=psX[:, 0:1]).then_inc(g("s_mi"), 1)

        v.wait_ge(g("s_wbd"), WBD_ALL)
        v.wait_ge(g("s_wl2"), WL2_ALL)
        for mb in range(NMB):
            for qc in range(16):
                idx = 16 * mb + qc
                tail = mb >= NMB - 2
                pva, pvb = (psB[:, 0:512], psB[:, 512:1024]) if (tail and idx % 2 == 1) else \
                           (pvpair[0][:, :], pvpair[1][:, :])
                v.wait_ge(g("s_pv"), idx + 1)
                if mb == 0:
                    v.tensor_copy(out=L_sb[:, qc:qc + 1], in_=psX[:, 0:1]).then_inc(g("s_la"), 1)
                    v.tensor_copy(out=oacc[:, 1024 * qc:1024 * qc + 512], in_=pva)
                    v.tensor_copy(out=oacc[:, 1024 * qc + 512:1024 * (qc + 1)],
                                  in_=pvb).then_inc(g("s_add"), 1)
                else:
                    v.tensor_add(out=L_sb[:, qc:qc + 1], in0=psX[:, 0:1],
                                 in1=L_sb[:, qc:qc + 1]).then_inc(g("s_la"), 1)
                    v.tensor_add(out=oacc[:, 1024 * qc:1024 * qc + 512],
                                 in0=pva, in1=oacc[:, 1024 * qc:1024 * qc + 512])
                    v.tensor_add(out=oacc[:, 1024 * qc + 512:1024 * (qc + 1)],
                                 in0=pvb,
                                 in1=oacc[:, 1024 * qc + 512:1024 * (qc + 1)]).then_inc(g("s_add"), 1)
        v.wait_ge(g("s_la"), NMB * 16)
        v.reciprocal(out=rl_sb[:, :], in_=L_sb[:, :]).then_inc(g("s_lrl"), 1)

        def ffn_evac(qc):
            v.wait_ge(g("s_pq"), 96 + 2 * (qc + 1))
            pair = (ps0, ps1) if qc % 2 == 0 else (psL, psX)
            v.tensor_add(out=oacc[:, 1024 * qc:1024 * qc + 512],
                         in0=pair[0][:, :], in1=oacc[:, 1024 * qc:1024 * qc + 512])
            v.tensor_add(out=oacc[:, 1024 * qc + 512:1024 * (qc + 1)],
                         in0=pair[1][:, :],
                         in1=oacc[:, 1024 * qc + 512:1024 * (qc + 1)]).then_inc(g("s_ev"), 2)

        for tt in range(NTILE):
            v.wait_ge(g("s_x"), 256 + 16 * (tt + 1))
            if tt == 0:
                v.wait_ge(g("s_lrl"), 1)
            v.scalar_tensor_tensor(out=oacc_t(tt), in0=oacc_t(tt),
                                   scalar=rl_sb[:, tt:tt + 1], in1=xt_t(tt),
                                   op0=ALU.mult, op1=ALU.add).then_inc(g("s_x2"), 1)
            if tt >= 1:
                evac_tile(tt - 1, 1)
            v.wait_ge(g("s_x2"), tt + 1)
            ln_vec(tt, 1, oacc_t(tt))
            if tt >= 3:
                ffn_evac(tt - 3)
        evac_tile(NTILE - 1, 1)
        ffn_evac(NTILE - 3)
        ffn_evac(NTILE - 2)
        ffn_evac(NTILE - 1)

    return nc, ctx


_cache = {}


def kernel(L0_attn_x, L0_attn_W_qkv, L0_ffn_G1_R, L0_ffn_G2_R, L0_ffn_G1_I,
           L0_ffn_G2_I, L0_ffn_bias_real, L0_ffn_bias_imag):
    x2d = np.ascontiguousarray(np.asarray(L0_attn_x, dtype=np.float32).reshape(NTOK, D))
    wqkv = np.ascontiguousarray(np.asarray(L0_attn_W_qkv, dtype=np.float32))
    g1r = np.ascontiguousarray(np.asarray(L0_ffn_G1_R, dtype=np.float32))
    g2r = np.ascontiguousarray(np.asarray(L0_ffn_G2_R, dtype=np.float32))
    g1i = np.ascontiguousarray(np.asarray(L0_ffn_G1_I, dtype=np.float32))
    g2i = np.ascontiguousarray(np.asarray(L0_ffn_G2_I, dtype=np.float32))
    br = np.ascontiguousarray(np.asarray(L0_ffn_bias_real, dtype=np.float32))
    bi = np.ascontiguousarray(np.asarray(L0_ffn_bias_imag, dtype=np.float32))

    if "nc" not in _cache:
        _cache["nc"] = build_nc()
    nc, _ctx = _cache["nc"]

    in_maps = [dict(x=x2d[NT * c:NT * (c + 1)], wqkv=wqkv, g1r=g1r, g2r=g2r,
                    g1i=g1i, g2i=g2i, br=br, bi=bi) for c in range(NCORE)]
    res = run_bass_kernel_spmd(nc, in_maps, core_ids=list(range(NCORE)))
    out = np.concatenate([res.results[c]["out"] for c in range(NCORE)], axis=0)
    return out.reshape(1, NTOK, D)


# version 18
# speedup vs baseline: 1.9404x; 1.0344x over previous
import numpy as np
from contextlib import ExitStack

import concourse.bass as bass
import concourse.mybir as mybir
from concourse.bass_utils import run_bass_kernel_spmd

F32 = mybir.dt.float32
BF16 = mybir.dt.bfloat16
AF = mybir.ActivationFunctionType
ALU = mybir.AluOpType
AX = mybir.AxisListType

NCORE = 8
NTOK = 16384
D = 1024
NT = NTOK // NCORE
NTILE = NT // 128
NBLK = NT // 512
MB = 512
NMB = NTOK // MB
NPAIR = NMB // 2
EPS = 1e-5
QSCALE = 1.0 / 32.0


def build_nc():
    nc = bass.Bass("TRN2", target_bir_lowering=False)

    x_h = nc.dram_tensor("x", [NT, D], F32, kind="ExternalInput").ap()
    wqkv_h = nc.dram_tensor("wqkv", [3 * D, D], F32, kind="ExternalInput").ap()
    g1r_h = nc.dram_tensor("g1r", [1, 32, 4, 32], F32, kind="ExternalInput").ap()
    g2r_h = nc.dram_tensor("g2r", [4, 16, 1, 16], F32, kind="ExternalInput").ap()
    g1i_h = nc.dram_tensor("g1i", [1, 32, 4, 32], F32, kind="ExternalInput").ap()
    g2i_h = nc.dram_tensor("g2i", [4, 16, 1, 16], F32, kind="ExternalInput").ap()
    br_h = nc.dram_tensor("br", [512], F32, kind="ExternalInput").ap()
    bi_h = nc.dram_tensor("bi", [512], F32, kind="ExternalInput").ap()
    out_h = nc.dram_tensor("out", [NT, D], F32, kind="ExternalOutput").ap()

    wqkv_bf = nc.dram_tensor("wqkv_bf", [3 * D, D], BF16, kind="Internal")
    kslice = [nc.dram_tensor(f"kslice{j}", [D, 512], BF16, kind="Internal") for j in range(4)]
    vslice = [nc.dram_tensor(f"vslice{j}", [512, D], BF16, kind="Internal") for j in range(4)]
    kfull = [nc.dram_tensor(f"kfull{j}", [NCORE * D, 512], BF16, kind="Internal",
                            addr_space="Shared") for j in range(4)]
    vfull = [nc.dram_tensor(f"vfull{j}", [NCORE * 512, D], BF16, kind="Internal",
                            addr_space="Shared") for j in range(4)]
    nrm_in = nc.dram_tensor("nrm_in", [1, 512], F32, kind="Internal")
    nrm_out = nc.dram_tensor("nrm_out", [1, 512], F32, kind="Internal")
    wb_l1 = [nc.dram_tensor(f"wb_l1_{w}", [1024, 256], F32, kind="Internal") for w in range(2)]
    wb_l2 = [nc.dram_tensor(f"wb_l2_{w}", [512, 512], F32, kind="Internal") for w in range(2)]
    eye_h = nc.inline_tensor(np.eye(128, dtype=np.float32), "eyec")

    ctx = ExitStack()

    regA = ctx.enter_context(nc.sbuf_tensor("regA", [128, 24576], BF16))
    regB = ctx.enter_context(nc.sbuf_tensor("regB", [128, 8192], BF16))
    regC = ctx.enter_context(nc.sbuf_tensor("regC", [128, 8192], BF16))
    wtR = ctx.enter_context(nc.sbuf_tensor("wtR", [128, 2048], BF16))
    wtI = ctx.enter_context(nc.sbuf_tensor("wtI", [128, 2048], BF16))
    wtIn = ctx.enter_context(nc.sbuf_tensor("wtIn", [128, 2048], BF16))
    qfm = ctx.enter_context(nc.sbuf_tensor("qfm", [128, 16384], BF16))
    oacc = ctx.enter_context(nc.sbuf_tensor("oacc", [128, 16384], F32))
    xt = ctx.enter_context(nc.sbuf_tensor("xt", [128, 2048], F32))
    xnf = ctx.enter_context(nc.sbuf_tensor("xnf", [128, 2048], BF16))
    eye_sb = ctx.enter_context(nc.sbuf_tensor("eye_sb", [128, 128], BF16))
    stats = ctx.enter_context(nc.sbuf_tensor("stats", [128, 16], F32))
    nstats = ctx.enter_context(nc.sbuf_tensor("nstats", [1, 16], F32))
    seps = ctx.enter_context(nc.sbuf_tensor("seps", [128, 1], F32))
    negm_bc = ctx.enter_context(nc.sbuf_tensor("negm_bc", [128, 1], F32))
    L_sb = ctx.enter_context(nc.sbuf_tensor("L_sb", [128, 16], F32))
    rl_sb = ctx.enter_context(nc.sbuf_tensor("rl_sb", [128, 16], F32))
    nrm_sb = ctx.enter_context(nc.sbuf_tensor("nrm_sb", [1, 512], F32))
    ones_f = ctx.enter_context(nc.sbuf_tensor("ones_f", [1, 128], F32))
    ones_b = ctx.enter_context(nc.sbuf_tensor("ones_b", [128, 1], BF16))
    sbr = ctx.enter_context(nc.sbuf_tensor("sbr", [1, 512], F32))
    sbi = ctx.enter_context(nc.sbuf_tensor("sbi", [1, 512], F32))
    g1cm = oacc[0:4, 0:1024]
    g2s = oacc[0:4, 1024:1280]
    a_tmp = [oacc[:, 1280:1536], oacc[:, 1536:1792]]
    wbA = oacc[0:32, 2048:6144]
    wbA2 = oacc[0:32, 6144:10240]
    assert nc.sbuf_bytes_remaining >= 0

    psA = ctx.enter_context(nc.psum_tensor("psA", [128, 1024], F32))
    psL = ctx.enter_context(nc.psum_tensor("psL", [128, 512], F32))
    psX = ctx.enter_context(nc.psum_tensor("psX", [128, 512], F32))
    ptctx = ExitStack()
    pT0 = ptctx.enter_context(nc.psum_tensor("pT0", [128, 512], F32))
    pT1 = ptctx.enter_context(nc.psum_tensor("pT1", [128, 512], F32))
    ptctx.close()
    psB = ctx.enter_context(nc.psum_tensor("psB", [128, 1024], F32))
    psV10 = ctx.enter_context(nc.psum_tensor("psV10", [128, 512], F32))
    psV11 = ctx.enter_context(nc.psum_tensor("psV11", [128, 512], F32))
    ps0 = psA[:, 0:512]
    ps1 = psA[:, 512:1024]
    _sbanks = [psA[:, 0:512], psA[:, 512:1024], psB[:, 0:512], psB[:, 512:1024]]
    pvpair = (psV10, psV11)
    pT = [pT0, pT1]

    semnames = ("s_dve s_act s_wb s_wbv s_wbs s_wbd s_wl2 s_wq s_wt s_wt2 s_x s_sq s_st s_rs s_ln s_tp s_fm s_pq s_ev "
                "s_ks s_qs s_nm s_nf s_nmsb s_store s_kst s_kst2 s_vst s_vst2 s_nst s_ccn s_cck s_ccv s_nb s_mi s_kv s_smm "
                "s_exp s_pv s_add s_lrl s_x2 s_la s_vv s_bb").split()
    sems = {n: ctx.enter_context(nc.semaphore(n)) for n in semnames}
    g = lambda n: sems[n]

    blk = ctx.enter_context(nc.Block())

    def wqkvT(c):
        return regA[:, 3072 * c:3072 * (c + 1)]
    def PT(u, mc):
        return regA[:, 8192 * u + 2048 * mc:8192 * u + 2048 * (mc + 1)]
    def hfm(c):
        return regA[:, 2048 * c:2048 * (c + 1)]
    def xnfm(par, c):
        return regB[:, 4096 * par + 512 * c:4096 * par + 512 * (c + 1)]
    def kbuf(par, c):
        return regB[:, 4096 * par + 512 * c:4096 * par + 512 * (c + 1)]
    def ktile(j):
        return regC[:, 512 * j:512 * (j + 1)]
    def vtile(j):
        return regC[:, 1024 + 512 * j:1024 + 512 * (j + 1)]
    def ksq(c):
        return regC[:, 2048 + 512 * c:2048 + 512 * (c + 1)]
    def trash():
        return regC[:, 6144:7168]
    def vbuf(par, mc):
        return regC[:, 4096 * par + 1024 * mc:4096 * par + 1024 * (mc + 1)]
    def xnf_t(t_idx):
        return xnf[:, 1024 * (t_idx % 2):1024 * (t_idx % 2 + 1)]
    def xt_t(t_idx):
        return xt[:, 1024 * (t_idx % 2):1024 * (t_idx % 2 + 1)]
    def oacc_t(t_idx):
        return oacc[:, 1024 * t_idx:1024 * (t_idx + 1)]
    def st_col(par, i):
        return stats[:, 8 * par + i:8 * par + i + 1]

    WB, WBv, WBs, WBd, WL2 = {}, {}, {}, {}, {}
    for w in range(2):
        for j in range(8):
            WB[f"mm{w}_{j}"] = 8 * w + j + 1
            WBv[f"evac{w}_{j}"] = 10 * w + j + 1
            WBd[f"store{w}_{j}"] = 16 * (8 * w + j + 1)
        for h in range(2):
            WBv[f"reorder{w}_{h}"] = 10 * w + 9 + h
            WBs[f"read1_{w}_{h}"] = 80 + 32 * w + 16 * h
            WL2[f"write2_{w}_{h}"] = 16 * (6 * w + h + 1)
        WBs[f"loads{w}"] = 32 + 32 * w
        for q in range(4):
            WL2[f"read2_{w}_{q}"] = 16 * (6 * w + 3 + q)
    WB_ALL = 16
    WBV_ALL = 20
    WBD_ALL = 16 * 16
    WL2_ALL = 16 * 12
    KV_STORES = 16 * NBLK
    ST_ALL_P1 = KV_STORES + 1
    ST_ALL = ST_ALL_P1 + NTILE

    @blk.sync
    def _(sync):
        s = sync
        _qlast = [None]
        def sdma(out, in_, sem, **kw):
            if _qlast[0] is not None:
                s.wait_ge(*_qlast[0])
            ins = s.dma_start(out=out, in_=in_, **kw)
            ins.then_inc(sem, 16)
            _cnt = _qcnt.get(sem.name if hasattr(sem, 'name') else id(sem), 0) + 16
            _qcnt[sem.name if hasattr(sem, 'name') else id(sem)] = _cnt
            _qlast[0] = (sem, _cnt)
            return ins
        _qcnt = {}
        def xload(t_):
            if t_ >= 2:
                s.wait_ge(g("s_ln"), t_ - 1)
                s.wait_ge(g("s_sq"), t_ - 1)
            sdma(xt_t(t_), x_h[128 * t_:128 * (t_ + 1), :], g("s_x"))

        xload(0)
        xload(1)
        for w in range(2):
            g1h, g2h = (g1r_h, g2r_h) if w == 0 else (g1i_h, g2i_h)
            if w == 1:
                s.wait_ge(g("s_wb"), WB["mm0_7"])
            sdma(g1cm.rearrange("c (b d) -> c b d", b=32),
                 g1h.rearrange("a b c d -> c (a b) d"), g("s_wbs"))
            sdma(g2s, g2h.rearrange("c e f g -> c (e f g)"), g("s_wbs"))
        for w in range(2):
            for h in range(2):
                if (w, h) != (0, 0):
                    prevr = WBv[f"reorder{w}_{h - 1}"] if h == 1 else WBv[f"reorder{w - 1}_1"]
                    s.wait_ge(g("s_wbv"), prevr)
                s.wait_ge(g("s_wbd"), WBd[f"store{w}_{4 * h + 3}"])
                sdma(wbA.rearrange("p (b eg) -> p b eg", b=16),
                     wb_l1[w].ap()[512 * h:512 * (h + 1), :]
                     .rearrange("(b d) eg -> d b eg", b=16), g("s_wbs"))
        s.wait_ge(g("s_wq"), 32)
        for c in range(4):
            sdma(wqkvT(c), wqkv_bf.ap()[:, 128 * c:128 * (c + 1)], g("s_wt"), transpose=True)
        s.wait_ge(g("s_wq"), 48)
        for c in range(4, 8):
            sdma(wqkvT(c), wqkv_bf.ap()[:, 128 * c:128 * (c + 1)], g("s_wt"), transpose=True)
        for t_ in range(2, NTILE):
            xload(t_)
        sdma(sbr[:, :], br_h[None, :], g("s_nb"))
        sdma(sbi[:, :], bi_h[None, :], g("s_nb"))
        def kload(mb):
            par = mb % 2
            r, lb = mb // 4, mb % 4
            if mb < 4:
                s.wait_ge(g("s_cck"), lb + 1)
            if mb >= 2:
                s.wait_ge(g("s_smm"), 4 * (mb - 1))
            sdma(regB[:, 4096 * par:4096 * (par + 1)].rearrange("p (c m) -> p c m", c=8),
                 kfull[lb].ap()[1024 * r:1024 * (r + 1), :]
                 .rearrange("(c p) m -> p c m", p=128), g("s_kv"))
        s.wait_ge(g("s_pq"), 96)
        kload(0)
        kload(1)
        s.wait_ge(g("s_ccn"), 1)
        sdma(nrm_sb[:, :], nrm_out.ap()[:, :], g("s_nb"))
        for mb_ in range(2, NMB):
            kload(mb_)
        for t in range(NTILE):
            if t >= 2:
                s.wait_ge(g("s_x2"), t - 1)
            sdma(xt_t(t), x_h[128 * t:128 * (t + 1), :], g("s_x"))

    @blk.gpsimd
    def _(gp):
        _glast = [None]
        _gcnt = {}
        def gdma(out, in_, sem, **kw):
            if _glast[0] is not None:
                gp.wait_ge(*_glast[0])
            ins = gp.dma_start(out=out, in_=in_, **kw)
            ins.then_inc(sem, 16)
            k = sem.name if hasattr(sem, 'name') else id(sem)
            _gcnt[k] = _gcnt.get(k, 0) + 16
            _glast[0] = (sem, _gcnt[k])
            return ins
        gdma(eye_sb[:, :], eye_h.ap()[:, :], g("s_wq"))
        gdma(wqkv_bf.ap()[:, 0:512], wqkv_h[:, 0:512], g("s_wq"))
        gdma(wqkv_bf.ap()[:, 512:1024], wqkv_h[:, 512:1024], g("s_wq"))
        for w in range(2):
            for h in range(2):
                gp.wait_ge(g("s_wbv"), WBv[f"reorder{w}_{h}"])
                gdma(wb_l2[w].ap().rearrange("(d gg) (b e) -> d gg b e", gg=16, b=32)[:, :, 16 * h:16 * (h + 1), :],
                     wbA2.rearrange("p (gg b e) -> p gg b e", gg=16, b=16), g("s_wl2"))
            wt = wtR if w == 0 else wtI
            for q in range(4):
                gdma(wt[:, 512 * q:512 * (q + 1)], wb_l2[w].ap()[128 * q:128 * (q + 1), :], g("s_wl2"))
        rg = [list(range(NCORE))]
        for b2 in range(3):
            gp.wait_ge(g("s_kst"), 16 * 4 * (b2 + 1))
            gp.wait_ge(g("s_kst2"), 16 * 4 * (b2 + 1))
            gp.collective_compute("AllGather", ALU.bypass, replica_groups=rg,
                                  ins=[kslice[b2].ap()[:, :]], outs=[kfull[b2].ap()[:, :]]).then_inc(g("s_cck"))
        gp.wait_ge(g("s_nmsb"), 1)
        gdma(nrm_in.ap()[:, :], nrm_sb[:, :], g("s_nst"))
        gp.wait_ge(g("s_nst"), 16)
        gp.collective_compute("AllReduce", ALU.max, replica_groups=rg,
                              ins=[nrm_in.ap()[:, :]], outs=[nrm_out.ap()[:, :]]).then_inc(g("s_ccn"))
        gp.wait_ge(g("s_kst"), 16 * 4 * 4)
        gp.wait_ge(g("s_kst2"), 16 * 4 * 4)
        gp.collective_compute("AllGather", ALU.bypass, replica_groups=rg,
                              ins=[kslice[3].ap()[:, :]], outs=[kfull[3].ap()[:, :]]).then_inc(g("s_cck"))
        for b2 in range(4):
            gp.wait_ge(g("s_vst"), 16 * 4 * (b2 + 1))
            gp.wait_ge(g("s_vst2"), 16 * 4 * (b2 + 1))
            gp.collective_compute("AllGather", ALU.bypass, replica_groups=rg,
                                  ins=[vslice[b2].ap()[:, :]], outs=[vfull[b2].ap()[:, :]]).then_inc(g("s_ccv"))
        _glast[0] = (g("s_ccv"), 1)
        gp.wait_ge(g("s_nm"), 8)
        for mb_ in range(NMB):
            par_ = mb_ % 2
            r_, lb_ = mb_ // 4, mb_ % 4
            if mb_ < 4:
                gp.wait_ge(g("s_ccv"), lb_ + 1)
            if mb_ >= 2:
                gp.wait_ge(g("s_pv"), 16 * (mb_ - 1))
            gdma(regC[:, 4096 * par_:4096 * (par_ + 1)].rearrange("p (c m) -> p c m", c=4),
                 vfull[lb_].ap()[512 * r_:512 * (r_ + 1), :]
                 .rearrange("(c p) m -> p c m", p=128), g("s_vv"))


    @blk.tensor
    def _(t):
        for w in range(2):
            t.wait_ge(g("s_wbs"), WBs[f"loads{w}"])
            for j in range(8):
                if not (w == 0 and j == 0):
                    prev = WBv[f"evac{w}_{j - 1}"] if j > 0 else WBv["evac0_7"]
                    t.wait_ge(g("s_wbv"), prev)
                t.matmul(psX[:, 0:256], g1cm[:, 128 * j:128 * (j + 1)], g2s,
                         start=True, stop=True).then_inc(g("s_wb"), 1)

        t.wait_ge(g("s_wq"), 16)

        def transpose_tile(t_idx, ph3):
            base = 128 * ph3
            for c in range(8):
                G = base + 8 * t_idx + c
                if c == 0:
                    t.wait_ge(g("s_ln"), 16 * ph3 + t_idx + 1)
                if G >= 2:
                    t.wait_ge(g("s_fm"), G - 1)
                t.matmul(pT[c % 2][:, 0:128],
                         xnf_t(t_idx)[:, 128 * c:128 * (c + 1)],
                         eye_sb[:, :], start=True, stop=True).then_inc(g("s_tp"), 1)

        _pj4 = [ps0, ps1, psV10[:, :], psV11[:, :]]
        def proj_block(b):
            par = b % 2
            gbase = 24 * b
            if b == 0:
                t.wait_ge(g("s_wt"), 128)
            t.wait_ge(g("s_fm"), 32 * (b + 1))
            grp = 0
            for ek in range(8):
                gi = gbase + grp
                if gi >= 4:
                    t.wait_ge(g("s_ev"), gi - 3)
                ps = _pj4[grp % 4]
                for c in range(8):
                    ins = t.matmul(ps[:, :], wqkvT(c)[:, 1024 + 128 * ek:1024 + 128 * (ek + 1)],
                                   xnfm(par, c), start=(c == 0), stop=(c == 7))
                    if c == 7:
                        ins.then_inc(g("s_pq"), 1)
                grp += 1
            if b < NBLK - 1:
                transpose_tile(4 * b + 4, 0)
                transpose_tile(4 * b + 5, 0)
            for ek in range(8):
                t.wait_ge(g("s_ks"), 8 * b + ek + 1)
                if ek == 0 and b >= 1:
                    t.wait_ge(g("s_nf"), b)
                ins = t.matmul(psL[0:1, 0:512], ones_b[:, :], ksq(ek),
                               start=(ek == 0), stop=(ek == 7))
                if ek == 7:
                    ins.then_inc(g("s_nm"), 1)
            for eq in range(8):
                gi = gbase + grp
                t.wait_ge(g("s_ev"), gi - 3)
                ps = _pj4[grp % 4]
                for c in range(8):
                    ins = t.matmul(ps[:, :], wqkvT(c)[:, 128 * eq:128 * (eq + 1)],
                                   xnfm(par, c), start=(c == 0), stop=(c == 7))
                    if c == 7:
                        ins.then_inc(g("s_pq"), 1)
                grp += 1
            if b < NBLK - 1:
                transpose_tile(4 * b + 6, 0)
                transpose_tile(4 * b + 7, 0)
            for i in range(8):
                nc4, h = i // 2, i % 2
                gi = gbase + grp
                t.wait_ge(g("s_ev"), gi - 3)
                ps = _pj4[grp % 4]
                for c in range(8):
                    ins = t.matmul(ps[:, :], xnfm(par, c)[:, 128 * nc4:128 * (nc4 + 1)],
                                   wqkvT(c)[:, 2048 + 512 * h:2048 + 512 * (h + 1)],
                                   start=(c == 0), stop=(c == 7))
                    if c == 7:
                        ins.then_inc(g("s_pq"), 1)
                grp += 1

        for tt in range(4):
            transpose_tile(tt, 0)
        for b in range(NBLK):
            proj_block(b)

        for ns in range(4):
            t.wait_ge(g("s_qs"), 8 * (ns + 1))
            t.wait_ge(g("s_nf"), 4 + ns)
            for c in range(8):
                ins = t.matmul(psL[0:1, 0:512], ones_b[:, :], ksq(c),
                               start=(c == 0), stop=(c == 7))
                if c == 7:
                    ins.then_inc(g("s_nm"), 1)

        t.wait_ge(g("s_mi"), 2)
        t.matmul(psX[:, 0:1], ones_f[:, :], nstats[0:1, 5:6],
                 start=True, stop=True).then_inc(g("s_mi"), 1)

        def s_part(mb, mc, cpair):
            par = mb % 2
            EG = 4 * mb + mc
            if mc == 0 and cpair == 0:
                t.wait_ge(g("s_kv"), 16 * (mb + 1))
            if EG == 0 and cpair == 0:
                t.wait_ge(g("s_ev"), 96)
            if cpair == 0 and EG >= 1:
                t.wait_ge(g("s_exp"), 2 * EG)
            for c in (2 * cpair, 2 * cpair + 1):
                lhs = kbuf(par, c)[:, 128 * mc:128 * (mc + 1)]
                for sqs in range(4):
                    ins = t.matmul(_sbanks[sqs], lhs,
                                   qfm[:, 2048 * c + 512 * sqs:2048 * c + 512 * (sqs + 1)],
                                   start=(c == 0), stop=(c == 7))
                    if c == 7 and sqs == 3:
                        ins.then_inc(g("s_smm"), 1)

        def pv_group(mb, qc, tail=False):
            idx = 16 * mb + qc
            pr = (psB[:, 0:512], psB[:, 512:1024]) if (tail and idx % 2 == 1) else \
                 (pvpair[0][:, :], pvpair[1][:, :])
            if qc == 0:
                t.wait_ge(g("s_vv"), 16 * (mb + 1))
                t.wait_ge(g("s_exp"), 8 * (mb + 1))
            if idx == 0:
                t.wait_ge(g("s_mi"), 4)
            if idx >= 1:
                t.wait_ge(g("s_la"), idx)
                t.wait_ge(g("s_add"), idx - 1 if (tail and idx > 16 * (NMB - 2)) else idx)
            for sub in range(4):
                lhs = PT(mb % 3, sub)[:, 128 * qc:128 * (qc + 1)]
                t.matmul(pr[0], lhs, vbuf(mb % 2, sub)[:, 0:512],
                         start=(sub == 0), stop=(sub == 3))
                t.matmul(psX[:, 0:1], lhs, ones_b[:, :],
                         start=(sub == 0), stop=(sub == 3))
                ins = t.matmul(pr[1], lhs, vbuf(mb % 2, sub)[:, 512:1024],
                               start=(sub == 0), stop=(sub == 3))
                if sub == 3:
                    ins.then_inc(g("s_pv"), 1)

        for mb in range(NMB):
            for u in range(16):
                if mb >= 2:
                    pv_group(mb - 2, u)
                s_part(mb, u // 4, u % 4)
        t.wait_ge(g("s_exp"), 8 * NMB)
        for mb in (NMB - 2, NMB - 1):
            for qc in range(16):
                pv_group(mb, qc, tail=True)

        def ffn_group(qc):
            t.wait_ge(g("s_fm"), 128 + 8 * (qc + 1))
            if qc >= 2:
                t.wait_ge(g("s_ev"), 96 + 2 * (qc - 1))
            pair = (ps0, ps1) if qc % 2 == 0 else (psL, psX)
            for half in range(2):
                ps = pair[half]
                wa, wb_ = (wtR, wtIn) if half == 0 else (wtI, wtR)
                for c in range(4):
                    t.matmul(ps[:, :], hfm(c)[:, 128 * qc:128 * (qc + 1)],
                             wa[:, 512 * c:512 * (c + 1)], start=(c == 0), stop=False)
                for c in range(4):
                    ins = t.matmul(ps[:, :], hfm(4 + c)[:, 128 * qc:128 * (qc + 1)],
                                   wb_[:, 512 * c:512 * (c + 1)], start=False, stop=(c == 3))
                ins.then_inc(g("s_pq"), 1)

        t.wait_ge(g("s_nb"), 32)
        t.wait_ge(g("s_add"), NMB * 16)
        t.matmul(psV10[:, :], ones_f[:, :], sbr[:, :], start=True, stop=True).then_inc(g("s_bb"), 1)
        t.matmul(psV11[:, :], ones_f[:, :], sbi[:, :], start=True, stop=True).then_inc(g("s_bb"), 1)

        transpose_tile(0, 1)
        transpose_tile(1, 1)
        for qc in range(NTILE - 2):
            transpose_tile(qc + 2, 1)
            ffn_group(qc)
        ffn_group(NTILE - 2)
        ffn_group(NTILE - 1)

    @blk.scalar
    def _(sc):
        _acnt = [0]
        def amark(ins):
            ins.then_inc(g("s_act"), 1)
            _acnt[0] += 1
        def ajoin():
            sc.wait_ge(g("s_act"), _acnt[0])
        for w in range(2):
            for j in range(8):
                sc.wait_ge(g("s_wbv"), WBv[f"evac{w}_{j}"])
                if 8 * w + j >= 1:
                    sc.wait_ge(g("s_wbd"), 16 * (8 * w + j))
                sc.dma_start(out=wb_l1[w].ap()[128 * j:128 * (j + 1), :],
                             in_=a_tmp[j % 2]).then_inc(g("s_wbd"), 16)

        def ln_stats(t_idx, ph3, src_ap):
            par = t_idx % 2
            off = 16 * ph3
            if ph3:
                sc.wait_ge(g("s_x2"), t_idx + 1)
            else:
                sc.wait_ge(g("s_x"), 16 * (t_idx + 1))
            sc.activation(out=trash(), in_=src_ap, func=AF.Square,
                          accum_out=st_col(par, 4)).then_inc(g("s_sq"), 1)
            sc.wait_ge(g("s_st"), off + t_idx + 1)
            sc.wait_ge(g("s_sq"), off + t_idx + 1)
            amark(sc.activation(out=st_col(par, 5), in_=st_col(par, 4), func=AF.Ln,
                                scale=1.0 / D, bias=st_col(par, 3)))
            ajoin()
            sc.activation(out=st_col(par, 6), in_=st_col(par, 5), func=AF.Exp,
                          scale=-0.5).then_inc(g("s_rs"), 1)

        _aj4 = [ps0, ps1, psV10[:, :], psV11[:, :]]
        def evac_block(b, part=None):
            gbase = 24 * b
            ek_r = {None: range(8), 0: range(4), 1: range(4, 8)}.get(part, [])
            eq_r = range(8) if part in (None, 2) else []
            v_r = range(8) if part in (None, 3) else []
            for ek in ek_r:
                sc.wait_ge(g("s_pq"), gbase + ek + 1)
                kk = 8 * b + ek
                if kk >= 2:
                    sc.wait_ge(g("s_kst" if kk % 2 == 0 else "s_kst2"),
                               16 * (kk // 2))
                if ek == 0 and b >= 1:
                    sc.wait_ge(g("s_nm"), b)
                ps = _aj4[ek % 4]
                sc.activation(out=ksq(ek), in_=ps[:, :], func=AF.Square).then_inc(g("s_ks"), 1)
                sc.activation(out=ktile(ek % 2), in_=ps[:, :], func=AF.Copy).then_inc(g("s_ev"), 1)
                sc.wait_ge(g("s_ev"), gbase + ek + 1)
                sc.dma_start(out=kslice[b].ap()[128 * ek:128 * (ek + 1), :],
                             in_=ktile(ek % 2)).then_inc(g("s_kst" if kk % 2 == 0 else "s_kst2"), 16)
            for eq in eq_r:
                sc.wait_ge(g("s_pq"), gbase + 8 + eq + 1)
                ps = _aj4[eq % 4]
                sc.activation(out=qfm[:, 2048 * eq + 512 * b:2048 * eq + 512 * (b + 1)],
                              in_=ps[:, :], func=AF.Copy, scale=QSCALE).then_inc(g("s_ev"), 1)
            for i in v_r:
                sc.wait_ge(g("s_pq"), gbase + 16 + i + 1)
                vv = 8 * b + i
                if vv >= 2:
                    sc.wait_ge(g("s_vst" if vv % 2 == 0 else "s_vst2"),
                               16 * (vv // 2))
                nc4, h = i // 2, i % 2
                ps = _aj4[i % 4]
                sc.activation(out=vtile(i % 2), in_=ps[:, :], func=AF.Copy).then_inc(g("s_ev"), 1)
                sc.wait_ge(g("s_ev"), gbase + 16 + i + 1)
                sc.dma_start(out=vslice[b].ap()[128 * nc4:128 * (nc4 + 1),
                                                512 * h:512 * (h + 1)],
                             in_=vtile(i % 2)).then_inc(g("s_vst" if vv % 2 == 0 else "s_vst2"), 16)

        for b in range(NBLK):
            for i, tt in enumerate(range(4 * b, 4 * b + 4)):
                ln_stats(tt, 0, xt_t(tt))
                if b >= 1:
                    evac_block(b - 1, part=i)
        evac_block(NBLK - 1)

        for ns in range(4):
            sc.wait_ge(g("s_nm"), 4 + ns)
            for c in range(8):
                sc.activation(out=ksq(c), in_=qfm[:, 2048 * c + 512 * ns:2048 * c + 512 * (ns + 1)],
                              func=AF.Square).then_inc(g("s_qs"), 1)

        sc.wait_ge(g("s_mi"), 1)
        amark(sc.activation(out=nstats[0:1, 4:5], in_=nstats[0:1, 3:4], func=AF.Sqrt))
        ajoin()
        sc.activation(out=nstats[0:1, 5:6], in_=nstats[0:1, 4:5], func=AF.Copy,
                      scale=-1.0).then_inc(g("s_mi"), 1)

        ecount = 0
        for mb in range(NMB):
            u = mb % 3
            if mb >= 3:
                sc.wait_ge(g("s_pv"), 16 * (mb - 2))
            for mc in range(4):
                sc.wait_ge(g("s_smm"), ecount + 1)
                if ecount == 0:
                    sc.wait_ge(g("s_mi"), 4)
                sc.activation(out=PT(u, mc)[:, 0:1024], in_=psA[:, :], func=AF.Exp,
                              bias=negm_bc[:, :]).then_inc(g("s_exp"), 1)
                sc.activation(out=PT(u, mc)[:, 1024:2048], in_=psB[:, :], func=AF.Exp,
                              bias=negm_bc[:, :]).then_inc(g("s_exp"), 1)
                ecount += 1

        def act_evac_tile(t_idx):
            for c in range(8):
                G = 128 + 8 * t_idx + c
                sc.wait_ge(g("s_tp"), G + 1)
                dst = regA[:, 2048 * c + 128 * t_idx:2048 * c + 128 * (t_idx + 1)]
                sc.activation(out=dst, in_=pT[c % 2][:, 0:128],
                              func=AF.Copy).then_inc(g("s_fm"), 1)

        for tt in range(NTILE):
            ln_stats(tt, 1, oacc_t(tt))
            if tt >= 1:
                act_evac_tile(tt - 1)
        act_evac_tile(NTILE - 1)
        for qc in range(NTILE):
            sc.wait_ge(g("s_ev"), 96 + 2 * (qc + 1))
            sc.dma_start(out=out_h[128 * qc:128 * (qc + 1), :],
                         in_=oacc_t(qc)).then_inc(g("s_store"), 16)
        sc.wait_ge(g("s_store"), 16 * NTILE)

    @blk.vector
    def _(v):
        _dcnt = [0]
        def dmark(ins):
            ins.then_inc(g("s_dve"), 1)
            _dcnt[0] += 1
        def djoin():
            v.wait_ge(g("s_dve"), _dcnt[0])
        v.memset(ones_f[:, :], 1.0)
        v.memset(ones_b[:, :], 1.0)
        v.memset(seps[:, :], EPS)
        for w in range(2):
            for j in range(8):
                v.wait_ge(g("s_wb"), WB[f"mm{w}_{j}"])
                if (w, j) > (0, 1):
                    v.wait_ge(g("s_wbd"), 16 * (8 * w + j - 1))
                v.tensor_copy(out=a_tmp[j % 2], in_=psX[:, 0:256]).then_inc(g("s_wbv"), 1)
            for h in range(2):
                v.wait_ge(g("s_wbs"), WBs[f"read1_{w}_{h}"])
                if not (w == 0 and h == 0):
                    v.wait_ge(g("s_wl2"), WL2[f"write2_{w}_{h - 1}"] if h == 1
                              else WL2["write2_0_1"])
                v.tensor_copy(out=wbA2.rearrange("p (gg b e) -> p gg b e", gg=16, b=16),
                              in_=wbA.rearrange("p (b e gg) -> p gg b e", b=16, e=16)) \
                    .then_inc(g("s_wbv"), 1)
        v.wait_ge(g("s_wl2"), WL2_ALL)
        ins = None
        for q in range(4):
            ins = v.tensor_scalar_mul(out=wtIn[:, 512 * q:512 * (q + 1)],
                                      in0=wtI[:, 512 * q:512 * (q + 1)], scalar1=-1.0)
        ins.then_inc(g("s_wbv"), 1)

        def ln_vec(t_idx, ph3, src_ap):
            par = t_idx % 2
            off = 16 * ph3
            if not ph3:
                v.wait_ge(g("s_x"), 16 * (t_idx + 1))
            dmark(v.reduce_sum(st_col(par, 0), src_ap, axis=AX.X))
            djoin()
            dmark(v.tensor_scalar_mul(out=st_col(par, 1), in0=st_col(par, 0), scalar1=1.0 / D))
            djoin()
            dmark(v.scalar_tensor_tensor(out=st_col(par, 2), in0=st_col(par, 1), scalar=-1.0,
                                         in1=st_col(par, 1), op0=ALU.mult, op1=ALU.mult))
            djoin()
            v.tensor_scalar_add(out=st_col(par, 3), in0=st_col(par, 2),
                                scalar1=EPS).then_inc(g("s_st"), 1)
            v.wait_ge(g("s_rs"), off + t_idx + 1)
            dmark(v.scalar_tensor_tensor(out=st_col(par, 7), in0=st_col(par, 1), scalar=-1.0,
                                         in1=st_col(par, 6), op0=ALU.mult, op1=ALU.mult))
            if t_idx >= 2:
                v.wait_ge(g("s_tp"), 128 * ph3 + 8 * (t_idx - 1))
            djoin()
            v.tensor_scalar(out=xnf_t(t_idx), in0=src_ap,
                            scalar1=st_col(par, 6), scalar2=st_col(par, 7),
                            op0=ALU.mult, op1=ALU.add).then_inc(g("s_ln"), 1)

        def evac_tile(t_idx, ph3):
            for c in range(8):
                G = 128 * ph3 + 8 * t_idx + c
                v.wait_ge(g("s_tp"), G + 1)
                if (not ph3) and t_idx % 4 == 0 and c == 0 and t_idx // 4 >= 2:
                    v.wait_ge(g("s_pq"), 24 * (t_idx // 4 - 1))
                if ph3:
                    dst = regA[:, 2048 * c + 128 * t_idx:2048 * c + 128 * (t_idx + 1)]
                else:
                    par = (t_idx // 4) % 2
                    dst = regB[:, 4096 * par + 512 * c + 128 * (t_idx % 4):
                               4096 * par + 512 * c + 128 * (t_idx % 4) + 128]
                v.tensor_copy(out=dst, in_=pT[c % 2][:, 0:128]).then_inc(g("s_fm"), 1)

        def knorm_read(b):
            v.wait_ge(g("s_nm"), b + 1)
            if b == 0:
                v.reduce_max(nstats[0:1, 0:1], psL[0:1, 0:512], axis=AX.X).then_inc(g("s_nf"), 1)
            else:
                dmark(v.reduce_max(nstats[0:1, 2:3], psL[0:1, 0:512], axis=AX.X))
                djoin()
                v.tensor_max(out=nstats[0:1, 0:1], in0=nstats[0:1, 0:1],
                             in1=nstats[0:1, 2:3]).then_inc(g("s_nf"), 1)

        for tt in range(NTILE):
            ln_vec(tt, 0, xt_t(tt))
            if tt == 10:
                knorm_read(0)
            evac_tile(tt, 0)
            if tt == 13:
                knorm_read(1)
        knorm_read(NBLK - 2)
        knorm_read(NBLK - 1)
        djoin()
        dmark(v.memset(nrm_sb[:, :], 0.0))
        djoin()
        v.tensor_scalar_add(out=nrm_sb[:, :], in0=nrm_sb[:, :],
                            scalar1=nstats[0:1, 0:1]).then_inc(g("s_nmsb"), 1)
        for ns in range(4):
            v.wait_ge(g("s_nm"), 5 + ns)
            if ns == 0:
                v.reduce_max(nstats[0:1, 1:2], psL[0:1, 0:512], axis=AX.X).then_inc(g("s_nf"), 1)
            else:
                dmark(v.reduce_max(nstats[0:1, 2:3], psL[0:1, 0:512], axis=AX.X))
                djoin()
                v.tensor_max(out=nstats[0:1, 1:2], in0=nstats[0:1, 1:2],
                             in1=nstats[0:1, 2:3]).then_inc(g("s_nf"), 1)
        v.wait_ge(g("s_nb"), 48)
        v.wait_ge(g("s_nf"), 8)
        dmark(v.reduce_max(nstats[0:1, 2:3], nrm_sb[0:1, :], axis=AX.X))
        djoin()
        v.tensor_mul(out=nstats[0:1, 3:4], in0=nstats[0:1, 2:3],
                     in1=nstats[0:1, 1:2]).then_inc(g("s_mi"), 1)
        v.wait_ge(g("s_mi"), 3)
        v.tensor_copy(out=negm_bc[:, :], in_=psX[:, 0:1]).then_inc(g("s_mi"), 1)

        v.wait_ge(g("s_wbd"), WBD_ALL)
        v.wait_ge(g("s_wl2"), WL2_ALL)
        for mb in range(NMB):
            for qc in range(16):
                idx = 16 * mb + qc
                tail = mb >= NMB - 2
                pva, pvb = (psB[:, 0:512], psB[:, 512:1024]) if (tail and idx % 2 == 1) else \
                           (pvpair[0][:, :], pvpair[1][:, :])
                v.wait_ge(g("s_pv"), idx + 1)
                if mb == 0:
                    v.tensor_copy(out=L_sb[:, qc:qc + 1], in_=psX[:, 0:1]).then_inc(g("s_la"), 1)
                    v.tensor_copy(out=oacc[:, 1024 * qc:1024 * qc + 512], in_=pva)
                    v.tensor_copy(out=oacc[:, 1024 * qc + 512:1024 * (qc + 1)],
                                  in_=pvb).then_inc(g("s_add"), 1)
                else:
                    v.tensor_add(out=L_sb[:, qc:qc + 1], in0=psX[:, 0:1],
                                 in1=L_sb[:, qc:qc + 1]).then_inc(g("s_la"), 1)
                    v.tensor_add(out=oacc[:, 1024 * qc:1024 * qc + 512],
                                 in0=pva, in1=oacc[:, 1024 * qc:1024 * qc + 512])
                    v.tensor_add(out=oacc[:, 1024 * qc + 512:1024 * (qc + 1)],
                                 in0=pvb,
                                 in1=oacc[:, 1024 * qc + 512:1024 * (qc + 1)]).then_inc(g("s_add"), 1)
        v.wait_ge(g("s_la"), NMB * 16)
        v.reciprocal(out=rl_sb[:, :], in_=L_sb[:, :]).then_inc(g("s_lrl"), 1)

        def ffn_evac(qc):
            v.wait_ge(g("s_pq"), 96 + 2 * (qc + 1))
            if qc == 0:
                v.wait_ge(g("s_bb"), 2)
            pair = (ps0, ps1) if qc % 2 == 0 else (psL, psX)
            dmark(v.tensor_add(out=oacc[:, 1024 * qc:1024 * qc + 512],
                               in0=pair[0][:, :], in1=oacc[:, 1024 * qc:1024 * qc + 512]))
            dmark(v.tensor_add(out=oacc[:, 1024 * qc + 512:1024 * (qc + 1)],
                               in0=pair[1][:, :],
                               in1=oacc[:, 1024 * qc + 512:1024 * (qc + 1)]))
            djoin()
            v.tensor_add(out=oacc[:, 1024 * qc:1024 * qc + 512],
                         in0=psV10[:, :], in1=oacc[:, 1024 * qc:1024 * qc + 512])
            v.tensor_add(out=oacc[:, 1024 * qc + 512:1024 * (qc + 1)],
                         in0=psV11[:, :],
                         in1=oacc[:, 1024 * qc + 512:1024 * (qc + 1)]).then_inc(g("s_ev"), 2)

        for tt in range(NTILE):
            v.wait_ge(g("s_x"), 256 + 16 * (tt + 1))
            if tt == 0:
                v.wait_ge(g("s_lrl"), 1)
            v.scalar_tensor_tensor(out=oacc_t(tt), in0=oacc_t(tt),
                                   scalar=rl_sb[:, tt:tt + 1], in1=xt_t(tt),
                                   op0=ALU.mult, op1=ALU.add).then_inc(g("s_x2"), 1)
            v.wait_ge(g("s_x2"), tt + 1)
            ln_vec(tt, 1, oacc_t(tt))
            if tt >= 3:
                ffn_evac(tt - 3)
        ffn_evac(NTILE - 3)
        ffn_evac(NTILE - 2)
        ffn_evac(NTILE - 1)

    return nc, ctx


_cache = {}


def kernel(L0_attn_x, L0_attn_W_qkv, L0_ffn_G1_R, L0_ffn_G2_R, L0_ffn_G1_I,
           L0_ffn_G2_I, L0_ffn_bias_real, L0_ffn_bias_imag):
    x2d = np.ascontiguousarray(np.asarray(L0_attn_x, dtype=np.float32).reshape(NTOK, D))
    wqkv = np.ascontiguousarray(np.asarray(L0_attn_W_qkv, dtype=np.float32))
    g1r = np.ascontiguousarray(np.asarray(L0_ffn_G1_R, dtype=np.float32))
    g2r = np.ascontiguousarray(np.asarray(L0_ffn_G2_R, dtype=np.float32))
    g1i = np.ascontiguousarray(np.asarray(L0_ffn_G1_I, dtype=np.float32))
    g2i = np.ascontiguousarray(np.asarray(L0_ffn_G2_I, dtype=np.float32))
    br = np.ascontiguousarray(np.asarray(L0_ffn_bias_real, dtype=np.float32))
    bi = np.ascontiguousarray(np.asarray(L0_ffn_bias_imag, dtype=np.float32))

    if "nc" not in _cache:
        _cache["nc"] = build_nc()
    nc, _ctx = _cache["nc"]

    in_maps = [dict(x=x2d[NT * c:NT * (c + 1)], wqkv=wqkv, g1r=g1r, g2r=g2r,
                    g1i=g1i, g2i=g2i, br=br, bi=bi) for c in range(NCORE)]
    res = run_bass_kernel_spmd(nc, in_maps, core_ids=list(range(NCORE)))
    out = np.concatenate([res.results[c]["out"] for c in range(NCORE)], axis=0)
    return out.reshape(1, NTOK, D)
